# version 09
# baseline (speedup 1.0000x reference)
import contextlib
import numpy as np
import ml_dtypes

import concourse.bass as bass
import concourse.mybir as mybir
from concourse.bass_utils import run_bass_kernel_spmd

F32 = mybir.dt.float32
BF16 = mybir.dt.bfloat16
AF = mybir.ActivationFunctionType
ALU = mybir.AluOpType
AX = mybir.AxisListType

B = 8
T = 2048
D = 2048
NC_CHUNK = 16
H, KVH, HD = 16, 8, 128
R = 32
NE, INTER, MCH = 16, 1408, 11
SI, SMCH = 2816, 22
TT = 512
NTT = T // TT
RMS_EPS = 1.1920929e-07
LN_EPS = 1e-5
ISQ_HD = float(1.0 / np.sqrt(128.0))
NEG_BIG = -1.0e30

DEBUG = False
PH5_NEXP = NE + 2
PH5_FINAL = True
PH5_RES = True
PH3_INTER = True


class Sem:
    def __init__(self, es, nc, name, step=1):
        self.h = es.enter_context(nc.semaphore(name))
        self.n = 0
        self.step = step

    def bump(self, k=1):
        self.n += k * self.step
        return self.n


class Em:
    ENGINES = ("sync", "tensor", "vector", "scalar", "gpsimd")

    def __init__(self):
        self.q = {e: [] for e in self.ENGINES}

    def emit(self, eng, fn, waits=(), inc=None, inck=1):
        wl = tuple(waits)
        tgt = None
        if inc is not None:
            tgt = inc.bump(inck)
        def run(e, fn=fn, wl=wl, inc=inc, inck=inck):
            for s, v in wl:
                e.wait_ge(s.h, v)
            inst = fn(e)
            if inc is not None:
                inst.then_inc(inc.h, inck * inc.step)
        self.q[eng].append(run)
        return tgt

    def wait(self, eng, sem, val):
        self.q[eng].append(lambda e, s=sem, v=val: e.wait_ge(s.h, v))

    def install(self, block):
        for eng in self.ENGINES:
            fns = self.q[eng]
            if fns:
                getattr(block, eng)(lambda e, fns=fns: [f(e) for f in fns])


def _c(a):
    return np.ascontiguousarray(a)


def pack_weights(i):
    d = {}
    W_q = np.asarray(i["L0_attn_W_q"]); W_k = np.asarray(i["L0_attn_W_k"])
    W_v = np.asarray(i["L0_attn_W_v"]); W_o = np.asarray(i["L0_attn_W_o"])
    d["wq_t"] = _c(W_q.reshape(16, 128, 16, 128).transpose(0, 3, 2, 1))
    d["wk_t"] = _c(W_k.reshape(8, 128, 16, 128).transpose(0, 3, 2, 1))
    d["wv_t"] = _c(W_v.reshape(8, 128, 16, 128).transpose(0, 3, 2, 1))
    d["wo_t"] = _c(W_o.reshape(16, 128, 16, 128).transpose(0, 3, 2, 1))
    for nm, key in (("aq", "A_q"), ("ak", "A_k"), ("av", "A_v"), ("ao", "A_o")):
        A = np.asarray(i[f"L0_attn_{key}"])
        d[nm + "_t"] = _c(A.reshape(R, 16, 128).transpose(2, 1, 0))
    for nm, key, noc in (("bq", "B_q", 16), ("bk", "B_k", 8), ("bv", "B_v", 8), ("bo", "B_o", 16)):
        Bm = np.asarray(i[f"L0_attn_{key}"])
        d[nm + "_t"] = _c(Bm.reshape(noc, 128, R).transpose(2, 0, 1))
    d["svec"] = _c(np.stack([np.asarray(i["L0_attn_s_q"]).reshape(()),
                             np.asarray(i["L0_attn_s_k"]).reshape(()),
                             np.asarray(i["L0_attn_s_v"]).reshape(()),
                             np.asarray(i["L0_attn_s_o"]).reshape(())])[None, :])
    d["qgain"] = _c(np.asarray(i["L0_attn_q_gain"]).reshape(1, H))
    gw = np.asarray(i["L0_ffn_gate_w"])
    d["gate_t"] = _c(gw.reshape(NE, 16, 128).transpose(2, 1, 0))
    w1 = np.asarray(i["L0_ffn_w1"]); w2 = np.asarray(i["L0_ffn_w2"]); w3 = np.asarray(i["L0_ffn_w3"])
    d["w1_t"] = _c(w1.reshape(NE, MCH, 128, 16, 128).transpose(0, 1, 4, 3, 2))
    d["w3_t"] = _c(w3.reshape(NE, MCH, 128, 16, 128).transpose(0, 1, 4, 3, 2))
    d["w2_t"] = _c(w2.reshape(NE, 16, 128, MCH, 128).transpose(0, 1, 4, 3, 2))
    sw1 = np.asarray(i["L0_ffn_sw1"]); sw2 = np.asarray(i["L0_ffn_sw2"]); sw3 = np.asarray(i["L0_ffn_sw3"])
    d["sw1_t"] = _c(sw1.reshape(SMCH, 128, 16, 128).transpose(0, 3, 2, 1))
    d["sw3_t"] = _c(sw3.reshape(SMCH, 128, 16, 128).transpose(0, 3, 2, 1))
    d["sw2_t"] = _c(sw2.reshape(16, 128, 2, 11, 128).transpose(0, 2, 4, 3, 1))
    return d


def host_constants():
    inv = 1.0 / (10000.0 ** (np.arange(0, 64, 2, dtype=np.float64) / 64.0))
    freqs = np.outer(np.arange(T, dtype=np.float64), inv)
    cos = np.cos(freqs).astype(np.float32).T
    sin = np.sin(freqs).astype(np.float32).T
    cs = np.concatenate([cos, cos], axis=0)
    ss = np.concatenate([-sin, sin], axis=0)
    cm = np.where(np.arange(128)[:, None] >= np.arange(128)[None, :],
                  0.0, NEG_BIG).astype(np.float32)
    ident = np.eye(128, dtype=ml_dtypes.bfloat16)
    ident_f32 = np.eye(128, dtype=np.float32)
    ones_col_bf = np.ones((128, 1), dtype=ml_dtypes.bfloat16)
    ones_row_f32 = np.ones((1, 128), dtype=np.float32)
    onehot = np.zeros((16, NE, 128), dtype=np.float32)
    for e in range(NE):
        onehot[e, e, :] = 1.0
    onehot = _c(onehot.transpose(1, 0, 2))
    return dict(ropecs=_c(cs), ropess=_c(ss), cmask=cm, ident=ident,
                ones_col_bf=ones_col_bf, ones_row_f32=ones_row_f32,
                onehot=onehot, ident_f32=ident_f32)


def pack_x(x):
    out = []
    for b in range(B):
        xt = x[b].T
        a = xt.reshape(16, 128, NTT, TT).transpose(2, 1, 0, 3)
        out.append(_c(a.astype(np.float32, copy=False)))
    return out


def unpack_out(o):
    return _c(o.transpose(2, 1, 0, 3).reshape(D, T).T)


def declare_ios(nc):
    t = {}
    def din(name, shape, dt=F32):
        t[name] = nc.dram_tensor(name, list(shape), dt, kind="ExternalInput").ap()
    def dout(name, shape, dt=F32):
        t[name] = nc.dram_tensor(name, list(shape), dt, kind="ExternalOutput").ap()
    def dint(name, shape, dt):
        t[name] = nc.dram_tensor(name, list(shape), dt, kind="Internal").ap()

    din("x_t", (NTT, 128, 16, TT))
    din("wq_t", (16, 128, 16, 128)); din("wk_t", (8, 128, 16, 128))
    din("wv_t", (8, 128, 16, 128)); din("wo_t", (16, 128, 16, 128))
    for nm in ("aq", "ak", "av", "ao"):
        din(nm + "_t", (128, 16, R))
    din("bq_t", (R, 16, 128)); din("bk_t", (R, 8, 128))
    din("bv_t", (R, 8, 128)); din("bo_t", (R, 16, 128))
    din("svec", (1, 4)); din("qgain", (1, H))
    din("gate_t", (128, 16, NE))
    din("w1_t", (NE, MCH, 128, 16, 128)); din("w3_t", (NE, MCH, 128, 16, 128))
    din("w2_t", (NE, 16, 128, MCH, 128))
    din("sw1_t", (SMCH, 128, 16, 128)); din("sw3_t", (SMCH, 128, 16, 128))
    din("sw2_t", (16, 2, 128, 11, 128))
    din("ropecs", (64, T)); din("ropess", (64, T))
    din("cmask", (128, 128)); din("ident", (128, 128), BF16)
    din("ones_col_bf", (128, 1), BF16); din("ones_row_f32", (1, 128))
    din("onehot", (16, NE, 128)); din("ident_f32", (128, 128))

    dout("out_t", (NTT, 128, 16, TT))

    dint("q_d", (H, 128, T), BF16)
    dint("o_d", (H, 128, T), BF16)
    dint("x2_d", (NTT, 128, 16, TT), F32)
    dint("ln2_d", (NTT, 128, 16, TT), BF16)
    dint("glog_d", (NTT, 16, TT), F32)
    dint("w1b_d", (NE, MCH, 128, 16, 128), BF16)
    dint("w3b_d", (NE, MCH, 128, 16, 128), BF16)
    dint("w2b_d", (NE, 16, 128, MCH, 128), BF16)
    dint("sw1b_d", (SMCH, 128, 16, 128), BF16)
    dint("sw3b_d", (SMCH, 128, 16, 128), BF16)
    dint("sw2b_d", (16, 128, SMCH, 128), BF16)

    if DEBUG:
        dout("ln1_dbg", (128, 16, T), BF16)
        dout("q_dbg", (H, 128, T), BF16)
        dout("k_dbg", (KVH, 128, T), BF16)
        dout("v_dbg", (128, 16, KVH * 128), BF16)
        dout("o_dbg", (H, 128, T), BF16)
        dout("x2_dbg", (NTT, 128, 16, TT), F32)
        dout("ln2_dbg", (NTT, 128, 16, TT), BF16)
        dout("gatew_dbg", (128, NTT * 4, NE), F32)
        dout("pr_dbg", (128, TT), BF16)
        dout("sm_dbg", (128, 8), F32)
    return t


def alloc_persistent(nc, es):
    s = {}
    def sb(name, shape, dt):
        s[name] = es.enter_context(nc.sbuf_tensor(name, list(shape), dt))
    sb("gate_sb", (128, 16, NE), F32)
    sb("svec_sb", (1, 4), F32)
    sb("qgain_sb", (1, H), F32)
    sb("scol", (128, 4), F32)
    sb("cm_sb", (128, 128), F32)
    sb("id_sb", (128, 128), BF16)
    sb("idf_sb", (128, 128), F32)
    sb("ones_col", (128, 1), BF16)
    sb("ones_row", (1, 128), F32)
    sb("cpool", (1, 4), F32)
    return s


def ph0_setup(nc, t, s, em, sems, es):
    ld = sems["ld0"]
    lds = sems["ld0s"]
    def dma(eng, out, in_):
        return em.emit(eng, lambda e, o=out, i=in_: e.dma_start(out=o, in_=i),
                       inc=(ld if eng == "gpsimd" else lds))
    for j, nm in enumerate(("aq", "ak", "av", "ao")):
        dma("gpsimd", s["lora_a"][:, j, :, :], t[nm + "_t"][:, :, :])
    dma("gpsimd", s["bq_sb"][:, :, :], t["bq_t"][:, :, :])
    dma("gpsimd", s["bk_sb"][:, :, :], t["bk_t"][:, :, :])
    dma("gpsimd", s["bv_sb"][:, :, :], t["bv_t"][:, :, :])
    dma("gpsimd", s["bo_sb"][:, :, :], t["bo_t"][:, :, :])
    dma("sync", s["gate_sb"][:, :, :], t["gate_t"][:, :, :])
    dma("sync", s["svec_sb"][:, :], t["svec"][:, :])
    dma("sync", s["qgain_sb"][:, :], t["qgain"][:, :])
    dma("sync", s["cm_sb"][:, :], t["cmask"][:, :])
    dma("sync", s["id_sb"][:, :], t["ident"][:, :])
    dma("sync", s["idf_sb"][:, :], t["ident_f32"][:, :])
    dma("sync", s["ones_col"][:, :], t["ones_col_bf"][:, :])
    dma("sync", s["ones_row"][:, :], t["ones_row_f32"][:, :])
    n_all = ld.n
    em.emit("vector", lambda e: e.memset(s["cpool"][0:1, 0:1], LN_EPS))
    em.emit("vector", lambda e: e.memset(s["cpool"][0:1, 1:2], RMS_EPS))
    em.emit("vector", lambda e: e.memset(s["cpool"][0:1, 2:3], -0.5))
    ps = es.enter_context(nc.psum_tensor("ph0_ps", [128, 4], F32))
    mmdone = sems["ev0"]
    em.emit("tensor",
            lambda e: e.matmul(ps[:, :], s["ones_row"][:, :], s["svec_sb"][:, :],
                               start=True, stop=True),
            waits=[(ld, n_all), (lds, lds.n)], inc=mmdone)
    em.emit("vector", lambda e: e.tensor_copy(s["scol"][:, :], ps[:, :]),
            waits=[(mmdone, mmdone.n)], inc=mmdone)


def ph1_ln1(nc, t, s, em, sems, es, es_local):
    ln1 = es.enter_context(nc.sbuf_tensor("ln1", [128, 16, T], BF16))
    sb = lambda n, shp, dt: es_local.enter_context(nc.sbuf_tensor(n, list(shp), dt))
    ps = lambda n, shp: es_local.enter_context(nc.psum_tensor(n, list(shp), F32))

    xbf = sb("ph1_xbf", (128, 2, 16, TT), BF16)
    sqb = sb("ph1_sqb", (128, 16, TT), BF16)
    rows = sb("ph1_rows", (1, 5, TT), F32)
    tmp = sb("ph1_tmp", (128, TT), F32)

    s1p = ps("ph1_s1", (1, TT)); s2p = ps("ph1_s2", (1, TT))
    ap_ = ps("ph1_a", (128, TT)); bp_ = ps("ph1_b", (128, TT))

    xld = sems["xld"]
    S = dict(sems["LS"])
    S["aff"] = sems["mk"]("p1aff")

    MEAN, M2, VAR, NMR, RSTD = 0, 1, 2, 3, 4

    for tt in range(NTT):
        buf = xbf[:, tt % 2]
        em.emit("gpsimd", lambda e, o=buf, i=t["x_t"][tt]: e.dma_start(out=o, in_=i),
                waits=([(S["aff"], tt - 1)] if tt >= 2 else []), inc=xld)
        n_x = xld.n
        em.emit("scalar", lambda e, i=buf: e.activation(sqb[:, :, :], i[:, :, :], AF.Square),
                waits=[(xld, n_x)] + ([(S["stat"], tt)] if tt >= 1 else []),
                inc=S["sq"])
        for c in range(16):
            em.emit("tensor", lambda e, c=c, b_=buf: e.matmul(
                s1p[:, :], s["ones_col"][:, :], b_[:, c, :],
                start=(c == 0), stop=(c == 15)),
                waits=([(S["sq"], tt + 1)] + ([(S["var"], tt)] if tt >= 1 else []))
                if c == 0 else [])
        for c in range(16):
            em.emit("tensor", lambda e, c=c: e.matmul(
                s2p[:, :], s["ones_col"][:, :], sqb[:, c, :],
                start=(c == 0), stop=(c == 15)),
                inc=(S["stat"] if c == 15 else None))
        em.emit("vector", lambda e: e.tensor_scalar(
            rows[:, MEAN, :], s1p[:, :], 1.0 / D, None, ALU.mult),
            waits=[(S["stat"], tt + 1)])
        em.emit("vector", lambda e: e.tensor_mul(rows[:, M2, :], rows[:, MEAN, :], rows[:, MEAN, :]))
        em.emit("vector", lambda e: e.scalar_tensor_tensor(
            rows[:, VAR, :], s2p[:, :], 1.0 / D, rows[:, M2, :],
            ALU.mult, ALU.subtract), inc=S["var"])
        em.emit("scalar", lambda e: e.activation(
            rows[:, RSTD, :], rows[:, VAR, :], AF.Ln, bias=s["cpool"][0:1, 0:1]),
            waits=[(S["var"], tt + 1)])
        em.emit("scalar", lambda e: e.activation(
            rows[:, RSTD, :], rows[:, RSTD, :], AF.Exp, scale=s["cpool"][0:1, 2:3]),
            inc=S["rstd"])
        em.emit("vector", lambda e: e.tensor_mul(
            rows[:, NMR, :], rows[:, MEAN, :], rows[:, RSTD, :]),
            waits=[(S["rstd"], tt + 1)], inc=S["nmr"])
        em.emit("tensor", lambda e: e.matmul(
            ap_[:, :], s["ones_row"][:, :], rows[:, RSTD, :], start=True, stop=True),
            waits=[(S["nmr"], tt + 1)])
        em.emit("tensor", lambda e: e.matmul(
            bp_[:, :], s["ones_row"][:, :], rows[:, NMR, :], start=True, stop=True),
            inc=S["bc"])
        for c in range(16):
            em.emit("vector", lambda e, c=c, b_=buf: e.tensor_mul(
                tmp[:, :], b_[:, c, :], ap_[:, :]),
                waits=[(S["bc"], tt + 1)] if c == 0 else [])
            em.emit("vector", lambda e, c=c, tt=tt: e.tensor_tensor(
                ln1[:, c, tt * TT:(tt + 1) * TT], tmp[:, :], bp_[:, :], ALU.subtract),
                inc=(S["aff"] if c == 15 else None))
    if DEBUG:
        st = sems["dbg"]
        em.emit("sync", lambda e: e.dma_start(out=t["ln1_dbg"][:, :, :], in_=ln1[:, :, :]),
                waits=[(S["aff"], NTT)], inc=st)
        em.wait("sync", st, st.n)
    es._ln1 = ln1
    return ln1


def ph2_qkv(nc, t, s, em, sems, es, ln1, kv):
    sb = lambda n, shp, dt, **kw: es.enter_context(nc.sbuf_tensor(n, list(shp), dt, **kw))
    psf = lambda n, shp: es.enter_context(nc.psum_tensor(n, list(shp), F32))
    psb = lambda n, shp: es.enter_context(nc.psum_tensor(n, list(shp), BF16))

    k_sb, v_sb = kv["k_sb"], kv["v_sb"]
    cs_sb = sb("cs_sb", (64, T), BF16)
    ss_sb = sb("ss_sb", (64, T), BF16)
    wring = sb("wring", (128, 2, 16, 128), BF16)
    u_sb = sb("u_sb", (R, 3, T), BF16)
    ev_f32 = sb("ev_f32", (128, TT), F32)
    qraw = sb("qraw", (128, TT), F32)
    qn_sb = sb("qn_sb", (128, TT), F32)
    qf_hb = sb("qf_hb", (128, 2, T), BF16)
    vbf = sb("vbf", (128, TT), BF16)
    sq_sc = sb("sq_sc", (128, 2, TT), BF16)
    rt1 = sb("rt1", (64, TT), F32)
    rt2 = sb("rt2", (64, TT), F32)
    rrow = sb("rrow", (1, 4, TT), F32)

    pW = [psf("p2_w0", (128, TT)), psf("p2_w1", (128, TT))]
    pLs = psf("p2_l", (128, TT))
    pU = [psf("p2_u0", (128, TT)), psf("p2_u1", (128, TT))]
    pS = [psf("p2_sA", (1, TT)), psf("p2_sB", (1, TT))]
    pT = psb("p2_vt", (128, TT))

    mk = sems["mk"]
    wld = mk("p2wld", 16); tld = mk("p2tld", 16)
    umm = mk("p2umm"); uev = mk("p2uev")
    gmm = mk("p2gmm"); gsc = mk("p2gsc"); gad = mk("p2gad")
    ssqq = mk("p2sqq"); sstat = mk("p2stat"); sms = mk("p2ms")
    srst = mk("p2rst"); ssrow = mk("p2srow"); sscp = mk("p2scp")
    srope = mk("p2rope"); qst = mk("p2qst", 16)
    svtr = mk("p2vtr"); svcp = mk("p2vcp")

    em.emit("gpsimd", lambda e: e.dma_start(out=cs_sb[:, :], in_=t["ropecs"][:, :]), inc=tld)
    em.emit("gpsimd", lambda e: e.dma_start(out=ss_sb[:, :], in_=t["ropess"][:, :]), inc=tld)
    for g, (wi, tt) in enumerate([(wi, tt) for tt in range(NTT) for wi in range(3)]):
        pu = pU[g % 2]
        for c in range(16):
            em.emit("tensor", lambda e, c=c, wi=wi, tt=tt, pu=pu: e.matmul(
                pu[0:R, :], s["lora_a"][:, wi, c, :], ln1[:, c, tt * TT:(tt + 1) * TT],
                start=(c == 0), stop=(c == 15)),
                waits=([(uev, g - 1)] if g >= 2 and c == 0 else []),
                inc=(umm if c == 15 else None))
        em.emit("scalar", lambda e, wi=wi, tt=tt, pu=pu: e.copy(
            u_sb[:, wi, tt * TT:(tt + 1) * TT], pu[0:R, :]),
            waits=[(umm, g + 1)], inc=uev)

    heads = ([(0, h) for h in range(H)] + [(1, a) for a in range(KVH)]
             + [(2, a) for a in range(KVH)])
    wt_of = {0: t["wq_t"], 1: t["wk_t"], 2: t["wv_t"]}
    b_of = {0: s["bq_sb"], 1: s["bk_sb"], 2: s["bv_sb"]}
    g_glob = 0
    scp_g = 0
    vtr_g = 0
    qk_g = 0
    last_mm_for_wbuf = {}
    msp = [0, 0]
    rstp = [0, 0]
    srp = [0, 0]
    scpp = [0, 0]
    statp = [0, 0]

    for hi, (role, h) in enumerate(heads):
        wbuf = wring[:, hi % 2]
        em.emit("gpsimd", lambda e, o=wbuf, i=wt_of[role][h]: e.dma_start(out=o, in_=i),
                waits=([(gmm, last_mm_for_wbuf[hi - 2])] if hi >= 2 else []), inc=wld)
        n_w = wld.n
        for j in range(NTT):
            g = g_glob; g_glob += 1
            pw, pl = pW[g % 2], pLs
            jsl = slice(j * TT, (j + 1) * TT)
            waits0 = [(wld, n_w)]
            if g >= 2:
                waits0 += [(gsc, g - 1)]
            if hi == 0 and j == 0:
                waits0 += [(uev, 12)]
            for c in range(16):
                em.emit("tensor", lambda e, c=c, pw=pw, w_=wbuf, jsl=jsl: e.matmul(
                    pw[:, :], w_[:, c, :], ln1[:, c, jsl],
                    start=(c == 0), stop=(c == 15)),
                    waits=(waits0 if c == 0 else []))
            em.emit("tensor", lambda e, pl=pl, role=role, h=h, jsl=jsl: e.matmul(
                pl[:, :], b_of[role][:, h, :], u_sb[:, role, jsl],
                start=True, stop=True),
                waits=([(gad, g)] if g >= 1 else []), inc=gmm)
            em.emit("scalar", lambda e, pw=pw, role=role: e.activation(
                ev_f32[:, :], pw[:, :], AF.Copy, scale=s["scol"][:, role:role + 1]),
                waits=[(gmm, g + 1)] + ([(gad, g - 1)] if g >= 1 else []),
                inc=gsc)
            if role == 2:
                em.emit("vector", lambda e, pl=pl: e.tensor_tensor(
                    vbf[:, :], ev_f32[:, :], pl[:, :], ALU.add),
                    waits=[(gsc, g + 1)], inc=gad)
                vg = vtr_g; vtr_g += 1
                for ti in range(4):
                    em.emit("tensor", lambda e, ti=ti: e.transpose(
                        pT[:, ti * 128:(ti + 1) * 128], vbf[:, ti * 128:(ti + 1) * 128],
                        s["id_sb"][:, :]),
                        waits=(([(gad, g + 1)] + ([(svcp, vg)] if vg >= 1 else []))
                               if ti == 0 else []),
                        inc=(svtr if ti == 3 else None))
                em.emit("vector", lambda e, h=h, j=j: e.tensor_copy(
                    v_sb[:, 4 * j:4 * j + 4, h * 128:(h + 1) * 128], pT[:, :]),
                    waits=[(svtr, svtr.n)], inc=svcp)
            else:
                qg = qk_g; qk_g += 1
                p_ = qg % 2
                em.emit("vector", lambda e, pl=pl: e.tensor_tensor(
                    qraw[:, :], ev_f32[:, :], pl[:, :], ALU.add),
                    waits=[(gsc, g + 1)], inc=gad)
                em.emit("scalar", lambda e, p_=p_: e.activation(
                    sq_sc[:, p_, :], qraw[:, :], AF.Square),
                    waits=[(gad, g + 1), (sstat, statp[p_])], inc=ssqq)
                em.emit("tensor", lambda e, p_=p_: e.matmul(
                    pS[p_][0:1, :], s["ones_col"][:, :], sq_sc[:, p_, :],
                    start=True, stop=True),
                    waits=[(ssqq, ssqq.n), (sms, msp[p_])], inc=sstat)
                statp[p_] = sstat.n
                em.emit("vector", lambda e, p_=p_: e.tensor_scalar(
                    rrow[:, 2 * p_, :], pS[p_][0:1, :], 1.0 / 128.0, None, ALU.mult),
                    waits=[(sstat, sstat.n), (srst, rstp[p_]), (sscp, scpp[p_])],
                    inc=sms)
                msp[p_] = sms.n
                em.emit("scalar", lambda e, p_=p_: e.activation(
                    rrow[:, 2 * p_ + 1, :], rrow[:, 2 * p_, :], AF.Ln,
                    bias=s["cpool"][0:1, 1:2]),
                    waits=[(sms, sms.n), (ssrow, srp[p_])])
                em.emit("scalar", lambda e, p_=p_: e.activation(
                    rrow[:, 2 * p_ + 1, :], rrow[:, 2 * p_ + 1, :], AF.Exp,
                    scale=s["cpool"][0:1, 2:3]),
                    inc=srst)
                rstp[p_] = srst.n
                if role == 0:
                    em.emit("vector", lambda e, h=h, p_=p_: e.tensor_scalar(
                        rrow[:, 2 * p_, :], rrow[:, 2 * p_ + 1, :],
                        s["qgain_sb"][0:1, h:h + 1],
                        ISQ_HD, ALU.mult, ALU.mult),
                        waits=[(srst, srst.n), (sscp, scpp[p_])], inc=ssrow)
                else:
                    em.emit("vector", lambda e, p_=p_: e.tensor_copy(
                        rrow[:, 2 * p_, :], rrow[:, 2 * p_ + 1, :]),
                        waits=[(srst, srst.n), (sscp, scpp[p_])], inc=ssrow)
                srp[p_] = ssrow.n
                sg = scp_g; scp_g += 1
                pscp = pU[sg % 2]
                em.emit("tensor", lambda e, pscp=pscp, p_=p_: e.matmul(
                    pscp[:, :], s["ones_row"][:, :], rrow[:, 2 * p_, :],
                    start=True, stop=True),
                    waits=[(ssrow, ssrow.n)] + ([(srope, sg - 1)] if sg >= 2 else []),
                    inc=sscp)
                scpp[p_] = sscp.n
                em.emit("vector", lambda e, pscp=pscp: e.tensor_tensor(
                    qn_sb[:, :], qraw[:, :], pscp[:, :], ALU.mult),
                    waits=[(sscp, sscp.n)])
                if role == 0:
                    w_rope = [(qst, 16 * (h - 1))] if (h >= 2 and j == 0) else []
                    out64 = qf_hb[:, hi % 2][0:64, jsl]
                    outp_d, outp_s = qf_hb[:, hi % 2][64:128, jsl], None
                else:
                    w_rope = []
                    out64 = k_sb[0:64, h, jsl]
                    outp_d = k_sb[64:128, h, jsl]
                if qg == 0:
                    w_rope = w_rope + [(tld, 32)]
                em.emit("vector", lambda e, jsl=jsl: e.tensor_mul(
                    rt1[:, :], qn_sb[0:64, :], cs_sb[:, jsl]), waits=w_rope)
                em.emit("vector", lambda e, jsl=jsl: e.tensor_mul(
                    rt2[0:32, :], qn_sb[32:64, :], ss_sb[32:64, jsl]))
                em.emit("vector", lambda e, jsl=jsl: e.tensor_mul(
                    rt2[32:64, :], qn_sb[0:32, :], ss_sb[0:32, jsl]))
                em.emit("vector", lambda e, o=out64: e.tensor_tensor(
                    o, rt1[:, :], rt2[:, :], ALU.add))
                em.emit("vector", lambda e, o=outp_d: e.tensor_copy(
                    o, qn_sb[64:128, :]), inc=srope)
        last_mm_for_wbuf[hi] = gmm.n
        if role == 0:
            em.emit("sync", lambda e, h=h, hi=hi: e.dma_start(
                out=t["q_d"][h], in_=qf_hb[:, hi % 2]),
                waits=[(srope, srope.n)], inc=qst)

    if DEBUG:
        st = sems["dbg"]
        em.emit("sync", lambda e: e.dma_start(out=t["q_dbg"][:, :, :], in_=t["q_d"][:, :, :]),
                waits=[(qst, qst.n)], inc=st)
        for a in range(KVH):
            em.emit("sync", lambda e, a=a: e.dma_start(out=t["k_dbg"][a], in_=k_sb[:, a, :]),
                    waits=[(srope, srope.n)], inc=st)
        em.emit("sync", lambda e: e.dma_start(out=t["v_dbg"][:, :, :], in_=v_sb[:, :, :]),
                waits=[(svcp, svcp.n)], inc=st)
        em.wait("sync", st, st.n)


def ph3_attn(nc, t, s, em, sems, es, kv):
    sb = lambda n, shp, dt: es.enter_context(nc.sbuf_tensor(n, list(shp), dt))
    psf = lambda n, shp: es.enter_context(nc.psum_tensor(n, list(shp), F32))
    psb = lambda n, shp: es.enter_context(nc.psum_tensor(n, list(shp), BF16))

    k_sb, v_sb = kv["k_sb"], kv["v_sb"]
    qh = sb("p3_qh", (128, 4, T), BF16)
    oh = sb("p3_oh", (128, 4, T), BF16)
    probs = sb("p3_probs", (128, 2, T), BF16)
    pt_sb = sb("p3_pt", (128, 2, T), BF16)
    nmx = sb("p3_nmx", (128, 2), F32)
    ssum = sb("p3_ssum", (128, 2), F32)
    rc = sb("p3_rc", (128, 2), F32)

    NSC = 5
    scb = psf("p3_scb", (128, NSC, TT))
    scf = scb[:, :, :].rearrange("p a b -> p (a b)")
    tp = [psb(f"p3_tp{i}", (128, 1024)) for i in range(2)]
    op = psf("p3_op0", (128, 512))

    mk = sems["mk"]
    qld = [sems["mk"]("p3qld0", 16), sems["mk"]("p3qld1", 16)]
    smm = mk("p3mm"); smsk = mk("p3msk")
    smax = mk("p3max"); sexp = mk("p3exp")
    srcp = mk("p3rcp"); spn = mk("p3pn"); str_ = mk("p3tr")
    sptc = mk("p3ptc"); sav = mk("p3av"); soev = mk("p3oev")
    sost = [mk("p3ost0", 16), mk("p3ost1", 16)]
    p2qst = sems["p2qst"]

    slot_free = [0] * NSC
    last_tr = [0, 0]
    last_av = [0, 0]
    last_exp_w = [0, 0]
    tp_last = [0, 0]
    gtb = 0
    cur = 0

    for g in range(H // 2):
        kvh = g
        pp = (g % 2) * 2
        nq = [0, 0]
        for w in range(2):
            em.emit("sync", lambda e, w=w, g=g, pp=pp: e.dma_start(
                out=qh[:, pp + w], in_=t["q_d"][2 * g + w]),
                waits=[(p2qst, 256)] + ([(smm, smm.n)] if g >= 2 and w == 0 else []),
                inc=qld[g % 2])
        n_q = qld[g % 2].n
        for qt in range(16):
            cols = (qt + 1) * 128
            ngr = (cols + TT - 1) // TT
            s0s = [0, 0]

            def st_scores(w, qt=qt, cols=cols, ngr=ngr, kvh=kvh, pp=pp, n_q=n_q, gpar=g % 2):
                nonlocal cur
                if cur + ngr > NSC:
                    cur = 0
                s0 = cur; cur += ngr
                s0s[w] = s0
                for j in range(ngr):
                    ncol = min(TT, cols - j * TT)
                    em.emit("tensor", lambda e, j=j, sl=s0 + j, ncol=ncol, w=w,
                            qt=qt, kvh=kvh, pp=pp: e.matmul(
                        scb[:, sl, 0:ncol], qh[:, pp + w, qt * 128:(qt + 1) * 128],
                        k_sb[:, kvh, j * TT:j * TT + ncol], start=True, stop=True),
                        waits=[(qld[gpar], n_q), (sexp, slot_free[s0 + j])],
                        inc=(smm if j == ngr - 1 else None))

            def st_mask(w, qt=qt, cols=cols):
                s0 = s0s[w]
                dgr = (cols - 128) // TT
                dcol = (cols - 128) - dgr * TT
                em.emit("vector", lambda e, s0=s0, dgr=dgr, dcol=dcol: e.tensor_tensor(
                    scb[:, s0 + dgr, dcol:dcol + 128], scb[:, s0 + dgr, dcol:dcol + 128],
                    s["cm_sb"][:, :], ALU.add),
                    waits=[(smm, smm.n)], inc=smsk)

            def st_max(w, cols=cols):
                s0 = s0s[w]
                em.emit("vector", lambda e, s0=s0, cols=cols, w=w: e.reduce_max(
                    nmx[:, w:w + 1], scf[:, s0 * TT:s0 * TT + cols], axis=AX.X,
                    negate=True),
                    waits=[(smsk, smsk.n), (sexp, last_exp_w[w])], inc=smax)

            def st_exp(w, cols=cols, ngr=ngr):
                s0 = s0s[w]
                em.emit("scalar", lambda e, s0=s0, cols=cols, w=w: e.activation(
                    probs[:, w, 0:cols], scf[:, s0 * TT:s0 * TT + cols], AF.Exp,
                    bias=nmx[:, w:w + 1], accum_out=ssum[:, w:w + 1]),
                    waits=[(smax, smax.n), (str_, last_tr[w])], inc=sexp)
                for j in range(ngr):
                    slot_free[s0 + j] = sexp.n
                last_exp_w[w] = sexp.n

            def st_norm(w, cols=cols):
                em.emit("vector", lambda e, w=w: e.reciprocal(
                    rc[:, w:w + 1], ssum[:, w:w + 1]),
                    waits=[(sexp, sexp.n)], inc=srcp)
                em.emit("vector", lambda e, cols=cols, w=w: e.tensor_scalar(
                    probs[:, w, 0:cols], probs[:, w, 0:cols], rc[:, w:w + 1],
                    None, ALU.mult),
                    waits=[(srcp, srcp.n)], inc=spn)

            def st_tr(w, qt=qt):
                nonlocal gtb
                ntb = (qt + 4) // 4
                for tb in range(ntb):
                    kts = range(tb * 4, min(tb * 4 + 4, qt + 1))
                    bi = gtb % 2; gtb += 1
                    tpb = tp[bi]
                    for i, kt in enumerate(kts):
                        wtr = []
                        if i == 0:
                            wtr = [(spn, spn.n), (sptc, tp_last[bi])]
                            if tb == 0:
                                wtr.append((sav, last_av[w]))
                        em.emit("tensor", lambda e, i=i, kt=kt, tpb=tpb, w=w: e.transpose(
                            tpb[:, i * 128:(i + 1) * 128],
                            probs[:, w, kt * 128:(kt + 1) * 128], s["id_sb"][:, :]),
                            waits=wtr, inc=(str_ if i == len(kts) - 1 else None))
                    em.emit("vector", lambda e, tpb=tpb, nk=len(kts), tb=tb, w=w: e.tensor_copy(
                        pt_sb[:, w, tb * TT:tb * TT + nk * 128], tpb[:, 0:nk * 128]),
                        waits=[(str_, str_.n)], inc=sptc)
                    tp_last[bi] = sptc.n
                last_tr[w] = str_.n

            def st_av(w, qt=qt, kvh=kvh):
                for kt in range(qt + 1):
                    w_av = []
                    if kt == 0:
                        w_av = [(sptc, sptc.n), (soev, soev.n)]
                    em.emit("tensor", lambda e, kt=kt, kvh=kvh, qt=qt, w=w: e.matmul(
                        op[:, w * 128:(w + 1) * 128],
                        v_sb[:, kt, kvh * 128:(kvh + 1) * 128],
                        pt_sb[:, w, kt * 128:(kt + 1) * 128],
                        start=(kt == 0), stop=(kt == qt)),
                        waits=w_av, inc=(sav if kt == qt else None))
                last_av[w] = sav.n

            def st_evac(w, qt=qt, pp=pp, gpar=g % 2, g_=g):
                em.emit("scalar", lambda e, qt=qt, w=w, pp=pp: e.activation(
                    oh[:, pp + w][:, qt * 128:(qt + 1) * 128],
                    op[:, w * 128:(w + 1) * 128], AF.Copy),
                    waits=[(sav, sav.n)] +
                          ([(sost[gpar], sost[gpar].n)] if (g_ >= 2 and qt == 0 and w == 0)
                           else []),
                    inc=soev)

            if PH3_INTER and 2 * ngr <= NSC:
                for st in (st_scores, st_mask, st_max, st_exp, st_norm, st_tr):
                    st(0); st(1)
                st_av(0); st_evac(0); st_av(1); st_evac(1)
            else:
                for w in range(2):
                    for st in (st_scores, st_mask, st_max, st_exp, st_norm, st_tr):
                        st(w)
                st_av(0); st_evac(0); st_av(1); st_evac(1)
        for w in range(2):
            em.emit("sync", lambda e, g=g, w=w, pp=pp: e.dma_start(
                out=t["o_d"][2 * g + w], in_=oh[:, pp + w]),
                waits=[(soev, soev.n - (1 - w))], inc=sost[g % 2])

    if DEBUG:
        st = sems["dbg"]
        em.emit("sync", lambda e: e.dma_start(out=t["o_dbg"][:, :, :], in_=t["o_d"][:, :, :]),
                waits=[(sost[0], sost[0].n), (sost[1], sost[1].n)], inc=st)
        em.wait("sync", st, st.n)


def emit_ln_stats(em, s, src_bf, sqb, s1p, s2p, ap_, bp_, rows, S,
                  w_src, w_psum_war, w_bc_war):
    em.emit("scalar", lambda e: e.activation(sqb[:, :, :], src_bf[:, :, :], AF.Square),
            waits=list(w_src), inc=S["sq"])
    for c in range(16):
        em.emit("tensor", lambda e, c=c: e.matmul(
            s1p[0:1, :], s["ones_col"][:, :], src_bf[:, c, :],
            start=(c == 0), stop=(c == 15)),
            waits=([(S["sq"], S["sq"].n)] + list(w_psum_war)) if c == 0 else [])
    for c in range(16):
        em.emit("tensor", lambda e, c=c: e.matmul(
            s2p[0:1, :], s["ones_col"][:, :], sqb[:, c, :],
            start=(c == 0), stop=(c == 15)),
            inc=(S["stat"] if c == 15 else None))
    em.emit("vector", lambda e: e.tensor_scalar(
        rows[:, 0, :], s1p[0:1, :], 1.0 / D, None, ALU.mult),
        waits=[(S["stat"], S["stat"].n), (S["rstd"], S["rstd"].n)])
    em.emit("vector", lambda e: e.tensor_mul(rows[:, 1, :], rows[:, 0, :], rows[:, 0, :]))
    em.emit("vector", lambda e: e.scalar_tensor_tensor(
        rows[:, 2, :], s2p[0:1, :], 1.0 / D, rows[:, 1, :],
        ALU.mult, ALU.subtract), inc=S["var"])
    em.emit("scalar", lambda e: e.activation(
        rows[:, 3, :], rows[:, 2, :], AF.Ln, bias=s["cpool"][0:1, 0:1]),
        waits=[(S["var"], S["var"].n)])
    em.emit("scalar", lambda e: e.activation(
        rows[:, 3, :], rows[:, 3, :], AF.Exp, scale=s["cpool"][0:1, 2:3]),
        inc=S["rstd"])
    em.emit("vector", lambda e: e.tensor_mul(
        rows[:, 4, :], rows[:, 0, :], rows[:, 3, :]),
        waits=[(S["rstd"], S["rstd"].n)], inc=S["nmr"])
    em.emit("tensor", lambda e: e.matmul(
        ap_[:, :], s["ones_row"][:, :], rows[:, 3, :], start=True, stop=True),
        waits=[(S["nmr"], S["nmr"].n)] + list(w_bc_war))
    em.emit("tensor", lambda e: e.matmul(
        bp_[:, :], s["ones_row"][:, :], rows[:, 4, :], start=True, stop=True),
        inc=S["bc"])
    return S["bc"].n


def ph4_woln(nc, t, s, em, sems, es):
    sb = lambda n, shp, dt: es.enter_context(nc.sbuf_tensor(n, list(shp), dt))
    psf = lambda n, shp: es.enter_context(nc.psum_tensor(n, list(shp), F32))

    o_sb = sb("p4_o", (128, 2, 16, TT), BF16)
    wob = sb("p4_wob", (128, 2, 16, 128), BF16)
    uo_sb = sb("p4_uo", (R, TT), BF16)
    ev4 = sb("p4_ev", (128, TT), F32)
    ao_sb = sb("p4_ao", (128, 16, TT), F32)
    abf = sb("p4_abf", (128, 16, TT), BF16)
    sq4 = sb("p4_sq", (128, 16, TT), BF16)
    xf = sb("p4_x", (128, 16, TT), F32)
    ln2t = sb("p4_ln2", (128, 16, TT), BF16)
    rows = sb("p4_rows", (1, 5, TT), F32)
    tmp = sb("p4_tmp", (128, TT), F32)
    l2f = sb("p4_l2f", (128, 2, TT), F32)
    glit = sb("p4_glit", (16, TT), F32)

    pwW = [psf("p4_w0", (128, TT)), psf("p4_w1", (128, TT))]
    pwL = psf("p4_l", (128, TT))
    pu = psf("p4_u", (R, TT))
    s1p = psf("p4_s1", (1, TT)); s2p = psf("p4_s2", (1, TT))
    ap_ = psf("p4_a", (128, TT)); bp_ = psf("p4_b", (128, TT))

    mk = sems["mk"]
    sold = mk("p4old", 16); swo = mk("p4wo", 16); sxl = mk("p4xl", 16)
    su4 = mk("p4umm"); sue4 = mk("p4uev")
    sm4 = mk("p4mm"); sc4 = mk("p4sc"); sd4 = mk("p4ad")
    SA = sems["LS"]
    SB = sems["LS"]
    sxa = mk("p4xa"); scast2 = mk("p4c2"); sla = mk("p4la")
    sx2st = mk("p4x2st", 16); sln2st = mk("p4l2st", 16)
    sgmm = mk("p4gmm"); sgc = mk("p4gc"); sgev = mk("p4gev"); sgst = mk("p4gst", 16)
    slf = mk("p4lf"); slc = mk("p4lc")
    p3ost0 = sems["p3ost0"]; p3ost1 = sems["p3ost1"]

    for tt in range(NTT):
        ob = o_sb[:, tt % 2]
        jsl = slice(tt * TT, (tt + 1) * TT)
        for hh in range(16):
            em.emit("sync", lambda e, hh=hh, tt=tt, ob=ob: e.dma_start(
                out=ob[:, hh, :], in_=t["o_d"][hh][:, tt * TT:(tt + 1) * TT]),
                waits=(([(p3ost0, 128), (p3ost1, 128)] if tt == 0 else []) +
                       ([(sm4, sm4.n)] if (tt >= 2 and hh == 0) else [])),
                inc=sold)
        n_o = sold.n
        em.emit("sync", lambda e, tt=tt: e.dma_start(out=xf[:, :, :], in_=t["x_t"][tt]),
                waits=[(sx2st, sx2st.n), (sla, sla.n)], inc=sxl)
        n_x = sxl.n
        for c in range(16):
            em.emit("tensor", lambda e, c=c, ob=ob: e.matmul(
                pu[0:R, :], s["lora_a"][:, 3, c, :], ob[:, c, :],
                start=(c == 0), stop=(c == 15)),
                waits=([(sold, n_o)] + ([(sue4, sue4.n), (sgev, sgev.n)] if tt >= 1 else []))
                if c == 0 else [],
                inc=(su4 if c == 15 else None))
        em.emit("scalar", lambda e: e.copy(uo_sb[:, :], pu[0:R, :]),
                waits=[(su4, su4.n)], inc=sue4)
        for oc in range(16):
            wb = wob[:, oc % 2]
            em.emit("gpsimd", lambda e, oc=oc, wb=wb: e.dma_start(
                out=wb, in_=t["wo_t"][oc]),
                waits=([(sm4, sm4.n - 1)] if (tt * 16 + oc) >= 2 else []),
                inc=swo)
            n_w = swo.n
            g = tt * 16 + oc
            pw = pwW[g % 2]
            for c in range(16):
                em.emit("tensor", lambda e, c=c, pw=pw, wb=wb, ob=ob: e.matmul(
                    pw[:, :], wb[:, c, :], ob[:, c, :],
                    start=(c == 0), stop=(c == 15)),
                    waits=([(swo, n_w)] + ([(sc4, g - 1)] if g >= 2 else []))
                    if c == 0 else [])
            em.emit("tensor", lambda e, oc=oc: e.matmul(
                pwL[:, :], s["bo_sb"][:, oc, :], uo_sb[:, :], start=True, stop=True),
                waits=[(sue4, sue4.n)] + ([(sd4, g)] if g >= 1 else []),
                inc=sm4)
            em.emit("scalar", lambda e, pw=pw: e.activation(
                ev4[:, :], pw[:, :], AF.Copy, scale=s["scol"][:, 3:4]),
                waits=[(sm4, sm4.n)] + ([(sd4, g)] if g >= 1 else []),
                inc=sc4)
            em.emit("vector", lambda e, oc=oc: e.tensor_tensor(
                ao_sb[:, oc, :], ev4[:, :], pwL[:, :], ALU.add),
                waits=[(sc4, sc4.n)], inc=sd4)
        em.emit("scalar", lambda e: e.activation(abf[:, :, :], ao_sb[:, :, :], AF.Copy),
                waits=[(sd4, sd4.n)], inc=SA["sq"])
        emit_ln_stats(em, s, abf, sq4, s1p, s2p, ap_, bp_, rows, SA,
                      w_src=[],
                      w_psum_war=([(SB["var"], SB["var"].n)] if tt >= 1 else []),
                      w_bc_war=([(sla, sla.n)] if tt >= 1 else []))
        for c in range(16):
            em.emit("vector", lambda e, c=c: e.tensor_mul(tmp[:, :], ao_sb[:, c, :], ap_[:, :]),
                    waits=([(SA["bc"], SA["bc"].n), (sxl, n_x)] if c == 0 else []))
            em.emit("vector", lambda e, c=c: e.tensor_tensor(
                tmp[:, :], tmp[:, :], bp_[:, :], ALU.subtract))
            em.emit("vector", lambda e, c=c: e.tensor_tensor(
                xf[:, c, :], xf[:, c, :], tmp[:, :], ALU.add),
                inc=(sxa if c == 15 else None))
        em.emit("sync", lambda e, tt=tt: e.dma_start(out=t["x2_d"][tt], in_=xf[:, :, :]),
                waits=[(sxa, sxa.n)], inc=sx2st)
        em.emit("scalar", lambda e: e.activation(abf[:, :, :], xf[:, :, :], AF.Copy),
                waits=[(sxa, sxa.n)], inc=scast2)
        emit_ln_stats(em, s, abf, sq4, s1p, s2p, ap_, bp_, rows, SB,
                      w_src=[(scast2, scast2.n)],
                      w_psum_war=[(SA["var"], SA["var"].n)],
                      w_bc_war=[(sxa, sxa.n)])
        for c in range(16):
            lf = l2f[:, c % 2]
            em.emit("vector", lambda e, c=c: e.tensor_mul(tmp[:, :], xf[:, c, :], ap_[:, :]),
                    waits=([(SB["bc"], SB["bc"].n), (sln2st, sln2st.n)] if c == 0 else []))
            g4 = tt * 16 + c
            em.emit("vector", lambda e, c=c, lf=lf: e.tensor_tensor(
                lf[:, :], tmp[:, :], bp_[:, :], ALU.subtract),
                waits=([(sgc, g4 - 1)] if g4 >= 2 else []),
                inc=slf)
            em.emit("scalar", lambda e, c=c, lf=lf: e.activation(
                ln2t[:, c, :], lf[:, :], AF.Copy),
                waits=[(slf, slf.n)] + ([(sln2st, sln2st.n)] if c == 0 else []),
                inc=(sla if c == 15 else slc))
            em.emit("tensor", lambda e, c=c, lf=lf: e.matmul(
                pu[0:16, :], s["gate_sb"][:, c, :], lf[:, :],
                start=(c == 0), stop=(c == 15)),
                waits=[(slf, slf.n)] + ([(sue4, sue4.n), (sgev, sgev.n)] if c == 0 else []),
                inc=sgc)
        em.emit("scalar", lambda e: e.copy(glit[:, :], pu[0:16, :]),
                waits=[(sgc, sgc.n)] + ([(sgst, sgst.n)] if tt >= 1 else []),
                inc=sgev)
        em.emit("sync", lambda e, tt=tt: e.dma_start(out=t["glog_d"][tt], in_=glit[:, :]),
                waits=[(sgev, sgev.n)], inc=sgst)
        em.emit("sync", lambda e, tt=tt: e.dma_start(out=t["ln2_d"][tt], in_=ln2t[:, :, :]),
                waits=[(sla, sla.n)], inc=sln2st)

    if DEBUG:
        st = sems["dbg"]
        for tt in range(NTT):
            em.emit("sync", lambda e, tt=tt: e.dma_start(
                out=t["x2_dbg"][tt], in_=t["x2_d"][tt]), waits=[(sx2st, sx2st.n)], inc=st)
            em.emit("sync", lambda e, tt=tt: e.dma_start(
                out=t["ln2_dbg"][tt], in_=t["ln2_d"][tt]), waits=[(sln2st, sln2st.n)], inc=st)
        em.wait("sync", st, st.n)


def ph5_ffn(nc, t, s, em, sems, es):
    sb = lambda n, shp, dt: es.enter_context(nc.sbuf_tensor(n, list(shp), dt))
    psf = lambda n, shp: es.enter_context(nc.psum_tensor(n, list(shp), F32))
    TH = 2 * TT

    ln2h = sb("p5_ln2h", (128, 16, TH), BF16)
    y_acc = sb("p5_y", (128, 16, TH), F32)
    h_sb = sb("p5_h", (128, MCH, TH), BF16)
    w13 = sb("p5_w13", (128, 3, 2, 16, 128), BF16)
    w2b = sb("p5_w2", (128, 2, MCH, 128), BF16)
    oneh = sb("p5_oneh", (16, NE, 128), F32)
    glog = sb("p5_glog", (16, 2, TT), F32)
    gtok = sb("p5_gtok", (128, 8, NE), F32)
    gw_tok = sb("p5_gw", (128, 8, NE), F32)
    wfm = sb("p5_wfm", (16, 2, TT), F32)
    grow = sb("p5_grow", (128, 4), F32)
    gl2 = sb("p5_gl2", (128, NE), F32)
    sil_sb = sb("p5_sil", (128, 2, TT), BF16)
    tmp5 = sb("p5_tmp", (128, TT), F32)
    sq5 = sb("p5_sq", (128, 16, TT), BF16)
    rows = sb("p5_rows", (1, 5, TT), F32)
    xc = sb("p5_xc", (128, 2, TT), F32)
    outc = sb("p5_outc", (128, 2, TT), F32)

    hp = [psf("p5_hA", (128, TT)), psf("p5_hB", (128, TT))]
    hp3 = [psf("p5_hC", (128, TT)), psf("p5_hD", (128, TT))]
    yp = [psf("p5_yE", (128, TT)), psf("p5_yF", (128, TT))]
    gp = [psf("p5_gG", (128, TT)), psf("p5_gH", (128, TT))]

    mk = sems["mk"]
    sl2l = mk("p5l2l", 16); sgl = mk("p5gl", 16); sw13 = mk("p5w13", 16)
    sw2l = mk("p5w2l", 16); sxr = mk("p5xr", 16); sout = mk("p5out", 16)
    sgt = mk("p5gt"); sgtc = mk("p5gtc"); sgm1 = mk("p5gm1")
    sgm2 = mk("p5gm2"); sgnm = mk("p5gnm"); sgex = mk("p5gex")
    sga = mk("p5ga"); sgb = mk("p5gb"); sgc5 = mk("p5gc5"); sgd = mk("p5gd")
    sgrc = mk("p5grc"); sgw = mk("p5gw"); sgwt = mk("p5gwt"); swfm = mk("p5wfm")
    sbc = mk("p5bc"); sh5 = mk("p5hmm"); ssil = mk("p5sil")
    she = mk("p5he"); sy5 = mk("p5ymm"); syd = mk("p5yd")
    SC = sems["LS"]
    sfa = mk("p5fa")
    sol = mk("p5onl", 16)
    p4l2 = sems["p4l2st"]; p4gst = sems["p4gst"]; p4x2 = sems["p4x2st"]

    em.emit("sync", lambda e: e.dma_start(out=oneh[:, :, :], in_=t["onehot"][:, :, :]),
            inc=sol)
    n_oneh = sol.n

    for hf in range(2):
        for u in range(2):
            tt = 2 * hf + u
            em.emit("sync", lambda e, tt=tt, u=u: e.dma_start(
                out=ln2h[:, :, u * TT:(u + 1) * TT], in_=t["ln2_d"][tt]),
                waits=[(p4l2, 64)] + ([(sh5, sh5.n)] if hf >= 1 and u == 0 else []),
                inc=sl2l)
            em.emit("sync", lambda e, tt=tt, u=u: e.dma_start(
                out=glog[:, u, :], in_=t["glog_d"][tt]),
                waits=[(p4gst, 64)] + ([(sgt, sgt.n)] if hf >= 1 and u == 0 else []),
                inc=sgl)
        n_l2 = sl2l.n
        n_gl = sgl.n
        for sl in range(8):
            u, sli = sl // 4, sl % 4
            wtr = [(sgl, n_gl)]
            if (hf, sl) != (0, 0):
                wtr.append((sgtc, sgtc.n))
            if hf >= 1 and sl == 0:
                wtr.append((SC["var"], SC["var"].n))
            em.emit("tensor", lambda e, u=u, sli=sli: e.transpose(
                hp[0][:, 0:16], glog[:, u, sli * 128:(sli + 1) * 128],
                s["idf_sb"][0:16, 0:16]),
                waits=wtr, inc=sgt)
            em.emit("vector", lambda e, sl=sl: e.tensor_copy(
                gtok[:, sl, :], hp[0][:, 0:16]),
                waits=[(sgt, sgt.n)] +
                      ([(sgex, sgex.n)] if hf >= 1 and sl == 0 else []),
                inc=sgtc)
        for sl in range(8):
            em.emit("vector", lambda e, sl=sl: e.reduce_max(
                grow[:, 0:1], gtok[:, sl, :], axis=AX.X), inc=sgm1)
            em.emit("vector", lambda e, sl=sl: e.tensor_scalar(
                gl2[:, :], gtok[:, sl, :], grow[:, 0:1], None, ALU.is_ge),
                waits=[(sgm1, sgm1.n)], inc=sga)
            em.emit("vector", lambda e, sl=sl: e.scalar_tensor_tensor(
                gl2[:, :], gl2[:, :], NEG_BIG, gtok[:, sl, :], ALU.mult, ALU.add),
                waits=[(sga, sga.n)], inc=sgb)
            em.emit("vector", lambda e: e.reduce_max(
                grow[:, 1:2], gl2[:, :], axis=AX.X),
                waits=[(sgb, sgb.n)], inc=sgm2)
            em.emit("vector", lambda e, sl=sl: e.tensor_scalar(
                gl2[:, :], gtok[:, sl, :], grow[:, 1:2], None, ALU.is_ge),
                waits=[(sgm2, sgm2.n)], inc=sgc5)
            em.emit("vector", lambda e: e.tensor_scalar(
                grow[:, 0:1], grow[:, 0:1], -1.0, None, ALU.mult), inc=sgnm)
            em.emit("scalar", lambda e, sl=sl: e.activation(
                gw_tok[:, sl, :], gtok[:, sl, :], AF.Exp, bias=grow[:, 0:1],
                accum_out=grow[:, 2:3]),
                waits=[(sgnm, sgnm.n)], inc=sgex)
            em.emit("vector", lambda e: e.reciprocal(grow[:, 3:4], grow[:, 2:3]),
                    waits=[(sgex, sgex.n)], inc=sgrc)
            em.emit("vector", lambda e, sl=sl: e.tensor_scalar(
                gw_tok[:, sl, :], gw_tok[:, sl, :], grow[:, 3:4], None, ALU.mult),
                waits=[(sgrc, sgrc.n)], inc=sgd)
            em.emit("vector", lambda e, sl=sl: e.tensor_mul(
                gw_tok[:, sl, :], gw_tok[:, sl, :], gl2[:, :]),
                waits=[(sgd, sgd.n), (sgc5, sgc5.n)], inc=sgw)
        for sl in range(8):
            u, sli = sl // 4, sl % 4
            wtr = [(sgw, sgw.n)]
            if (hf, sl) != (0, 0):
                wtr.append((swfm, swfm.n))
            if hf >= 1 and sl == 0:
                wtr.append((SC["var"], SC["var"].n))
            em.emit("tensor", lambda e, sl=sl: e.transpose(
                hp[1][0:16, 0:128], gw_tok[:, sl, :], s["idf_sb"][:, :]),
                waits=wtr, inc=sgwt)
            em.emit("vector", lambda e, u=u, sli=sli: e.tensor_copy(
                wfm[0:16, u, sli * 128:(sli + 1) * 128], hp[1][0:16, 0:128]),
                waits=[(sgwt, sgwt.n)], inc=swfm)
        for e_ in range(PH5_NEXP):
            routed = e_ < NE
            if routed:
                for u in range(2):
                    em.emit("tensor", lambda e, e_=e_, u=u: e.matmul(
                        gp[u][:, :], oneh[:, e_, :], wfm[:, u, :],
                        start=True, stop=True),
                        waits=([(sol, n_oneh), (swfm, swfm.n), (she, she.n)]
                               if u == 0 else []),
                        inc=sbc)
            for m in range(MCH):
                wg = (hf * (NE + 2) + e_) * MCH + m
                r = wg % 3
                if routed:
                    src1, src3 = t["w1_t"][e_, m], t["w3_t"][e_, m]
                else:
                    src1 = t["sw1_t"][(e_ - NE) * MCH + m]
                    src3 = t["sw3_t"][(e_ - NE) * MCH + m]
                em.emit("gpsimd", lambda e, o=w13[:, r, 0], i=src1: e.dma_start(out=o, in_=i),
                        waits=([(sh5, 2 * (wg - 2))] if wg >= 3 else []), inc=sw13)
                em.emit("gpsimd", lambda e, o=w13[:, r, 1], i=src3: e.dma_start(out=o, in_=i),
                        inc=sw13)
                n_w = sw13.n
                for u in range(2):
                    idx = m * 2 + u
                    h1, h3 = hp[idx % 2], hp3[idx % 2]
                    usl = slice(u * TT, (u + 1) * TT)
                    w0 = [(sw13, n_w), (sl2l, n_l2)]
                    if ssil.n >= 2:
                        w0.append((ssil, ssil.n - 1))
                    if she.n >= 2:
                        w0.append((she, she.n - 1))
                    if e_ == 0 and m == 0 and u == 0 and hf >= 1:
                        w0.append((sfa, sfa.n))
                    if e_ == 0 and m == 0 and u == 0:
                        w0.append((sgtc, sgtc.n))
                    if e_ == 0 and m == 0 and u == 1:
                        w0.append((swfm, swfm.n))
                    for c in range(16):
                        em.emit("tensor", lambda e, c=c, h1=h1, r=r, usl=usl: e.matmul(
                            h1[:, :], w13[:, r, 0, c, :], ln2h[:, c, usl],
                            start=(c == 0), stop=(c == 15)),
                            waits=(w0 if c == 0 else []))
                    for c in range(16):
                        em.emit("tensor", lambda e, c=c, h3=h3, r=r, usl=usl: e.matmul(
                            h3[:, :], w13[:, r, 1, c, :], ln2h[:, c, usl],
                            start=(c == 0), stop=(c == 15)),
                            inc=(sh5 if c == 15 else None))
                    sil = sil_sb[:, idx % 2]
                    em.emit("scalar", lambda e, h1=h1, sil=sil: e.activation(
                        sil[:, :], h1[:, :], AF.Silu),
                        waits=[(sh5, sh5.n)] + ([(she, she.n - 1)] if she.n >= 2 else []),
                        inc=ssil)
                    wh = [(ssil, ssil.n)]
                    if m == 0 and u == 0:
                        wh.append((sy5, sy5.n))
                        if routed:
                            wh.append((sbc, sbc.n))
                    if routed:
                        em.emit("vector", lambda e, h3=h3, sil=sil: e.tensor_tensor(
                            tmp5[:, :], sil[:, :], h3[:, :], ALU.mult), waits=wh)
                        em.emit("vector", lambda e, m=m, u=u, usl=usl: e.tensor_tensor(
                            h_sb[:, m, usl], tmp5[:, :], gp[u][:, :], ALU.mult),
                            inc=she)
                    else:
                        em.emit("vector", lambda e, h3=h3, m=m, usl=usl, sil=sil: e.tensor_tensor(
                            h_sb[:, m, usl], sil[:, :], h3[:, :], ALU.mult),
                            waits=wh, inc=she)
            for dc in range(16):
                wg2 = (hf * (NE + 2) + e_) * 16 + dc
                r2 = wg2 % 2
                src2 = t["w2_t"][e_, dc] if routed else t["sw2_t"][dc, e_ - NE]
                em.emit("gpsimd", lambda e, o=w2b[:, r2], i=src2: e.dma_start(out=o, in_=i),
                        waits=([(sy5, 2 * (wg2 - 1))] if wg2 >= 2 else []), inc=sw2l)
                n_w2 = sw2l.n
                for u in range(2):
                    idx2 = dc * 2 + u
                    ypb = yp[idx2 % 2]
                    usl = slice(u * TT, (u + 1) * TT)
                    wv = [(sw2l, n_w2)]
                    if dc == 0 and u == 0:
                        wv.append((she, she.n))
                    if syd.n >= 2:
                        wv.append((syd, syd.n - 1))
                    for m in range(MCH):
                        em.emit("tensor", lambda e, m=m, ypb=ypb, r2=r2, usl=usl: e.matmul(
                            ypb[:, :], w2b[:, r2, m, :], h_sb[:, m, usl],
                            start=(m == 0), stop=(m == MCH - 1)),
                            waits=(wv if m == 0 else []),
                            inc=(sy5 if m == MCH - 1 else None))
                    wy = [(sy5, sy5.n)]
                    if e_ == 0 and dc == 0 and u == 0 and hf >= 1:
                        wy.append((sfa, sfa.n))
                    if e_ == 0:
                        em.emit("vector", lambda e, dc=dc, ypb=ypb, usl=usl: e.tensor_copy(
                            y_acc[:, dc, usl], ypb[:, :]), waits=wy, inc=syd)
                    else:
                        em.emit("vector", lambda e, dc=dc, ypb=ypb, usl=usl: e.tensor_tensor(
                            y_acc[:, dc, usl], y_acc[:, dc, usl], ypb[:, :], ALU.add),
                            waits=wy, inc=syd)
        if not PH5_FINAL:
            continue
        ybf = h_sb[:, :, :].rearrange("p a b -> p (a b)")[:, 0:16 * TT].rearrange(
            "p (c s) -> p c s", c=16)
        for u in range(2):
            tt = 2 * hf + u
            usl = slice(u * TT, (u + 1) * TT)
            em.emit("scalar", lambda e, usl=usl, ybf=ybf: e.activation(
                ybf[:, :, :], y_acc[:, :, usl], AF.Copy),
                waits=[(syd, syd.n), (sy5, sy5.n)], inc=SC["sq"])
            emit_ln_stats(em, s, ybf, sq5, hp[0][0:1, :], hp[1][0:1, :],
                          hp3[0], hp3[1], rows, SC,
                          w_src=[],
                          w_psum_war=[(she, she.n), (SC["var"], SC["var"].n)],
                          w_bc_war=[(sfa, sfa.n)])
            for c in range(16):
                xcb = xc[:, c % 2]
                ocb = outc[:, c % 2]
                gc = (hf * 2 + u) * 16 + c
                if PH5_RES:
                    em.emit("sync", lambda e, tt=tt, c=c, xcb=xcb: e.dma_start(
                        out=xcb, in_=t["x2_d"][tt][:, c, :]),
                        waits=[(p4x2, 64)] + ([(sfa, gc - 1)] if gc >= 2 else []),
                        inc=sxr)
                n_xc = sxr.n
                em.emit("vector", lambda e, c=c, usl=usl: e.tensor_mul(
                    tmp5[:, :], y_acc[:, c, usl], hp3[0][:, :]),
                    waits=([(SC["bc"], SC["bc"].n)] if c == 0 else []))
                em.emit("vector", lambda e: e.tensor_tensor(
                    tmp5[:, :], tmp5[:, :], hp3[1][:, :], ALU.subtract))
                if PH5_RES:
                    em.emit("vector", lambda e, xcb=xcb, ocb=ocb, gc=gc: e.tensor_tensor(
                        ocb, xcb, tmp5[:, :], ALU.add),
                        waits=[(sxr, n_xc)] + ([(sout, 16 * (gc - 1))] if gc >= 2 else []),
                        inc=sfa)
                else:
                    em.emit("vector", lambda e, ocb=ocb, gc=gc: e.tensor_copy(
                        ocb, tmp5[:, :]),
                        waits=([(sout, 16 * (gc - 1))] if gc >= 2 else []),
                        inc=sfa)
                em.emit("sync", lambda e, tt=tt, c=c, ocb=ocb: e.dma_start(
                    out=t["out_t"][tt][:, c, :], in_=ocb),
                    waits=[(sfa, sfa.n)], inc=sout)
    em.wait("sync", sout, sout.n)
    em.wait("gpsimd", sout, sout.n)


BUILD_PHASES = ("ph1", "ph2", "ph3", "ph4", "ph5")


RACE_DET = True


def build_nc():
    nc = bass.Bass("TRN2", target_bir_lowering=False, detect_race_conditions=RACE_DET)
    t = declare_ios(nc)
    es = contextlib.ExitStack()
    with es:
        s = alloc_persistent(nc, es)
        es_lora = contextlib.ExitStack()
        for nm, shp in (("lora_a", [128, 4, 16, R]), ("bq_sb", [R, 16, 128]),
                        ("bk_sb", [R, 8, 128]), ("bv_sb", [R, 8, 128]),
                        ("bo_sb", [R, 16, 128])):
            s[nm] = es_lora.enter_context(nc.sbuf_tensor(nm, shp, BF16, side="right"))
        sems = {}
        for nm, step in (("ld0", 16), ("ld0s", 16), ("ev0", 1), ("xld", 16), ("ph1", 1),
                         ("dbg", 16), ("fin", 16)):
            sems[nm] = Sem(es, nc, nm, step)
        def _mk(nm, step=1):
            sems[nm] = Sem(es, nc, nm, step)
            return sems[nm]
        sems["mk"] = _mk
        sems["LS"] = {k: _mk("L" + k) for k in
                      ("sq", "stat", "var", "rstd", "nmr", "bc")}

        es0 = contextlib.ExitStack()
        with es0:
            em0 = Em()
            with nc.Block("ph0") as blk:
                ph0_setup(nc, t, s, em0, sems, es0)
                em0.install(blk)

        kv = {}
        es_kv = contextlib.ExitStack()
        with es_kv:
            if "ph2" in BUILD_PHASES:
                kv["k_sb"] = es_kv.enter_context(
                    nc.sbuf_tensor("k_sb", [128, KVH, T], BF16, side="right"))
                kv["v_sb"] = es_kv.enter_context(
                    nc.sbuf_tensor("v_sb", [128, 16, KVH * 128], BF16, side="right"))
            es_ln1 = contextlib.ExitStack()
            with es_ln1:
                if "ph1" in BUILD_PHASES:
                    em1 = Em()
                    es1loc = contextlib.ExitStack()
                    with es1loc:
                        with nc.Block("ph1") as blk:
                            ph1_ln1(nc, t, s, em1, sems, es_ln1, es1loc)
                            ln1 = es_ln1._ln1
                            em1.install(blk)
                if "ph2" in BUILD_PHASES:
                    es2 = contextlib.ExitStack()
                    with es2:
                        em2 = Em()
                        with nc.Block("ph2") as blk:
                            ph2_qkv(nc, t, s, em2, sems, es2, ln1, kv)
                            em2.install(blk)
            if "ph3" in BUILD_PHASES:
                es3 = contextlib.ExitStack()
                with es3:
                    em3 = Em()
                    with nc.Block("ph3") as blk:
                        ph3_attn(nc, t, s, em3, sems, es3, kv)
                        em3.install(blk)
        if "ph4" in BUILD_PHASES:
            es4 = contextlib.ExitStack()
            with es4:
                em4 = Em()
                with nc.Block("ph4") as blk:
                    ph4_woln(nc, t, s, em4, sems, es4)
                    em4.install(blk)
        es_lora.close()
        if "ph5" in BUILD_PHASES:
            es5 = contextlib.ExitStack()
            with es5:
                em5 = Em()
                with nc.Block("ph5") as blk:
                    ph5_ffn(nc, t, s, em5, sems, es5)
                    em5.install(blk)

        if "ph5" not in BUILD_PHASES or not PH5_FINAL:
            emf = Em()
            fin = sems["fin"]
            with nc.Block("dummy_out") as blk:
                for tt in range(NTT):
                    emf.emit("sync", lambda e, tt=tt: e.dma_start(
                        out=t["out_t"][tt], in_=t["x_t"][tt]), inc=fin)
                emf.wait("sync", fin, fin.n)
                emf.install(blk)
    return nc


def kernel(**inputs):
    global DEBUG
    nc = build_nc()
    w = pack_weights(inputs)
    consts = host_constants()
    xs = pack_x(np.asarray(inputs["L0_attn_x"], np.float32))
    in_maps = []
    for b in range(B):
        m = dict(w)
        m.update(consts)
        m["x_t"] = xs[b]
        in_maps.append(m)
    res = run_bass_kernel_spmd(nc, in_maps, core_ids=list(range(B)))
    outs = np.stack([unpack_out(res.results[b]["out_t"]) for b in range(B)])
    return outs.astype(np.float32)


# version 10
# speedup vs baseline: 1.0194x; 1.0194x over previous
import contextlib
import numpy as np
import ml_dtypes

import concourse.bass as bass
import concourse.mybir as mybir
from concourse.bass_utils import run_bass_kernel_spmd

F32 = mybir.dt.float32
BF16 = mybir.dt.bfloat16
AF = mybir.ActivationFunctionType
ALU = mybir.AluOpType
AX = mybir.AxisListType

B = 8
T = 2048
D = 2048
NC_CHUNK = 16
H, KVH, HD = 16, 8, 128
R = 32
NE, INTER, MCH = 16, 1408, 11
SI, SMCH = 2816, 22
TT = 512
NTT = T // TT
RMS_EPS = 1.1920929e-07
LN_EPS = 1e-5
ISQ_HD = float(1.0 / np.sqrt(128.0))
NEG_BIG = -1.0e30

DEBUG = False
PH5_NEXP = NE + 2
PH5_FINAL = True
PH5_RES = True
PH3_INTER = True


class Sem:
    def __init__(self, es, nc, name, step=1):
        self.h = es.enter_context(nc.semaphore(name))
        self.n = 0
        self.step = step

    def bump(self, k=1):
        self.n += k * self.step
        return self.n


class Em:
    ENGINES = ("sync", "tensor", "vector", "scalar", "gpsimd")

    def __init__(self):
        self.q = {e: [] for e in self.ENGINES}

    def emit(self, eng, fn, waits=(), inc=None, inck=1):
        wl = tuple(waits)
        tgt = None
        if inc is not None:
            tgt = inc.bump(inck)
        def run(e, fn=fn, wl=wl, inc=inc, inck=inck):
            for s, v in wl:
                e.wait_ge(s.h, v)
            inst = fn(e)
            if inc is not None:
                inst.then_inc(inc.h, inck * inc.step)
        self.q[eng].append(run)
        return tgt

    def wait(self, eng, sem, val):
        self.q[eng].append(lambda e, s=sem, v=val: e.wait_ge(s.h, v))

    def install(self, block):
        for eng in self.ENGINES:
            fns = self.q[eng]
            if fns:
                getattr(block, eng)(lambda e, fns=fns: [f(e) for f in fns])


def _c(a):
    return np.ascontiguousarray(a)


def pack_weights(i):
    d = {}
    W_q = np.asarray(i["L0_attn_W_q"]); W_k = np.asarray(i["L0_attn_W_k"])
    W_v = np.asarray(i["L0_attn_W_v"]); W_o = np.asarray(i["L0_attn_W_o"])
    d["wq_t"] = _c(W_q.reshape(16, 128, 16, 128).transpose(0, 3, 2, 1))
    d["wk_t"] = _c(W_k.reshape(8, 128, 16, 128).transpose(0, 3, 2, 1))
    d["wv_t"] = _c(W_v.reshape(8, 128, 16, 128).transpose(0, 3, 2, 1))
    d["wo_t"] = _c(W_o.reshape(16, 128, 16, 128).transpose(0, 3, 2, 1))
    for nm, key in (("aq", "A_q"), ("ak", "A_k"), ("av", "A_v"), ("ao", "A_o")):
        A = np.asarray(i[f"L0_attn_{key}"])
        d[nm + "_t"] = _c(A.reshape(R, 16, 128).transpose(2, 1, 0))
    for nm, key, noc in (("bq", "B_q", 16), ("bk", "B_k", 8), ("bv", "B_v", 8), ("bo", "B_o", 16)):
        Bm = np.asarray(i[f"L0_attn_{key}"])
        d[nm + "_t"] = _c(Bm.reshape(noc, 128, R).transpose(2, 0, 1))
    d["svec"] = _c(np.stack([np.asarray(i["L0_attn_s_q"]).reshape(()),
                             np.asarray(i["L0_attn_s_k"]).reshape(()),
                             np.asarray(i["L0_attn_s_v"]).reshape(()),
                             np.asarray(i["L0_attn_s_o"]).reshape(())])[None, :])
    d["qgain"] = _c(np.asarray(i["L0_attn_q_gain"]).reshape(1, H))
    gw = np.asarray(i["L0_ffn_gate_w"])
    d["gate_t"] = _c(gw.reshape(NE, 16, 128).transpose(2, 1, 0))
    w1 = np.asarray(i["L0_ffn_w1"]); w2 = np.asarray(i["L0_ffn_w2"]); w3 = np.asarray(i["L0_ffn_w3"])
    d["w1_t"] = _c(w1.reshape(NE, MCH, 128, 16, 128).transpose(0, 1, 4, 3, 2))
    d["w3_t"] = _c(w3.reshape(NE, MCH, 128, 16, 128).transpose(0, 1, 4, 3, 2))
    d["w2_t"] = _c(w2.reshape(NE, 16, 128, MCH, 128).transpose(0, 1, 4, 3, 2))
    sw1 = np.asarray(i["L0_ffn_sw1"]); sw2 = np.asarray(i["L0_ffn_sw2"]); sw3 = np.asarray(i["L0_ffn_sw3"])
    d["sw1_t"] = _c(sw1.reshape(SMCH, 128, 16, 128).transpose(0, 3, 2, 1))
    d["sw3_t"] = _c(sw3.reshape(SMCH, 128, 16, 128).transpose(0, 3, 2, 1))
    d["sw2_t"] = _c(sw2.reshape(16, 128, 2, 11, 128).transpose(0, 2, 4, 3, 1))
    return d


def host_constants():
    inv = 1.0 / (10000.0 ** (np.arange(0, 64, 2, dtype=np.float64) / 64.0))
    freqs = np.outer(np.arange(T, dtype=np.float64), inv)
    cos = np.cos(freqs).astype(np.float32).T
    sin = np.sin(freqs).astype(np.float32).T
    cs = np.concatenate([cos, cos], axis=0)
    ss = np.concatenate([-sin, sin], axis=0)
    cm = np.where(np.arange(128)[:, None] >= np.arange(128)[None, :],
                  0.0, NEG_BIG).astype(np.float32)
    ident = np.eye(128, dtype=ml_dtypes.bfloat16)
    ident_f32 = np.eye(128, dtype=np.float32)
    ones_col_bf = np.ones((128, 1), dtype=ml_dtypes.bfloat16)
    ones_row_f32 = np.ones((1, 128), dtype=np.float32)
    onehot = np.zeros((16, NE, 128), dtype=np.float32)
    for e in range(NE):
        onehot[e, e, :] = 1.0
    onehot = _c(onehot.transpose(1, 0, 2))
    return dict(ropecs=_c(cs), ropess=_c(ss), cmask=cm, ident=ident,
                ones_col_bf=ones_col_bf, ones_row_f32=ones_row_f32,
                onehot=onehot, ident_f32=ident_f32)


def pack_x(x):
    out = []
    for b in range(B):
        xt = x[b].T
        a = xt.reshape(16, 128, NTT, TT).transpose(2, 1, 0, 3)
        out.append(_c(a.astype(np.float32, copy=False)))
    return out


def unpack_out(o):
    return _c(o.transpose(2, 1, 0, 3).reshape(D, T).T)


def declare_ios(nc):
    t = {}
    def din(name, shape, dt=F32):
        t[name] = nc.dram_tensor(name, list(shape), dt, kind="ExternalInput").ap()
    def dout(name, shape, dt=F32):
        t[name] = nc.dram_tensor(name, list(shape), dt, kind="ExternalOutput").ap()
    def dint(name, shape, dt):
        t[name] = nc.dram_tensor(name, list(shape), dt, kind="Internal").ap()

    din("x_t", (NTT, 128, 16, TT))
    din("wq_t", (16, 128, 16, 128)); din("wk_t", (8, 128, 16, 128))
    din("wv_t", (8, 128, 16, 128)); din("wo_t", (16, 128, 16, 128))
    for nm in ("aq", "ak", "av", "ao"):
        din(nm + "_t", (128, 16, R))
    din("bq_t", (R, 16, 128)); din("bk_t", (R, 8, 128))
    din("bv_t", (R, 8, 128)); din("bo_t", (R, 16, 128))
    din("svec", (1, 4)); din("qgain", (1, H))
    din("gate_t", (128, 16, NE))
    din("w1_t", (NE, MCH, 128, 16, 128)); din("w3_t", (NE, MCH, 128, 16, 128))
    din("w2_t", (NE, 16, 128, MCH, 128))
    din("sw1_t", (SMCH, 128, 16, 128)); din("sw3_t", (SMCH, 128, 16, 128))
    din("sw2_t", (16, 2, 128, 11, 128))
    din("ropecs", (64, T)); din("ropess", (64, T))
    din("cmask", (128, 128)); din("ident", (128, 128), BF16)
    din("ones_col_bf", (128, 1), BF16); din("ones_row_f32", (1, 128))
    din("onehot", (16, NE, 128)); din("ident_f32", (128, 128))

    dout("out_t", (NTT, 128, 16, TT))

    dint("q_d", (H, 128, T), BF16)
    dint("o_d", (H, 128, T), BF16)
    dint("x2_d", (NTT, 128, 16, TT), F32)
    dint("ln2_d", (NTT, 128, 16, TT), BF16)
    dint("glog_d", (NTT, 16, TT), F32)
    dint("w1b_d", (NE, MCH, 128, 16, 128), BF16)
    dint("w3b_d", (NE, MCH, 128, 16, 128), BF16)
    dint("w2b_d", (NE, 16, 128, MCH, 128), BF16)
    dint("sw1b_d", (SMCH, 128, 16, 128), BF16)
    dint("sw3b_d", (SMCH, 128, 16, 128), BF16)
    dint("sw2b_d", (16, 128, SMCH, 128), BF16)

    if DEBUG:
        dout("ln1_dbg", (128, 16, T), BF16)
        dout("q_dbg", (H, 128, T), BF16)
        dout("k_dbg", (KVH, 128, T), BF16)
        dout("v_dbg", (128, 16, KVH * 128), BF16)
        dout("o_dbg", (H, 128, T), BF16)
        dout("x2_dbg", (NTT, 128, 16, TT), F32)
        dout("ln2_dbg", (NTT, 128, 16, TT), BF16)
        dout("gatew_dbg", (128, NTT * 4, NE), F32)
        dout("pr_dbg", (128, TT), BF16)
        dout("sm_dbg", (128, 8), F32)
    return t


def alloc_persistent(nc, es):
    s = {}
    def sb(name, shape, dt):
        s[name] = es.enter_context(nc.sbuf_tensor(name, list(shape), dt))
    sb("gate_sb", (128, 16, NE), F32)
    sb("svec_sb", (1, 4), F32)
    sb("qgain_sb", (1, H), F32)
    sb("scol", (128, 4), F32)
    sb("cm_sb", (128, 128), F32)
    sb("id_sb", (128, 128), BF16)
    sb("idf_sb", (128, 128), F32)
    sb("ones_col", (128, 1), BF16)
    sb("ones_row", (1, 128), F32)
    sb("cpool", (1, 4), F32)
    return s


def ph0_setup(nc, t, s, em, sems, es):
    ld = sems["ld0"]
    lds = sems["ld0s"]
    def dma(eng, out, in_):
        return em.emit(eng, lambda e, o=out, i=in_: e.dma_start(out=o, in_=i),
                       inc=(ld if eng == "gpsimd" else lds))
    for j, nm in enumerate(("aq", "ak", "av", "ao")):
        dma("gpsimd", s["lora_a"][:, j, :, :], t[nm + "_t"][:, :, :])
    dma("gpsimd", s["bq_sb"][:, :, :], t["bq_t"][:, :, :])
    dma("gpsimd", s["bk_sb"][:, :, :], t["bk_t"][:, :, :])
    dma("gpsimd", s["bv_sb"][:, :, :], t["bv_t"][:, :, :])
    dma("gpsimd", s["bo_sb"][:, :, :], t["bo_t"][:, :, :])
    dma("sync", s["gate_sb"][:, :, :], t["gate_t"][:, :, :])
    dma("sync", s["svec_sb"][:, :], t["svec"][:, :])
    dma("sync", s["qgain_sb"][:, :], t["qgain"][:, :])
    dma("sync", s["cm_sb"][:, :], t["cmask"][:, :])
    dma("sync", s["id_sb"][:, :], t["ident"][:, :])
    dma("sync", s["idf_sb"][:, :], t["ident_f32"][:, :])
    dma("sync", s["ones_col"][:, :], t["ones_col_bf"][:, :])
    dma("sync", s["ones_row"][:, :], t["ones_row_f32"][:, :])
    n_all = ld.n
    em.emit("vector", lambda e: e.memset(s["cpool"][0:1, 0:1], LN_EPS))
    em.emit("vector", lambda e: e.memset(s["cpool"][0:1, 1:2], RMS_EPS))
    em.emit("vector", lambda e: e.memset(s["cpool"][0:1, 2:3], -0.5))
    ps = es.enter_context(nc.psum_tensor("ph0_ps", [128, 4], F32))
    mmdone = sems["ev0"]
    em.emit("tensor",
            lambda e: e.matmul(ps[:, :], s["ones_row"][:, :], s["svec_sb"][:, :],
                               start=True, stop=True),
            waits=[(ld, n_all), (lds, lds.n)], inc=mmdone)
    em.emit("vector", lambda e: e.tensor_copy(s["scol"][:, :], ps[:, :]),
            waits=[(mmdone, mmdone.n)], inc=mmdone)


def ph1_ln1(nc, t, s, em, sems, es, es_local):
    ln1 = es.enter_context(nc.sbuf_tensor("ln1", [128, 16, T], BF16))
    sb = lambda n, shp, dt: es_local.enter_context(nc.sbuf_tensor(n, list(shp), dt))
    ps = lambda n, shp: es_local.enter_context(nc.psum_tensor(n, list(shp), F32))

    xbf = sb("ph1_xbf", (128, 2, 16, TT), BF16)
    sqb = sb("ph1_sqb", (128, 16, TT), BF16)
    rows = sb("ph1_rows", (1, 5, TT), F32)
    tmp = sb("ph1_tmp", (128, TT), F32)

    s1p = ps("ph1_s1", (1, TT)); s2p = ps("ph1_s2", (1, TT))
    ap_ = ps("ph1_a", (128, TT)); bp_ = ps("ph1_b", (128, TT))

    xld = sems["xld"]
    S = dict(sems["LS"])
    S["aff"] = sems["mk"]("p1aff")

    MEAN, M2, VAR, NMR, RSTD = 0, 1, 2, 3, 4

    for tt in range(NTT):
        buf = xbf[:, tt % 2]
        em.emit("gpsimd", lambda e, o=buf, i=t["x_t"][tt]: e.dma_start(out=o, in_=i),
                waits=([(S["aff"], tt - 1)] if tt >= 2 else []), inc=xld)
        n_x = xld.n
        em.emit("scalar", lambda e, i=buf: e.activation(sqb[:, :, :], i[:, :, :], AF.Square),
                waits=[(xld, n_x)] + ([(S["stat"], tt)] if tt >= 1 else []),
                inc=S["sq"])
        for c in range(16):
            em.emit("tensor", lambda e, c=c, b_=buf: e.matmul(
                s1p[:, :], s["ones_col"][:, :], b_[:, c, :],
                start=(c == 0), stop=(c == 15)),
                waits=([(S["sq"], tt + 1)] + ([(S["var"], tt)] if tt >= 1 else []))
                if c == 0 else [])
        for c in range(16):
            em.emit("tensor", lambda e, c=c: e.matmul(
                s2p[:, :], s["ones_col"][:, :], sqb[:, c, :],
                start=(c == 0), stop=(c == 15)),
                inc=(S["stat"] if c == 15 else None))
        em.emit("vector", lambda e: e.tensor_scalar(
            rows[:, MEAN, :], s1p[:, :], 1.0 / D, None, ALU.mult),
            waits=[(S["stat"], tt + 1)])
        em.emit("vector", lambda e: e.tensor_mul(rows[:, M2, :], rows[:, MEAN, :], rows[:, MEAN, :]))
        em.emit("vector", lambda e: e.scalar_tensor_tensor(
            rows[:, VAR, :], s2p[:, :], 1.0 / D, rows[:, M2, :],
            ALU.mult, ALU.subtract), inc=S["var"])
        em.emit("scalar", lambda e: e.activation(
            rows[:, RSTD, :], rows[:, VAR, :], AF.Ln, bias=s["cpool"][0:1, 0:1]),
            waits=[(S["var"], tt + 1)])
        em.emit("scalar", lambda e: e.activation(
            rows[:, RSTD, :], rows[:, RSTD, :], AF.Exp, scale=s["cpool"][0:1, 2:3]),
            inc=S["rstd"])
        em.emit("vector", lambda e: e.tensor_mul(
            rows[:, NMR, :], rows[:, MEAN, :], rows[:, RSTD, :]),
            waits=[(S["rstd"], tt + 1)], inc=S["nmr"])
        em.emit("tensor", lambda e: e.matmul(
            ap_[:, :], s["ones_row"][:, :], rows[:, RSTD, :], start=True, stop=True),
            waits=[(S["nmr"], tt + 1)])
        em.emit("tensor", lambda e: e.matmul(
            bp_[:, :], s["ones_row"][:, :], rows[:, NMR, :], start=True, stop=True),
            inc=S["bc"])
        for c in range(16):
            em.emit("vector", lambda e, c=c, b_=buf: e.tensor_mul(
                tmp[:, :], b_[:, c, :], ap_[:, :]),
                waits=[(S["bc"], tt + 1)] if c == 0 else [])
            em.emit("vector", lambda e, c=c, tt=tt: e.tensor_tensor(
                ln1[:, c, tt * TT:(tt + 1) * TT], tmp[:, :], bp_[:, :], ALU.subtract),
                inc=(S["aff"] if c == 15 else None))
    if DEBUG:
        st = sems["dbg"]
        em.emit("sync", lambda e: e.dma_start(out=t["ln1_dbg"][:, :, :], in_=ln1[:, :, :]),
                waits=[(S["aff"], NTT)], inc=st)
        em.wait("sync", st, st.n)
    es._ln1 = ln1
    return ln1


def ph2_qkv(nc, t, s, em, sems, es, ln1, kv):
    sb = lambda n, shp, dt, **kw: es.enter_context(nc.sbuf_tensor(n, list(shp), dt, **kw))
    psf = lambda n, shp: es.enter_context(nc.psum_tensor(n, list(shp), F32))
    psb = lambda n, shp: es.enter_context(nc.psum_tensor(n, list(shp), BF16))

    k_sb, v_sb = kv["k_sb"], kv["v_sb"]
    cs_sb = sb("cs_sb", (64, T), F32)
    ss_sb = sb("ss_sb", (64, T), F32)
    wring = sb("wring", (128, 2, 16, 128), BF16)
    u_sb = sb("u_sb", (R, 3, T), BF16)
    ev_f32 = sb("ev_f32", (128, TT), F32)
    qraw = sb("qraw", (128, TT), F32)
    qn_sb = sb("qn_sb", (128, TT), F32)
    qf_hb = sb("qf_hb", (128, 2, T), BF16)
    vbf = sb("vbf", (128, TT), BF16)
    sq_sc = sb("sq_sc", (128, TT), BF16)
    rt1 = sb("rt1", (64, TT), F32)
    rt2 = sb("rt2", (64, TT), F32)
    rrow = sb("rrow", (1, 2, TT), F32)

    pW = [psf("p2_w0", (128, TT)), psf("p2_w1", (128, TT))]
    pL = [psf("p2_l0", (128, TT)), psf("p2_l1", (128, TT))]
    pU = [psf("p2_u0", (128, TT)), psf("p2_u1", (128, TT))]
    pS = psf("p2_ssq", (1, TT))
    pT = psb("p2_vt", (128, TT))

    mk = sems["mk"]
    wld = mk("p2wld", 16); tld = mk("p2tld", 16)
    umm = mk("p2umm"); uev = mk("p2uev")
    gmm = mk("p2gmm"); gsc = mk("p2gsc"); gad = mk("p2gad")
    ssqq = mk("p2sqq"); sstat = mk("p2stat"); sms = mk("p2ms")
    srst = mk("p2rst"); ssrow = mk("p2srow"); sscp = mk("p2scp")
    srope = mk("p2rope"); qst = mk("p2qst", 16)
    svtr = mk("p2vtr"); svcp = mk("p2vcp")

    em.emit("sync", lambda e: e.dma_start(out=cs_sb[:, :], in_=t["ropecs"][:, :]), inc=tld)
    em.emit("sync", lambda e: e.dma_start(out=ss_sb[:, :], in_=t["ropess"][:, :]), inc=tld)
    for g, (wi, tt) in enumerate([(wi, tt) for tt in range(NTT) for wi in range(3)]):
        pu = pU[g % 2]
        for c in range(16):
            em.emit("tensor", lambda e, c=c, wi=wi, tt=tt, pu=pu: e.matmul(
                pu[0:R, :], s["lora_a"][:, wi, c, :], ln1[:, c, tt * TT:(tt + 1) * TT],
                start=(c == 0), stop=(c == 15)),
                waits=([(uev, g - 1)] if g >= 2 and c == 0 else []),
                inc=(umm if c == 15 else None))
        em.emit("scalar", lambda e, wi=wi, tt=tt, pu=pu: e.copy(
            u_sb[:, wi, tt * TT:(tt + 1) * TT], pu[0:R, :]),
            waits=[(umm, g + 1)], inc=uev)

    heads = ([(0, h) for h in range(H)] + [(1, a) for a in range(KVH)]
             + [(2, a) for a in range(KVH)])
    wt_of = {0: t["wq_t"], 1: t["wk_t"], 2: t["wv_t"]}
    b_of = {0: s["bq_sb"], 1: s["bk_sb"], 2: s["bv_sb"]}
    g_glob = 0
    scp_g = 0
    vtr_g = 0
    qk_g = 0
    last_mm_for_wbuf = {}

    for hi, (role, h) in enumerate(heads):
        wbuf = wring[:, hi % 2]
        em.emit("gpsimd", lambda e, o=wbuf, i=wt_of[role][h]: e.dma_start(out=o, in_=i),
                waits=([(gmm, last_mm_for_wbuf[hi - 2])] if hi >= 2 else []), inc=wld)
        n_w = wld.n
        for j in range(NTT):
            g = g_glob; g_glob += 1
            pw, pl = pW[g % 2], pL[g % 2]
            jsl = slice(j * TT, (j + 1) * TT)
            waits0 = [(wld, n_w)]
            if g >= 2:
                waits0 += [(gsc, g - 1), (gad, g - 1)]
            if hi == 0 and j == 0:
                waits0 += [(uev, 12)]
            for c in range(16):
                em.emit("tensor", lambda e, c=c, pw=pw, w_=wbuf, jsl=jsl: e.matmul(
                    pw[:, :], w_[:, c, :], ln1[:, c, jsl],
                    start=(c == 0), stop=(c == 15)),
                    waits=(waits0 if c == 0 else []))
            em.emit("tensor", lambda e, pl=pl, role=role, h=h, jsl=jsl: e.matmul(
                pl[:, :], b_of[role][:, h, :], u_sb[:, role, jsl],
                start=True, stop=True), inc=gmm)
            em.emit("scalar", lambda e, pw=pw, role=role: e.activation(
                ev_f32[:, :], pw[:, :], AF.Copy, scale=s["scol"][:, role:role + 1]),
                waits=[(gmm, g + 1)] + ([(gad, g - 1)] if g >= 1 else []),
                inc=gsc)
            if role == 2:
                em.emit("vector", lambda e, pl=pl: e.tensor_tensor(
                    vbf[:, :], ev_f32[:, :], pl[:, :], ALU.add),
                    waits=[(gsc, g + 1)], inc=gad)
                vg = vtr_g; vtr_g += 1
                for ti in range(4):
                    em.emit("tensor", lambda e, ti=ti: e.transpose(
                        pT[:, ti * 128:(ti + 1) * 128], vbf[:, ti * 128:(ti + 1) * 128],
                        s["id_sb"][:, :]),
                        waits=(([(gad, g + 1)] + ([(svcp, vg)] if vg >= 1 else []))
                               if ti == 0 else []),
                        inc=(svtr if ti == 3 else None))
                em.emit("vector", lambda e, h=h, j=j: e.tensor_copy(
                    v_sb[:, 4 * j:4 * j + 4, h * 128:(h + 1) * 128], pT[:, :]),
                    waits=[(svtr, svtr.n)], inc=svcp)
            else:
                qg = qk_g; qk_g += 1
                em.emit("vector", lambda e, pl=pl: e.tensor_tensor(
                    qraw[:, :], ev_f32[:, :], pl[:, :], ALU.add),
                    waits=[(gsc, g + 1)], inc=gad)
                em.emit("scalar", lambda e: e.activation(
                    sq_sc[:, :], qraw[:, :], AF.Square),
                    waits=[(gad, g + 1)], inc=ssqq)
                em.emit("tensor", lambda e: e.matmul(
                    pS[0:1, :], s["ones_col"][:, :], sq_sc[:, :], start=True, stop=True),
                    waits=[(ssqq, ssqq.n), (sms, sms.n)], inc=sstat)
                em.emit("vector", lambda e: e.tensor_scalar(
                    rrow[:, 0, :], pS[0:1, :], 1.0 / 128.0, None, ALU.mult),
                    waits=[(sstat, sstat.n), (srst, srst.n)], inc=sms)
                em.emit("scalar", lambda e: e.activation(
                    rrow[:, 1, :], rrow[:, 0, :], AF.Ln, bias=s["cpool"][0:1, 1:2]),
                    waits=[(sms, sms.n)])
                em.emit("scalar", lambda e: e.activation(
                    rrow[:, 1, :], rrow[:, 1, :], AF.Exp, scale=s["cpool"][0:1, 2:3]),
                    inc=srst)
                if role == 0:
                    em.emit("vector", lambda e, h=h: e.tensor_scalar(
                        rrow[:, 0, :], rrow[:, 1, :], s["qgain_sb"][0:1, h:h + 1],
                        ISQ_HD, ALU.mult, ALU.mult),
                        waits=[(srst, srst.n), (sscp, sscp.n)], inc=ssrow)
                else:
                    em.emit("vector", lambda e: e.tensor_copy(rrow[:, 0, :], rrow[:, 1, :]),
                            waits=[(srst, srst.n), (sscp, sscp.n)], inc=ssrow)
                sg = scp_g; scp_g += 1
                pscp = pU[sg % 2]
                em.emit("tensor", lambda e, pscp=pscp: e.matmul(
                    pscp[:, :], s["ones_row"][:, :], rrow[:, 0, :], start=True, stop=True),
                    waits=[(ssrow, ssrow.n)] + ([(srope, sg - 1)] if sg >= 2 else []),
                    inc=sscp)
                em.emit("vector", lambda e, pscp=pscp: e.tensor_tensor(
                    qn_sb[:, :], qraw[:, :], pscp[:, :], ALU.mult),
                    waits=[(sscp, sscp.n)])
                if role == 0:
                    w_rope = [(qst, 16 * (h - 1))] if (h >= 2 and j == 0) else []
                    out64 = qf_hb[:, hi % 2][0:64, jsl]
                    outp_d, outp_s = qf_hb[:, hi % 2][64:128, jsl], None
                else:
                    w_rope = []
                    out64 = k_sb[0:64, h, jsl]
                    outp_d = k_sb[64:128, h, jsl]
                if qg == 0:
                    w_rope = w_rope + [(tld, 32)]
                em.emit("vector", lambda e, jsl=jsl: e.tensor_mul(
                    rt1[:, :], qn_sb[0:64, :], cs_sb[:, jsl]), waits=w_rope)
                em.emit("vector", lambda e, jsl=jsl: e.tensor_mul(
                    rt2[0:32, :], qn_sb[32:64, :], ss_sb[32:64, jsl]))
                em.emit("vector", lambda e, jsl=jsl: e.tensor_mul(
                    rt2[32:64, :], qn_sb[0:32, :], ss_sb[0:32, jsl]))
                em.emit("vector", lambda e, o=out64: e.tensor_tensor(
                    o, rt1[:, :], rt2[:, :], ALU.add))
                em.emit("vector", lambda e, o=outp_d: e.tensor_copy(
                    o, qn_sb[64:128, :]), inc=srope)
        last_mm_for_wbuf[hi] = gmm.n
        if role == 0:
            em.emit("sync", lambda e, h=h, hi=hi: e.dma_start(
                out=t["q_d"][h], in_=qf_hb[:, hi % 2]),
                waits=[(srope, srope.n)], inc=qst)

    if DEBUG:
        st = sems["dbg"]
        em.emit("sync", lambda e: e.dma_start(out=t["q_dbg"][:, :, :], in_=t["q_d"][:, :, :]),
                waits=[(qst, qst.n)], inc=st)
        for a in range(KVH):
            em.emit("sync", lambda e, a=a: e.dma_start(out=t["k_dbg"][a], in_=k_sb[:, a, :]),
                    waits=[(srope, srope.n)], inc=st)
        em.emit("sync", lambda e: e.dma_start(out=t["v_dbg"][:, :, :], in_=v_sb[:, :, :]),
                waits=[(svcp, svcp.n)], inc=st)
        em.wait("sync", st, st.n)


def ph3_attn(nc, t, s, em, sems, es, kv):
    sb = lambda n, shp, dt: es.enter_context(nc.sbuf_tensor(n, list(shp), dt))
    psf = lambda n, shp: es.enter_context(nc.psum_tensor(n, list(shp), F32))
    psb = lambda n, shp: es.enter_context(nc.psum_tensor(n, list(shp), BF16))

    k_sb, v_sb = kv["k_sb"], kv["v_sb"]
    qh = sb("p3_qh", (128, 4, T), BF16)
    oh = sb("p3_oh", (128, 4, T), BF16)
    probs = sb("p3_probs", (128, 4, T), BF16)
    pt_sb = sb("p3_pt", (128, 4, T), BF16)
    nmx = sb("p3_nmx", (128, 4), F32)
    ssum = sb("p3_ssum", (128, 4), F32)
    rc = sb("p3_rc", (128, 4), F32)

    NSC = 5
    scb = psf("p3_scb", (128, NSC, TT))
    scf = scb[:, :, :].rearrange("p a b -> p (a b)")
    tp = [psb(f"p3_tp{i}", (128, 1024)) for i in range(2)]
    op = psf("p3_op0", (128, 512))

    mk = sems["mk"]
    qld = [sems["mk"]("p3qld0", 16), sems["mk"]("p3qld1", 16)]
    smm = mk("p3mm"); smsk = mk("p3msk")
    smax = mk("p3max"); sexp = mk("p3exp")
    srcp = mk("p3rcp"); spn = mk("p3pn"); str_ = mk("p3tr")
    sptc = mk("p3ptc"); sav = mk("p3av"); soev = mk("p3oev")
    sost = [mk("p3ost0", 16), mk("p3ost1", 16)]
    p2qst = sems["p2qst"]

    slot_free = [0] * NSC
    last_tr = [0, 0, 0, 0]
    last_av = [0, 0, 0, 0]
    last_exp_w = [0, 0, 0, 0]
    tp_last = [0, 0]
    gtb = 0
    cur = 0

    for g in range(H // 2):
        kvh = g
        pp = (g % 2) * 2
        nq = [0, 0]
        for w in range(2):
            em.emit("sync", lambda e, w=w, g=g, pp=pp: e.dma_start(
                out=qh[:, pp + w], in_=t["q_d"][2 * g + w]),
                waits=[(p2qst, 256)] + ([(smm, smm.n)] if g >= 2 and w == 0 else []),
                inc=qld[g % 2])
        n_q = qld[g % 2].n
        for qt in range(16):
            cols = (qt + 1) * 128
            ngr = (cols + TT - 1) // TT
            s0s = [0, 0]
            qp = (qt % 2) * 2

            def st_scores(w, qt=qt, cols=cols, ngr=ngr, kvh=kvh, pp=pp, n_q=n_q, gpar=g % 2, qp=qp):
                nonlocal cur
                if cur + ngr > NSC:
                    cur = 0
                s0 = cur; cur += ngr
                s0s[w] = s0
                for j in range(ngr):
                    ncol = min(TT, cols - j * TT)
                    em.emit("tensor", lambda e, j=j, sl=s0 + j, ncol=ncol, w=w,
                            qt=qt, kvh=kvh, pp=pp: e.matmul(
                        scb[:, sl, 0:ncol], qh[:, pp + w, qt * 128:(qt + 1) * 128],
                        k_sb[:, kvh, j * TT:j * TT + ncol], start=True, stop=True),
                        waits=[(qld[gpar], n_q), (sexp, slot_free[s0 + j])],
                        inc=(smm if j == ngr - 1 else None))

            def st_mask(w, qt=qt, cols=cols, qp=qp):
                s0 = s0s[w]
                dgr = (cols - 128) // TT
                dcol = (cols - 128) - dgr * TT
                em.emit("vector", lambda e, s0=s0, dgr=dgr, dcol=dcol: e.tensor_tensor(
                    scb[:, s0 + dgr, dcol:dcol + 128], scb[:, s0 + dgr, dcol:dcol + 128],
                    s["cm_sb"][:, :], ALU.add),
                    waits=[(smm, smm.n)], inc=smsk)

            def st_max(w, cols=cols, qp=qp):
                s0 = s0s[w]
                em.emit("vector", lambda e, s0=s0, cols=cols, w=w: e.reduce_max(
                    nmx[:, qp + w:qp + w + 1], scf[:, s0 * TT:s0 * TT + cols], axis=AX.X,
                    negate=True),
                    waits=[(smsk, smsk.n), (sexp, last_exp_w[qp + w])], inc=smax)

            def st_exp(w, cols=cols, ngr=ngr, qp=qp):
                s0 = s0s[w]
                em.emit("scalar", lambda e, s0=s0, cols=cols, w=w: e.activation(
                    probs[:, qp + w, 0:cols], scf[:, s0 * TT:s0 * TT + cols], AF.Exp,
                    bias=nmx[:, qp + w:qp + w + 1], accum_out=ssum[:, qp + w:qp + w + 1]),
                    waits=[(smax, smax.n), (str_, last_tr[qp + w])], inc=sexp)
                for j in range(ngr):
                    slot_free[s0 + j] = sexp.n
                last_exp_w[qp + w] = sexp.n

            def st_norm(w, cols=cols, qp=qp):
                em.emit("vector", lambda e, w=w: e.reciprocal(
                    rc[:, qp + w:qp + w + 1], ssum[:, qp + w:qp + w + 1]),
                    waits=[(sexp, sexp.n)], inc=srcp)
                em.emit("vector", lambda e, cols=cols, w=w: e.tensor_scalar(
                    probs[:, qp + w, 0:cols], probs[:, qp + w, 0:cols], rc[:, qp + w:qp + w + 1],
                    None, ALU.mult),
                    waits=[(srcp, srcp.n)], inc=spn)

            def st_tr(w, qt=qt, qp=qp):
                nonlocal gtb
                ntb = (qt + 4) // 4
                for tb in range(ntb):
                    kts = range(tb * 4, min(tb * 4 + 4, qt + 1))
                    bi = gtb % 2; gtb += 1
                    tpb = tp[bi]
                    for i, kt in enumerate(kts):
                        wtr = []
                        if i == 0:
                            wtr = [(spn, spn.n), (sptc, tp_last[bi])]
                            if tb == 0:
                                wtr.append((sav, last_av[qp + w]))
                        em.emit("tensor", lambda e, i=i, kt=kt, tpb=tpb, w=w: e.transpose(
                            tpb[:, i * 128:(i + 1) * 128],
                            probs[:, qp + w, kt * 128:(kt + 1) * 128], s["id_sb"][:, :]),
                            waits=wtr, inc=(str_ if i == len(kts) - 1 else None))
                    em.emit("vector", lambda e, tpb=tpb, nk=len(kts), tb=tb, w=w: e.tensor_copy(
                        pt_sb[:, qp + w, tb * TT:tb * TT + nk * 128], tpb[:, 0:nk * 128]),
                        waits=[(str_, str_.n)], inc=sptc)
                    tp_last[bi] = sptc.n
                last_tr[qp + w] = str_.n

            def st_av(w, qt=qt, kvh=kvh, qp=qp):
                for kt in range(qt + 1):
                    w_av = []
                    if kt == 0:
                        w_av = [(sptc, sptc.n), (soev, soev.n)]
                    em.emit("tensor", lambda e, kt=kt, kvh=kvh, qt=qt, w=w: e.matmul(
                        op[:, w * 128:(w + 1) * 128],
                        v_sb[:, kt, kvh * 128:(kvh + 1) * 128],
                        pt_sb[:, qp + w, kt * 128:(kt + 1) * 128],
                        start=(kt == 0), stop=(kt == qt)),
                        waits=w_av, inc=(sav if kt == qt else None))
                last_av[qp + w] = sav.n

            def st_evac(w, qt=qt, pp=pp, gpar=g % 2, g_=g):
                em.emit("scalar", lambda e, qt=qt, w=w, pp=pp: e.activation(
                    oh[:, pp + w][:, qt * 128:(qt + 1) * 128],
                    op[:, w * 128:(w + 1) * 128], AF.Copy),
                    waits=[(sav, sav.n)] +
                          ([(sost[gpar], sost[gpar].n)] if (g_ >= 2 and qt == 0 and w == 0)
                           else []),
                    inc=soev)

            if PH3_INTER and 2 * ngr <= NSC:
                for st in (st_scores, st_mask, st_max, st_exp, st_norm, st_tr):
                    st(0); st(1)
                st_av(0); st_evac(0); st_av(1); st_evac(1)
            else:
                for w in range(2):
                    for st in (st_scores, st_mask, st_max, st_exp, st_norm, st_tr):
                        st(w)
                st_av(0); st_evac(0); st_av(1); st_evac(1)
        for w in range(2):
            em.emit("sync", lambda e, g=g, w=w, pp=pp: e.dma_start(
                out=t["o_d"][2 * g + w], in_=oh[:, pp + w]),
                waits=[(soev, soev.n - (1 - w))], inc=sost[g % 2])

    if DEBUG:
        st = sems["dbg"]
        em.emit("sync", lambda e: e.dma_start(out=t["o_dbg"][:, :, :], in_=t["o_d"][:, :, :]),
                waits=[(sost[0], sost[0].n), (sost[1], sost[1].n)], inc=st)
        em.wait("sync", st, st.n)


def emit_ln_stats(em, s, src_bf, sqb, s1p, s2p, ap_, bp_, rows, S,
                  w_src, w_psum_war, w_bc_war):
    em.emit("scalar", lambda e: e.activation(sqb[:, :, :], src_bf[:, :, :], AF.Square),
            waits=list(w_src), inc=S["sq"])
    for c in range(16):
        em.emit("tensor", lambda e, c=c: e.matmul(
            s1p[0:1, :], s["ones_col"][:, :], src_bf[:, c, :],
            start=(c == 0), stop=(c == 15)),
            waits=([(S["sq"], S["sq"].n)] + list(w_psum_war)) if c == 0 else [])
    for c in range(16):
        em.emit("tensor", lambda e, c=c: e.matmul(
            s2p[0:1, :], s["ones_col"][:, :], sqb[:, c, :],
            start=(c == 0), stop=(c == 15)),
            inc=(S["stat"] if c == 15 else None))
    em.emit("vector", lambda e: e.tensor_scalar(
        rows[:, 0, :], s1p[0:1, :], 1.0 / D, None, ALU.mult),
        waits=[(S["stat"], S["stat"].n), (S["rstd"], S["rstd"].n)])
    em.emit("vector", lambda e: e.tensor_mul(rows[:, 1, :], rows[:, 0, :], rows[:, 0, :]))
    em.emit("vector", lambda e: e.scalar_tensor_tensor(
        rows[:, 2, :], s2p[0:1, :], 1.0 / D, rows[:, 1, :],
        ALU.mult, ALU.subtract), inc=S["var"])
    em.emit("scalar", lambda e: e.activation(
        rows[:, 3, :], rows[:, 2, :], AF.Ln, bias=s["cpool"][0:1, 0:1]),
        waits=[(S["var"], S["var"].n)])
    em.emit("scalar", lambda e: e.activation(
        rows[:, 3, :], rows[:, 3, :], AF.Exp, scale=s["cpool"][0:1, 2:3]),
        inc=S["rstd"])
    em.emit("vector", lambda e: e.tensor_mul(
        rows[:, 4, :], rows[:, 0, :], rows[:, 3, :]),
        waits=[(S["rstd"], S["rstd"].n)], inc=S["nmr"])
    em.emit("tensor", lambda e: e.matmul(
        ap_[:, :], s["ones_row"][:, :], rows[:, 3, :], start=True, stop=True),
        waits=[(S["nmr"], S["nmr"].n)] + list(w_bc_war))
    em.emit("tensor", lambda e: e.matmul(
        bp_[:, :], s["ones_row"][:, :], rows[:, 4, :], start=True, stop=True),
        inc=S["bc"])
    return S["bc"].n


def ph4_woln(nc, t, s, em, sems, es):
    sb = lambda n, shp, dt: es.enter_context(nc.sbuf_tensor(n, list(shp), dt))
    psf = lambda n, shp: es.enter_context(nc.psum_tensor(n, list(shp), F32))

    o_sb = sb("p4_o", (128, 2, 16, TT), BF16)
    wob = sb("p4_wob", (128, 2, 16, 128), BF16)
    uo_sb = sb("p4_uo", (R, TT), BF16)
    ev4 = sb("p4_ev", (128, TT), F32)
    ao_sb = sb("p4_ao", (128, 16, TT), F32)
    abf = sb("p4_abf", (128, 16, TT), BF16)
    sq4 = sb("p4_sq", (128, 16, TT), BF16)
    xf = sb("p4_x", (128, 16, TT), F32)
    ln2t = sb("p4_ln2", (128, 16, TT), BF16)
    rows = sb("p4_rows", (1, 5, TT), F32)
    tmp = sb("p4_tmp", (128, TT), F32)
    l2f = sb("p4_l2f", (128, 2, TT), F32)
    glit = sb("p4_glit", (16, TT), F32)

    pwW = [psf("p4_w0", (128, TT)), psf("p4_w1", (128, TT))]
    pwL = psf("p4_l", (128, TT))
    pu = psf("p4_u", (R, TT))
    s1p = psf("p4_s1", (1, TT)); s2p = psf("p4_s2", (1, TT))
    ap_ = psf("p4_a", (128, TT)); bp_ = psf("p4_b", (128, TT))

    mk = sems["mk"]
    sold = mk("p4old", 16); swo = mk("p4wo", 16); sxl = mk("p4xl", 16)
    su4 = mk("p4umm"); sue4 = mk("p4uev")
    sm4 = mk("p4mm"); sc4 = mk("p4sc"); sd4 = mk("p4ad")
    SA = sems["LS"]
    SB = sems["LS"]
    sxa = mk("p4xa"); scast2 = mk("p4c2"); sla = mk("p4la")
    sx2st = mk("p4x2st", 16); sln2st = mk("p4l2st", 16)
    sgmm = mk("p4gmm"); sgc = mk("p4gc"); sgev = mk("p4gev"); sgst = mk("p4gst", 16)
    slf = mk("p4lf"); slc = mk("p4lc")
    p3ost0 = sems["p3ost0"]; p3ost1 = sems["p3ost1"]

    for tt in range(NTT):
        ob = o_sb[:, tt % 2]
        jsl = slice(tt * TT, (tt + 1) * TT)
        for hh in range(16):
            em.emit("sync", lambda e, hh=hh, tt=tt, ob=ob: e.dma_start(
                out=ob[:, hh, :], in_=t["o_d"][hh][:, tt * TT:(tt + 1) * TT]),
                waits=(([(p3ost0, 128), (p3ost1, 128)] if tt == 0 else []) +
                       ([(sm4, sm4.n)] if (tt >= 2 and hh == 0) else [])),
                inc=sold)
        n_o = sold.n
        em.emit("sync", lambda e, tt=tt: e.dma_start(out=xf[:, :, :], in_=t["x_t"][tt]),
                waits=[(sx2st, sx2st.n), (sla, sla.n)], inc=sxl)
        n_x = sxl.n
        for c in range(16):
            em.emit("tensor", lambda e, c=c, ob=ob: e.matmul(
                pu[0:R, :], s["lora_a"][:, 3, c, :], ob[:, c, :],
                start=(c == 0), stop=(c == 15)),
                waits=([(sold, n_o)] + ([(sue4, sue4.n), (sgev, sgev.n)] if tt >= 1 else []))
                if c == 0 else [],
                inc=(su4 if c == 15 else None))
        em.emit("scalar", lambda e: e.copy(uo_sb[:, :], pu[0:R, :]),
                waits=[(su4, su4.n)], inc=sue4)
        for oc in range(16):
            wb = wob[:, oc % 2]
            em.emit("gpsimd", lambda e, oc=oc, wb=wb: e.dma_start(
                out=wb, in_=t["wo_t"][oc]),
                waits=([(sm4, sm4.n - 1)] if (tt * 16 + oc) >= 2 else []),
                inc=swo)
            n_w = swo.n
            g = tt * 16 + oc
            pw = pwW[g % 2]
            for c in range(16):
                em.emit("tensor", lambda e, c=c, pw=pw, wb=wb, ob=ob: e.matmul(
                    pw[:, :], wb[:, c, :], ob[:, c, :],
                    start=(c == 0), stop=(c == 15)),
                    waits=([(swo, n_w)] + ([(sc4, g - 1)] if g >= 2 else []))
                    if c == 0 else [])
            em.emit("tensor", lambda e, oc=oc: e.matmul(
                pwL[:, :], s["bo_sb"][:, oc, :], uo_sb[:, :], start=True, stop=True),
                waits=[(sue4, sue4.n)] + ([(sd4, g)] if g >= 1 else []),
                inc=sm4)
            em.emit("scalar", lambda e, pw=pw: e.activation(
                ev4[:, :], pw[:, :], AF.Copy, scale=s["scol"][:, 3:4]),
                waits=[(sm4, sm4.n)] + ([(sd4, g)] if g >= 1 else []),
                inc=sc4)
            em.emit("vector", lambda e, oc=oc: e.tensor_tensor(
                ao_sb[:, oc, :], ev4[:, :], pwL[:, :], ALU.add),
                waits=[(sc4, sc4.n)], inc=sd4)
        em.emit("scalar", lambda e: e.activation(abf[:, :, :], ao_sb[:, :, :], AF.Copy),
                waits=[(sd4, sd4.n)], inc=SA["sq"])
        emit_ln_stats(em, s, abf, sq4, s1p, s2p, ap_, bp_, rows, SA,
                      w_src=[],
                      w_psum_war=([(SB["var"], SB["var"].n)] if tt >= 1 else []),
                      w_bc_war=([(sla, sla.n)] if tt >= 1 else []))
        for c in range(16):
            em.emit("vector", lambda e, c=c: e.tensor_mul(tmp[:, :], ao_sb[:, c, :], ap_[:, :]),
                    waits=([(SA["bc"], SA["bc"].n), (sxl, n_x)] if c == 0 else []))
            em.emit("vector", lambda e, c=c: e.tensor_tensor(
                tmp[:, :], tmp[:, :], bp_[:, :], ALU.subtract))
            em.emit("vector", lambda e, c=c: e.tensor_tensor(
                xf[:, c, :], xf[:, c, :], tmp[:, :], ALU.add),
                inc=(sxa if c == 15 else None))
        em.emit("sync", lambda e, tt=tt: e.dma_start(out=t["x2_d"][tt], in_=xf[:, :, :]),
                waits=[(sxa, sxa.n)], inc=sx2st)
        em.emit("scalar", lambda e: e.activation(abf[:, :, :], xf[:, :, :], AF.Copy),
                waits=[(sxa, sxa.n)], inc=scast2)
        emit_ln_stats(em, s, abf, sq4, s1p, s2p, ap_, bp_, rows, SB,
                      w_src=[(scast2, scast2.n)],
                      w_psum_war=[(SA["var"], SA["var"].n)],
                      w_bc_war=[(sxa, sxa.n)])
        for c in range(16):
            lf = l2f[:, c % 2]
            em.emit("vector", lambda e, c=c: e.tensor_mul(tmp[:, :], xf[:, c, :], ap_[:, :]),
                    waits=([(SB["bc"], SB["bc"].n), (sln2st, sln2st.n)] if c == 0 else []))
            g4 = tt * 16 + c
            em.emit("vector", lambda e, c=c, lf=lf: e.tensor_tensor(
                lf[:, :], tmp[:, :], bp_[:, :], ALU.subtract),
                waits=([(sgc, g4 - 1)] if g4 >= 2 else []),
                inc=slf)
            em.emit("scalar", lambda e, c=c, lf=lf: e.activation(
                ln2t[:, c, :], lf[:, :], AF.Copy),
                waits=[(slf, slf.n)] + ([(sln2st, sln2st.n)] if c == 0 else []),
                inc=(sla if c == 15 else slc))
            em.emit("tensor", lambda e, c=c, lf=lf: e.matmul(
                pu[0:16, :], s["gate_sb"][:, c, :], lf[:, :],
                start=(c == 0), stop=(c == 15)),
                waits=[(slf, slf.n)] + ([(sue4, sue4.n), (sgev, sgev.n)] if c == 0 else []),
                inc=sgc)
        em.emit("scalar", lambda e: e.copy(glit[:, :], pu[0:16, :]),
                waits=[(sgc, sgc.n)] + ([(sgst, sgst.n)] if tt >= 1 else []),
                inc=sgev)
        em.emit("sync", lambda e, tt=tt: e.dma_start(out=t["glog_d"][tt], in_=glit[:, :]),
                waits=[(sgev, sgev.n)], inc=sgst)
        em.emit("sync", lambda e, tt=tt: e.dma_start(out=t["ln2_d"][tt], in_=ln2t[:, :, :]),
                waits=[(sla, sla.n)], inc=sln2st)

    if DEBUG:
        st = sems["dbg"]
        for tt in range(NTT):
            em.emit("sync", lambda e, tt=tt: e.dma_start(
                out=t["x2_dbg"][tt], in_=t["x2_d"][tt]), waits=[(sx2st, sx2st.n)], inc=st)
            em.emit("sync", lambda e, tt=tt: e.dma_start(
                out=t["ln2_dbg"][tt], in_=t["ln2_d"][tt]), waits=[(sln2st, sln2st.n)], inc=st)
        em.wait("sync", st, st.n)


def ph5_ffn(nc, t, s, em, sems, es):
    sb = lambda n, shp, dt: es.enter_context(nc.sbuf_tensor(n, list(shp), dt))
    psf = lambda n, shp: es.enter_context(nc.psum_tensor(n, list(shp), F32))
    TH = 2 * TT

    ln2h = sb("p5_ln2h", (128, 16, TH), BF16)
    y_acc = sb("p5_y", (128, 16, TH), F32)
    h_sb = sb("p5_h", (128, MCH, TH), BF16)
    w13 = sb("p5_w13", (128, 3, 2, 16, 128), BF16)
    w2b = sb("p5_w2", (128, 2, MCH, 128), BF16)
    oneh = sb("p5_oneh", (16, NE, 128), F32)
    glog = sb("p5_glog", (16, 2, TT), F32)
    gtok = sb("p5_gtok", (128, 8, NE), F32)
    gw_tok = sb("p5_gw", (128, 8, NE), F32)
    wfm = sb("p5_wfm", (16, 2, TT), F32)
    grow = sb("p5_grow", (128, 4), F32)
    gl2 = sb("p5_gl2", (128, NE), F32)
    sil_sb = sb("p5_sil", (128, 2, TT), BF16)
    tmp5 = sb("p5_tmp", (128, TT), F32)
    sq5 = sb("p5_sq", (128, 16, TT), BF16)
    rows = sb("p5_rows", (1, 5, TT), F32)
    xc = sb("p5_xc", (128, 2, TT), F32)
    outc = sb("p5_outc", (128, 2, TT), F32)

    hp = [psf("p5_hA", (128, TT)), psf("p5_hB", (128, TT))]
    hp3 = [psf("p5_hC", (128, TT)), psf("p5_hD", (128, TT))]
    yp = [psf("p5_yE", (128, TT)), psf("p5_yF", (128, TT))]
    gp = [psf("p5_gG", (128, TT)), psf("p5_gH", (128, TT))]

    mk = sems["mk"]
    sl2l = mk("p5l2l", 16); sgl = mk("p5gl", 16); sw13 = mk("p5w13", 16)
    sw2l = mk("p5w2l", 16); sxr = mk("p5xr", 16); sout = mk("p5out", 16)
    sgt = mk("p5gt"); sgtc = mk("p5gtc"); sgm1 = mk("p5gm1")
    sgm2 = mk("p5gm2"); sgnm = mk("p5gnm"); sgex = mk("p5gex")
    sga = mk("p5ga"); sgb = mk("p5gb"); sgc5 = mk("p5gc5"); sgd = mk("p5gd")
    sgrc = mk("p5grc"); sgw = mk("p5gw"); sgwt = mk("p5gwt"); swfm = mk("p5wfm")
    sbc = mk("p5bc"); sh5 = mk("p5hmm"); ssil = mk("p5sil")
    she = mk("p5he"); sy5 = mk("p5ymm"); syd = mk("p5yd")
    SC = sems["LS"]
    sfa = mk("p5fa")
    sol = mk("p5onl", 16)
    p4l2 = sems["p4l2st"]; p4gst = sems["p4gst"]; p4x2 = sems["p4x2st"]

    em.emit("sync", lambda e: e.dma_start(out=oneh[:, :, :], in_=t["onehot"][:, :, :]),
            inc=sol)
    n_oneh = sol.n

    for hf in range(2):
        for u in range(2):
            tt = 2 * hf + u
            em.emit("sync", lambda e, tt=tt, u=u: e.dma_start(
                out=ln2h[:, :, u * TT:(u + 1) * TT], in_=t["ln2_d"][tt]),
                waits=[(p4l2, 64)] + ([(sh5, sh5.n)] if hf >= 1 and u == 0 else []),
                inc=sl2l)
            em.emit("sync", lambda e, tt=tt, u=u: e.dma_start(
                out=glog[:, u, :], in_=t["glog_d"][tt]),
                waits=[(p4gst, 64)] + ([(sgt, sgt.n)] if hf >= 1 and u == 0 else []),
                inc=sgl)
        n_l2 = sl2l.n
        n_gl = sgl.n
        for sl in range(8):
            u, sli = sl // 4, sl % 4
            wtr = [(sgl, n_gl)]
            if (hf, sl) != (0, 0):
                wtr.append((sgtc, sgtc.n))
            if hf >= 1 and sl == 0:
                wtr.append((SC["var"], SC["var"].n))
            em.emit("tensor", lambda e, u=u, sli=sli: e.transpose(
                hp[0][:, 0:16], glog[:, u, sli * 128:(sli + 1) * 128],
                s["idf_sb"][0:16, 0:16]),
                waits=wtr, inc=sgt)
            em.emit("vector", lambda e, sl=sl: e.tensor_copy(
                gtok[:, sl, :], hp[0][:, 0:16]),
                waits=[(sgt, sgt.n)] +
                      ([(sgex, sgex.n)] if hf >= 1 and sl == 0 else []),
                inc=sgtc)
        for sl in range(8):
            em.emit("vector", lambda e, sl=sl: e.reduce_max(
                grow[:, 0:1], gtok[:, sl, :], axis=AX.X), inc=sgm1)
            em.emit("vector", lambda e, sl=sl: e.tensor_scalar(
                gl2[:, :], gtok[:, sl, :], grow[:, 0:1], None, ALU.is_ge),
                waits=[(sgm1, sgm1.n)], inc=sga)
            em.emit("vector", lambda e, sl=sl: e.scalar_tensor_tensor(
                gl2[:, :], gl2[:, :], NEG_BIG, gtok[:, sl, :], ALU.mult, ALU.add),
                waits=[(sga, sga.n)], inc=sgb)
            em.emit("vector", lambda e: e.reduce_max(
                grow[:, 1:2], gl2[:, :], axis=AX.X),
                waits=[(sgb, sgb.n)], inc=sgm2)
            em.emit("vector", lambda e, sl=sl: e.tensor_scalar(
                gl2[:, :], gtok[:, sl, :], grow[:, 1:2], None, ALU.is_ge),
                waits=[(sgm2, sgm2.n)], inc=sgc5)
            em.emit("vector", lambda e: e.tensor_scalar(
                grow[:, 0:1], grow[:, 0:1], -1.0, None, ALU.mult), inc=sgnm)
            em.emit("scalar", lambda e, sl=sl: e.activation(
                gw_tok[:, sl, :], gtok[:, sl, :], AF.Exp, bias=grow[:, 0:1],
                accum_out=grow[:, 2:3]),
                waits=[(sgnm, sgnm.n)], inc=sgex)
            em.emit("vector", lambda e: e.reciprocal(grow[:, 3:4], grow[:, 2:3]),
                    waits=[(sgex, sgex.n)], inc=sgrc)
            em.emit("vector", lambda e, sl=sl: e.tensor_scalar(
                gw_tok[:, sl, :], gw_tok[:, sl, :], grow[:, 3:4], None, ALU.mult),
                waits=[(sgrc, sgrc.n)], inc=sgd)
            em.emit("vector", lambda e, sl=sl: e.tensor_mul(
                gw_tok[:, sl, :], gw_tok[:, sl, :], gl2[:, :]),
                waits=[(sgd, sgd.n), (sgc5, sgc5.n)], inc=sgw)
        for sl in range(8):
            u, sli = sl // 4, sl % 4
            wtr = [(sgw, sgw.n)]
            if (hf, sl) != (0, 0):
                wtr.append((swfm, swfm.n))
            if hf >= 1 and sl == 0:
                wtr.append((SC["var"], SC["var"].n))
            em.emit("tensor", lambda e, sl=sl: e.transpose(
                hp[1][0:16, 0:128], gw_tok[:, sl, :], s["idf_sb"][:, :]),
                waits=wtr, inc=sgwt)
            em.emit("vector", lambda e, u=u, sli=sli: e.tensor_copy(
                wfm[0:16, u, sli * 128:(sli + 1) * 128], hp[1][0:16, 0:128]),
                waits=[(sgwt, sgwt.n)], inc=swfm)
        for e_ in range(PH5_NEXP):
            routed = e_ < NE
            if routed:
                for u in range(2):
                    em.emit("tensor", lambda e, e_=e_, u=u: e.matmul(
                        gp[u][:, :], oneh[:, e_, :], wfm[:, u, :],
                        start=True, stop=True),
                        waits=([(sol, n_oneh), (swfm, swfm.n), (she, she.n)]
                               if u == 0 else []),
                        inc=sbc)
            for m in range(MCH):
                wg = (hf * (NE + 2) + e_) * MCH + m
                r = wg % 3
                if routed:
                    src1, src3 = t["w1_t"][e_, m], t["w3_t"][e_, m]
                else:
                    src1 = t["sw1_t"][(e_ - NE) * MCH + m]
                    src3 = t["sw3_t"][(e_ - NE) * MCH + m]
                em.emit("gpsimd", lambda e, o=w13[:, r, 0], i=src1: e.dma_start(out=o, in_=i),
                        waits=([(sh5, 2 * (wg - 2))] if wg >= 3 else []), inc=sw13)
                em.emit("gpsimd", lambda e, o=w13[:, r, 1], i=src3: e.dma_start(out=o, in_=i),
                        inc=sw13)
                n_w = sw13.n
                for u in range(2):
                    idx = m * 2 + u
                    h1, h3 = hp[idx % 2], hp3[idx % 2]
                    usl = slice(u * TT, (u + 1) * TT)
                    w0 = [(sw13, n_w), (sl2l, n_l2)]
                    if ssil.n >= 2:
                        w0.append((ssil, ssil.n - 1))
                    if she.n >= 2:
                        w0.append((she, she.n - 1))
                    if e_ == 0 and m == 0 and u == 0 and hf >= 1:
                        w0.append((sfa, sfa.n))
                    if e_ == 0 and m == 0 and u == 0:
                        w0.append((sgtc, sgtc.n))
                    if e_ == 0 and m == 0 and u == 1:
                        w0.append((swfm, swfm.n))
                    for c in range(16):
                        em.emit("tensor", lambda e, c=c, h1=h1, r=r, usl=usl: e.matmul(
                            h1[:, :], w13[:, r, 0, c, :], ln2h[:, c, usl],
                            start=(c == 0), stop=(c == 15)),
                            waits=(w0 if c == 0 else []))
                    for c in range(16):
                        em.emit("tensor", lambda e, c=c, h3=h3, r=r, usl=usl: e.matmul(
                            h3[:, :], w13[:, r, 1, c, :], ln2h[:, c, usl],
                            start=(c == 0), stop=(c == 15)),
                            inc=(sh5 if c == 15 else None))
                    sil = sil_sb[:, idx % 2]
                    em.emit("scalar", lambda e, h1=h1, sil=sil: e.activation(
                        sil[:, :], h1[:, :], AF.Silu),
                        waits=[(sh5, sh5.n)] + ([(she, she.n - 1)] if she.n >= 2 else []),
                        inc=ssil)
                    wh = [(ssil, ssil.n)]
                    if m == 0 and u == 0:
                        wh.append((sy5, sy5.n))
                        if routed:
                            wh.append((sbc, sbc.n))
                    if routed:
                        em.emit("vector", lambda e, h3=h3, sil=sil: e.tensor_tensor(
                            tmp5[:, :], sil[:, :], h3[:, :], ALU.mult), waits=wh)
                        em.emit("vector", lambda e, m=m, u=u, usl=usl: e.tensor_tensor(
                            h_sb[:, m, usl], tmp5[:, :], gp[u][:, :], ALU.mult),
                            inc=she)
                    else:
                        em.emit("vector", lambda e, h3=h3, m=m, usl=usl, sil=sil: e.tensor_tensor(
                            h_sb[:, m, usl], sil[:, :], h3[:, :], ALU.mult),
                            waits=wh, inc=she)
            for dc in range(16):
                wg2 = (hf * (NE + 2) + e_) * 16 + dc
                r2 = wg2 % 2
                src2 = t["w2_t"][e_, dc] if routed else t["sw2_t"][dc, e_ - NE]
                em.emit("gpsimd", lambda e, o=w2b[:, r2], i=src2: e.dma_start(out=o, in_=i),
                        waits=([(sy5, 2 * (wg2 - 1))] if wg2 >= 2 else []), inc=sw2l)
                n_w2 = sw2l.n
                for u in range(2):
                    idx2 = dc * 2 + u
                    ypb = yp[idx2 % 2]
                    usl = slice(u * TT, (u + 1) * TT)
                    wv = [(sw2l, n_w2)]
                    if dc == 0 and u == 0:
                        wv.append((she, she.n))
                    if syd.n >= 2:
                        wv.append((syd, syd.n - 1))
                    for m in range(MCH):
                        em.emit("tensor", lambda e, m=m, ypb=ypb, r2=r2, usl=usl: e.matmul(
                            ypb[:, :], w2b[:, r2, m, :], h_sb[:, m, usl],
                            start=(m == 0), stop=(m == MCH - 1)),
                            waits=(wv if m == 0 else []),
                            inc=(sy5 if m == MCH - 1 else None))
                    wy = [(sy5, sy5.n)]
                    if e_ == 0 and dc == 0 and u == 0 and hf >= 1:
                        wy.append((sfa, sfa.n))
                    if e_ == 0:
                        em.emit("vector", lambda e, dc=dc, ypb=ypb, usl=usl: e.tensor_copy(
                            y_acc[:, dc, usl], ypb[:, :]), waits=wy, inc=syd)
                    else:
                        em.emit("vector", lambda e, dc=dc, ypb=ypb, usl=usl: e.tensor_tensor(
                            y_acc[:, dc, usl], y_acc[:, dc, usl], ypb[:, :], ALU.add),
                            waits=wy, inc=syd)
        if not PH5_FINAL:
            continue
        ybf = h_sb[:, :, :].rearrange("p a b -> p (a b)")[:, 0:16 * TT].rearrange(
            "p (c s) -> p c s", c=16)
        for u in range(2):
            tt = 2 * hf + u
            usl = slice(u * TT, (u + 1) * TT)
            em.emit("scalar", lambda e, usl=usl, ybf=ybf: e.activation(
                ybf[:, :, :], y_acc[:, :, usl], AF.Copy),
                waits=[(syd, syd.n), (sy5, sy5.n)], inc=SC["sq"])
            emit_ln_stats(em, s, ybf, sq5, hp[0][0:1, :], hp[1][0:1, :],
                          hp3[0], hp3[1], rows, SC,
                          w_src=[],
                          w_psum_war=[(she, she.n), (SC["var"], SC["var"].n)],
                          w_bc_war=[(sfa, sfa.n)])
            for c in range(16):
                xcb = xc[:, c % 2]
                ocb = outc[:, c % 2]
                gc = (hf * 2 + u) * 16 + c
                if PH5_RES:
                    em.emit("sync", lambda e, tt=tt, c=c, xcb=xcb: e.dma_start(
                        out=xcb, in_=t["x2_d"][tt][:, c, :]),
                        waits=[(p4x2, 64)] + ([(sfa, gc - 1)] if gc >= 2 else []),
                        inc=sxr)
                n_xc = sxr.n
                em.emit("vector", lambda e, c=c, usl=usl: e.tensor_mul(
                    tmp5[:, :], y_acc[:, c, usl], hp3[0][:, :]),
                    waits=([(SC["bc"], SC["bc"].n)] if c == 0 else []))
                em.emit("vector", lambda e: e.tensor_tensor(
                    tmp5[:, :], tmp5[:, :], hp3[1][:, :], ALU.subtract))
                if PH5_RES:
                    em.emit("vector", lambda e, xcb=xcb, ocb=ocb, gc=gc: e.tensor_tensor(
                        ocb, xcb, tmp5[:, :], ALU.add),
                        waits=[(sxr, n_xc)] + ([(sout, 16 * (gc - 1))] if gc >= 2 else []),
                        inc=sfa)
                else:
                    em.emit("vector", lambda e, ocb=ocb, gc=gc: e.tensor_copy(
                        ocb, tmp5[:, :]),
                        waits=([(sout, 16 * (gc - 1))] if gc >= 2 else []),
                        inc=sfa)
                em.emit("sync", lambda e, tt=tt, c=c, ocb=ocb: e.dma_start(
                    out=t["out_t"][tt][:, c, :], in_=ocb),
                    waits=[(sfa, sfa.n)], inc=sout)
    em.wait("sync", sout, sout.n)
    em.wait("gpsimd", sout, sout.n)


BUILD_PHASES = ("ph1", "ph2", "ph3", "ph4", "ph5")


RACE_DET = True


def build_nc():
    nc = bass.Bass("TRN2", target_bir_lowering=False, detect_race_conditions=RACE_DET)
    t = declare_ios(nc)
    es = contextlib.ExitStack()
    with es:
        s = alloc_persistent(nc, es)
        es_lora = contextlib.ExitStack()
        for nm, shp in (("lora_a", [128, 4, 16, R]), ("bq_sb", [R, 16, 128]),
                        ("bk_sb", [R, 8, 128]), ("bv_sb", [R, 8, 128]),
                        ("bo_sb", [R, 16, 128])):
            s[nm] = es_lora.enter_context(nc.sbuf_tensor(nm, shp, BF16, side="right"))
        sems = {}
        for nm, step in (("ld0", 16), ("ld0s", 16), ("ev0", 1), ("xld", 16), ("ph1", 1),
                         ("dbg", 16), ("fin", 16)):
            sems[nm] = Sem(es, nc, nm, step)
        def _mk(nm, step=1):
            sems[nm] = Sem(es, nc, nm, step)
            return sems[nm]
        sems["mk"] = _mk
        sems["LS"] = {k: _mk("L" + k) for k in
                      ("sq", "stat", "var", "rstd", "nmr", "bc")}

        es0 = contextlib.ExitStack()
        with es0:
            em0 = Em()
            with nc.Block("ph0") as blk:
                ph0_setup(nc, t, s, em0, sems, es0)
                em0.install(blk)

        kv = {}
        es_kv = contextlib.ExitStack()
        with es_kv:
            if "ph2" in BUILD_PHASES:
                kv["k_sb"] = es_kv.enter_context(
                    nc.sbuf_tensor("k_sb", [128, KVH, T], BF16, side="right"))
                kv["v_sb"] = es_kv.enter_context(
                    nc.sbuf_tensor("v_sb", [128, 16, KVH * 128], BF16, side="right"))
            es_ln1 = contextlib.ExitStack()
            with es_ln1:
                if "ph1" in BUILD_PHASES:
                    em1 = Em()
                    es1loc = contextlib.ExitStack()
                    with es1loc:
                        with nc.Block("ph1") as blk:
                            ph1_ln1(nc, t, s, em1, sems, es_ln1, es1loc)
                            ln1 = es_ln1._ln1
                            em1.install(blk)
                if "ph2" in BUILD_PHASES:
                    es2 = contextlib.ExitStack()
                    with es2:
                        em2 = Em()
                        with nc.Block("ph2") as blk:
                            ph2_qkv(nc, t, s, em2, sems, es2, ln1, kv)
                            em2.install(blk)
            if "ph3" in BUILD_PHASES:
                es3 = contextlib.ExitStack()
                with es3:
                    em3 = Em()
                    with nc.Block("ph3") as blk:
                        ph3_attn(nc, t, s, em3, sems, es3, kv)
                        em3.install(blk)
        if "ph4" in BUILD_PHASES:
            es4 = contextlib.ExitStack()
            with es4:
                em4 = Em()
                with nc.Block("ph4") as blk:
                    ph4_woln(nc, t, s, em4, sems, es4)
                    em4.install(blk)
        es_lora.close()
        if "ph5" in BUILD_PHASES:
            es5 = contextlib.ExitStack()
            with es5:
                em5 = Em()
                with nc.Block("ph5") as blk:
                    ph5_ffn(nc, t, s, em5, sems, es5)
                    em5.install(blk)

        if "ph5" not in BUILD_PHASES or not PH5_FINAL:
            emf = Em()
            fin = sems["fin"]
            with nc.Block("dummy_out") as blk:
                for tt in range(NTT):
                    emf.emit("sync", lambda e, tt=tt: e.dma_start(
                        out=t["out_t"][tt], in_=t["x_t"][tt]), inc=fin)
                emf.wait("sync", fin, fin.n)
                emf.install(blk)
    return nc


def kernel(**inputs):
    global DEBUG
    nc = build_nc()
    w = pack_weights(inputs)
    consts = host_constants()
    xs = pack_x(np.asarray(inputs["L0_attn_x"], np.float32))
    in_maps = []
    for b in range(B):
        m = dict(w)
        m.update(consts)
        m["x_t"] = xs[b]
        in_maps.append(m)
    res = run_bass_kernel_spmd(nc, in_maps, core_ids=list(range(B)))
    outs = np.stack([unpack_out(res.results[b]["out_t"]) for b in range(B)])
    return outs.astype(np.float32)


# version 13 (final)
# speedup vs baseline: 1.1054x; 1.0843x over previous
import contextlib
import numpy as np
import ml_dtypes

import concourse.bass as bass
import concourse.mybir as mybir
from concourse.bass_utils import run_bass_kernel_spmd

F32 = mybir.dt.float32
BF16 = mybir.dt.bfloat16
AF = mybir.ActivationFunctionType
ALU = mybir.AluOpType
AX = mybir.AxisListType

B = 8
T = 2048
D = 2048
NC_CHUNK = 16
H, KVH, HD = 16, 8, 128
R = 32
NE, INTER, MCH = 16, 1408, 11
SI, SMCH = 2816, 22
TT = 512
NTT = T // TT
RMS_EPS = 1.1920929e-07
LN_EPS = 1e-5
ISQ_HD = float(1.0 / np.sqrt(128.0))
NEG_BIG = -1.0e30

DEBUG = False
PH5_NEXP = NE + 2
PH5_FINAL = True
PH5_RES = True
PH3_INTER = True


class Sem:
    def __init__(self, es, nc, name, step=1):
        self.h = es.enter_context(nc.semaphore(name))
        self.n = 0
        self.step = step

    def bump(self, k=1):
        self.n += k * self.step
        return self.n


class Em:
    ENGINES = ("sync", "tensor", "vector", "scalar", "gpsimd")

    def __init__(self):
        self.q = {e: [] for e in self.ENGINES}

    def emit(self, eng, fn, waits=(), inc=None, inck=1):
        wl = tuple(waits)
        tgt = None
        if inc is not None:
            tgt = inc.bump(inck)
        def run(e, fn=fn, wl=wl, inc=inc, inck=inck):
            for s, v in wl:
                e.wait_ge(s.h, v)
            inst = fn(e)
            if inc is not None:
                inst.then_inc(inc.h, inck * inc.step)
        self.q[eng].append(run)
        return tgt

    def wait(self, eng, sem, val):
        self.q[eng].append(lambda e, s=sem, v=val: e.wait_ge(s.h, v))

    def install(self, block):
        for eng in self.ENGINES:
            fns = self.q[eng]
            if fns:
                getattr(block, eng)(lambda e, fns=fns: [f(e) for f in fns])


def _c(a):
    return np.ascontiguousarray(a)


def pack_weights(i):
    d = {}
    W_q = np.asarray(i["L0_attn_W_q"]); W_k = np.asarray(i["L0_attn_W_k"])
    W_v = np.asarray(i["L0_attn_W_v"]); W_o = np.asarray(i["L0_attn_W_o"])
    d["wq_t"] = _c(W_q.reshape(16, 128, 16, 128).transpose(0, 3, 2, 1))
    d["wk_t"] = _c(W_k.reshape(8, 128, 16, 128).transpose(0, 3, 2, 1))
    d["wv_t"] = _c(W_v.reshape(8, 128, 16, 128).transpose(0, 3, 2, 1))
    d["wo_t"] = _c(W_o.reshape(16, 128, 16, 128).transpose(0, 3, 2, 1))
    for nm, key in (("aq", "A_q"), ("ak", "A_k"), ("av", "A_v"), ("ao", "A_o")):
        A = np.asarray(i[f"L0_attn_{key}"])
        d[nm + "_t"] = _c(A.reshape(R, 16, 128).transpose(2, 1, 0))
    for nm, key, noc in (("bq", "B_q", 16), ("bk", "B_k", 8), ("bv", "B_v", 8), ("bo", "B_o", 16)):
        Bm = np.asarray(i[f"L0_attn_{key}"])
        d[nm + "_t"] = _c(Bm.reshape(noc, 128, R).transpose(2, 0, 1))
    d["svec"] = _c(np.stack([np.asarray(i["L0_attn_s_q"]).reshape(()),
                             np.asarray(i["L0_attn_s_k"]).reshape(()),
                             np.asarray(i["L0_attn_s_v"]).reshape(()),
                             np.asarray(i["L0_attn_s_o"]).reshape(())])[None, :])
    d["qgain"] = _c(np.asarray(i["L0_attn_q_gain"]).reshape(1, H))
    gw = np.asarray(i["L0_ffn_gate_w"])
    d["gate_t"] = _c(gw.reshape(NE, 16, 128).transpose(2, 1, 0))
    w1 = np.asarray(i["L0_ffn_w1"]); w2 = np.asarray(i["L0_ffn_w2"]); w3 = np.asarray(i["L0_ffn_w3"])
    d["w1_t"] = _c(w1.reshape(NE, MCH, 128, 16, 128).transpose(0, 1, 4, 3, 2))
    d["w3_t"] = _c(w3.reshape(NE, MCH, 128, 16, 128).transpose(0, 1, 4, 3, 2))
    d["w2_t"] = _c(w2.reshape(NE, 16, 128, MCH, 128).transpose(0, 1, 4, 3, 2))
    sw1 = np.asarray(i["L0_ffn_sw1"]); sw2 = np.asarray(i["L0_ffn_sw2"]); sw3 = np.asarray(i["L0_ffn_sw3"])
    d["sw1_t"] = _c(sw1.reshape(SMCH, 128, 16, 128).transpose(0, 3, 2, 1))
    d["sw3_t"] = _c(sw3.reshape(SMCH, 128, 16, 128).transpose(0, 3, 2, 1))
    d["sw2_t"] = _c(sw2.reshape(16, 128, 2, 11, 128).transpose(0, 2, 4, 3, 1))
    return d


def host_constants():
    inv = 1.0 / (10000.0 ** (np.arange(0, 64, 2, dtype=np.float64) / 64.0))
    freqs = np.outer(np.arange(T, dtype=np.float64), inv)
    cos = np.cos(freqs).astype(np.float32).T
    sin = np.sin(freqs).astype(np.float32).T
    cs = np.concatenate([cos, cos], axis=0)
    ss = np.concatenate([-sin, sin], axis=0)
    cm = np.where(np.arange(128)[:, None] >= np.arange(128)[None, :],
                  0.0, NEG_BIG).astype(np.float32)
    ident = np.eye(128, dtype=ml_dtypes.bfloat16)
    ident_f32 = np.eye(128, dtype=np.float32)
    ones_col_bf = np.ones((128, 1), dtype=ml_dtypes.bfloat16)
    ones_row_f32 = np.ones((1, 128), dtype=np.float32)
    onehot = np.zeros((16, NE, 128), dtype=np.float32)
    for e in range(NE):
        onehot[e, e, :] = 1.0
    onehot = _c(onehot.transpose(1, 0, 2))
    return dict(ropecs=_c(cs), ropess=_c(ss), cmask=cm, ident=ident,
                ones_col_bf=ones_col_bf, ones_row_f32=ones_row_f32,
                onehot=onehot, ident_f32=ident_f32)


def pack_x(x):
    out = []
    for b in range(B):
        xt = x[b].T
        a = xt.reshape(16, 128, NTT, TT).transpose(2, 1, 0, 3)
        out.append(_c(a.astype(np.float32, copy=False)))
    return out


def unpack_out(o):
    return _c(o.transpose(2, 1, 0, 3).reshape(D, T).T)


def declare_ios(nc):
    t = {}
    def din(name, shape, dt=F32):
        t[name] = nc.dram_tensor(name, list(shape), dt, kind="ExternalInput").ap()
    def dout(name, shape, dt=F32):
        t[name] = nc.dram_tensor(name, list(shape), dt, kind="ExternalOutput").ap()
    def dint(name, shape, dt):
        t[name] = nc.dram_tensor(name, list(shape), dt, kind="Internal").ap()

    din("x_t", (NTT, 128, 16, TT))
    din("wq_t", (16, 128, 16, 128)); din("wk_t", (8, 128, 16, 128))
    din("wv_t", (8, 128, 16, 128)); din("wo_t", (16, 128, 16, 128))
    for nm in ("aq", "ak", "av", "ao"):
        din(nm + "_t", (128, 16, R))
    din("bq_t", (R, 16, 128)); din("bk_t", (R, 8, 128))
    din("bv_t", (R, 8, 128)); din("bo_t", (R, 16, 128))
    din("svec", (1, 4)); din("qgain", (1, H))
    din("gate_t", (128, 16, NE))
    din("w1_t", (NE, MCH, 128, 16, 128)); din("w3_t", (NE, MCH, 128, 16, 128))
    din("w2_t", (NE, 16, 128, MCH, 128))
    din("sw1_t", (SMCH, 128, 16, 128)); din("sw3_t", (SMCH, 128, 16, 128))
    din("sw2_t", (16, 2, 128, 11, 128))
    din("ropecs", (64, T)); din("ropess", (64, T))
    din("cmask", (128, 128)); din("ident", (128, 128), BF16)
    din("ones_col_bf", (128, 1), BF16); din("ones_row_f32", (1, 128))
    din("onehot", (16, NE, 128)); din("ident_f32", (128, 128))

    dout("out_t", (NTT, 128, 16, TT))

    dint("q_d", (H, 128, T), BF16)
    dint("o_d", (H, 128, T), BF16)
    dint("x2_d", (NTT, 128, 16, TT), F32)
    dint("ln2_d", (NTT, 128, 16, TT), BF16)
    dint("glog_d", (NTT, 16, TT), F32)
    dint("w1b_d", (NE, MCH, 128, 16, 128), BF16)
    dint("w3b_d", (NE, MCH, 128, 16, 128), BF16)
    dint("w2b_d", (NE, 16, 128, MCH, 128), BF16)
    dint("sw1b_d", (SMCH, 128, 16, 128), BF16)
    dint("sw3b_d", (SMCH, 128, 16, 128), BF16)
    dint("sw2b_d", (16, 128, SMCH, 128), BF16)

    if DEBUG:
        dout("ln1_dbg", (128, 16, T), BF16)
        dout("q_dbg", (H, 128, T), BF16)
        dout("k_dbg", (KVH, 128, T), BF16)
        dout("v_dbg", (128, 16, KVH * 128), BF16)
        dout("o_dbg", (H, 128, T), BF16)
        dout("x2_dbg", (NTT, 128, 16, TT), F32)
        dout("ln2_dbg", (NTT, 128, 16, TT), BF16)
        dout("gatew_dbg", (128, NTT * 4, NE), F32)
        dout("pr_dbg", (128, TT), BF16)
        dout("sm_dbg", (128, 8), F32)
    return t


def alloc_persistent(nc, es):
    s = {}
    def sb(name, shape, dt):
        s[name] = es.enter_context(nc.sbuf_tensor(name, list(shape), dt))
    sb("gate_sb", (128, 16, NE), F32)
    sb("svec_sb", (1, 4), F32)
    sb("qgain_sb", (1, H), F32)
    sb("scol", (128, 4), F32)
    sb("cm_sb", (128, 128), F32)
    sb("id_sb", (128, 128), BF16)
    sb("idf_sb", (128, 128), F32)
    sb("ones_col", (128, 1), BF16)
    sb("ones_row", (1, 128), F32)
    sb("cpool", (1, 4), F32)
    return s


def ph0_setup(nc, t, s, em, sems, es):
    ld = sems["ld0"]
    lds = sems["ld0s"]
    def dma(eng, out, in_):
        return em.emit(eng, lambda e, o=out, i=in_: e.dma_start(out=o, in_=i),
                       inc=(ld if eng == "gpsimd" else lds))
    for j, nm in enumerate(("aq", "ak", "av", "ao")):
        dma("gpsimd", s["lora_a"][:, j, :, :], t[nm + "_t"][:, :, :])
    dma("gpsimd", s["bq_sb"][:, :, :], t["bq_t"][:, :, :])
    dma("gpsimd", s["bk_sb"][:, :, :], t["bk_t"][:, :, :])
    dma("gpsimd", s["bv_sb"][:, :, :], t["bv_t"][:, :, :])
    dma("gpsimd", s["bo_sb"][:, :, :], t["bo_t"][:, :, :])
    dma("sync", s["gate_sb"][:, :, :], t["gate_t"][:, :, :])
    dma("sync", s["svec_sb"][:, :], t["svec"][:, :])
    dma("sync", s["qgain_sb"][:, :], t["qgain"][:, :])
    dma("sync", s["cm_sb"][:, :], t["cmask"][:, :])
    dma("sync", s["id_sb"][:, :], t["ident"][:, :])
    dma("sync", s["idf_sb"][:, :], t["ident_f32"][:, :])
    dma("sync", s["ones_col"][:, :], t["ones_col_bf"][:, :])
    dma("sync", s["ones_row"][:, :], t["ones_row_f32"][:, :])
    n_all = ld.n
    em.emit("vector", lambda e: e.memset(s["cpool"][0:1, 0:1], LN_EPS))
    em.emit("vector", lambda e: e.memset(s["cpool"][0:1, 1:2], RMS_EPS))
    em.emit("vector", lambda e: e.memset(s["cpool"][0:1, 2:3], -0.5))
    ps = es.enter_context(nc.psum_tensor("ph0_ps", [128, 4], F32))
    mmdone = sems["ev0"]
    em.emit("tensor",
            lambda e: e.matmul(ps[:, :], s["ones_row"][:, :], s["svec_sb"][:, :],
                               start=True, stop=True),
            waits=[(ld, n_all), (lds, lds.n)], inc=mmdone)
    em.emit("vector", lambda e: e.tensor_copy(s["scol"][:, :], ps[:, :]),
            waits=[(mmdone, mmdone.n)], inc=mmdone)


def ph1_ln1(nc, t, s, em, sems, es, es_local):
    ln1 = es.enter_context(nc.sbuf_tensor("ln1", [128, 16, T], BF16))
    sb = lambda n, shp, dt: es_local.enter_context(nc.sbuf_tensor(n, list(shp), dt))
    ps = lambda n, shp: es_local.enter_context(nc.psum_tensor(n, list(shp), F32))

    xbf = sb("ph1_xbf", (128, 2, 16, TT), BF16)
    sqb = sb("ph1_sqb", (128, 16, TT), BF16)
    rows = sb("ph1_rows", (1, 5, TT), F32)
    tmp = sb("ph1_tmp", (128, TT), F32)

    s1p = ps("ph1_s1", (1, TT)); s2p = ps("ph1_s2", (1, TT))
    ap_ = ps("ph1_a", (128, TT)); bp_ = ps("ph1_b", (128, TT))

    xld = sems["xld"]
    S = dict(sems["LS"])
    S["aff"] = sems["mk"]("p1aff")

    MEAN, M2, VAR, NMR, RSTD = 0, 1, 2, 3, 4

    for tt in range(NTT):
        buf = xbf[:, tt % 2]
        em.emit("gpsimd", lambda e, o=buf, i=t["x_t"][tt]: e.dma_start(out=o, in_=i),
                waits=([(S["aff"], tt - 1)] if tt >= 2 else []), inc=xld)
        n_x = xld.n
        em.emit("scalar", lambda e, i=buf: e.activation(sqb[:, :, :], i[:, :, :], AF.Square),
                waits=[(xld, n_x)] + ([(S["stat"], tt)] if tt >= 1 else []),
                inc=S["sq"])
        for c in range(16):
            em.emit("tensor", lambda e, c=c, b_=buf: e.matmul(
                s1p[:, :], s["ones_col"][:, :], b_[:, c, :],
                start=(c == 0), stop=(c == 15)),
                waits=([(S["sq"], tt + 1)] + ([(S["var"], tt)] if tt >= 1 else []))
                if c == 0 else [])
        for c in range(16):
            em.emit("tensor", lambda e, c=c: e.matmul(
                s2p[:, :], s["ones_col"][:, :], sqb[:, c, :],
                start=(c == 0), stop=(c == 15)),
                inc=(S["stat"] if c == 15 else None))
        em.emit("vector", lambda e: e.tensor_scalar(
            rows[:, MEAN, :], s1p[:, :], 1.0 / D, None, ALU.mult),
            waits=[(S["stat"], tt + 1)])
        em.emit("vector", lambda e: e.tensor_mul(rows[:, M2, :], rows[:, MEAN, :], rows[:, MEAN, :]))
        em.emit("vector", lambda e: e.scalar_tensor_tensor(
            rows[:, VAR, :], s2p[:, :], 1.0 / D, rows[:, M2, :],
            ALU.mult, ALU.subtract), inc=S["var"])
        em.emit("scalar", lambda e: e.activation(
            rows[:, RSTD, :], rows[:, VAR, :], AF.Ln, bias=s["cpool"][0:1, 0:1]),
            waits=[(S["var"], tt + 1)])
        em.emit("scalar", lambda e: e.activation(
            rows[:, RSTD, :], rows[:, RSTD, :], AF.Exp, scale=s["cpool"][0:1, 2:3]),
            inc=S["rstd"])
        em.emit("vector", lambda e: e.tensor_mul(
            rows[:, NMR, :], rows[:, MEAN, :], rows[:, RSTD, :]),
            waits=[(S["rstd"], tt + 1)], inc=S["nmr"])
        em.emit("tensor", lambda e: e.matmul(
            ap_[:, :], s["ones_row"][:, :], rows[:, RSTD, :], start=True, stop=True),
            waits=[(S["nmr"], tt + 1)])
        em.emit("tensor", lambda e: e.matmul(
            bp_[:, :], s["ones_row"][:, :], rows[:, NMR, :], start=True, stop=True),
            inc=S["bc"])
        for c in range(16):
            em.emit("vector", lambda e, c=c, b_=buf: e.tensor_mul(
                tmp[:, :], b_[:, c, :], ap_[:, :]),
                waits=[(S["bc"], tt + 1)] if c == 0 else [])
            em.emit("vector", lambda e, c=c, tt=tt: e.tensor_tensor(
                ln1[:, c, tt * TT:(tt + 1) * TT], tmp[:, :], bp_[:, :], ALU.subtract),
                inc=(S["aff"] if c == 15 else None))
    if DEBUG:
        st = sems["dbg"]
        em.emit("sync", lambda e: e.dma_start(out=t["ln1_dbg"][:, :, :], in_=ln1[:, :, :]),
                waits=[(S["aff"], NTT)], inc=st)
        em.wait("sync", st, st.n)
    es._ln1 = ln1
    return ln1


def ph2_qkv(nc, t, s, em, sems, es, ln1, kv):
    sb = lambda n, shp, dt, **kw: es.enter_context(nc.sbuf_tensor(n, list(shp), dt, **kw))
    psf = lambda n, shp: es.enter_context(nc.psum_tensor(n, list(shp), F32))
    psb = lambda n, shp: es.enter_context(nc.psum_tensor(n, list(shp), BF16))

    k_sb, v_sb = kv["k_sb"], kv["v_sb"]
    cs_sb = sb("cs_sb", (64, T), F32)
    ss_sb = sb("ss_sb", (64, T), F32)
    wring = sb("wring", (128, 2, 16, 128), BF16)
    u_sb = sb("u_sb", (R, 3, T), BF16)
    ev_f32 = sb("ev_f32", (128, TT), F32)
    qraw = sb("qraw", (128, TT), F32)
    qn_sb = sb("qn_sb", (128, TT), F32)
    qf_hb = sb("qf_hb", (128, 2, T), BF16)
    vbf = sb("vbf", (128, TT), BF16)
    sq_sc = sb("sq_sc", (128, TT), BF16)
    rt1 = sb("rt1", (64, TT), F32)
    rt2 = sb("rt2", (64, TT), F32)
    rrow = sb("rrow", (1, 2, TT), F32)

    pW = [psf("p2_w0", (128, TT)), psf("p2_w1", (128, TT))]
    pL = [psf("p2_l0", (128, TT)), psf("p2_l1", (128, TT))]
    pU = [psf("p2_u0", (128, TT)), psf("p2_u1", (128, TT))]
    pS = psf("p2_ssq", (1, TT))
    pT = psb("p2_vt", (128, TT))

    mk = sems["mk"]
    wld = mk("p2wld", 16); tld = mk("p2tld", 16)
    umm = mk("p2umm"); uev = mk("p2uev")
    gmm = mk("p2gmm"); gsc = mk("p2gsc"); gad = mk("p2gad")
    ssqq = mk("p2sqq"); sstat = mk("p2stat"); sms = mk("p2ms")
    srst = mk("p2rst"); ssrow = mk("p2srow"); sscp = mk("p2scp")
    srope = mk("p2rope"); qst = mk("p2qst", 16)
    svtr = mk("p2vtr"); svcp = mk("p2vcp")

    em.emit("sync", lambda e: e.dma_start(out=cs_sb[:, :], in_=t["ropecs"][:, :]), inc=tld)
    em.emit("sync", lambda e: e.dma_start(out=ss_sb[:, :], in_=t["ropess"][:, :]), inc=tld)
    for g, (wi, tt) in enumerate([(wi, tt) for tt in range(NTT) for wi in range(3)]):
        pu = pU[g % 2]
        for c in range(16):
            em.emit("tensor", lambda e, c=c, wi=wi, tt=tt, pu=pu: e.matmul(
                pu[0:R, :], s["lora_a"][:, wi, c, :], ln1[:, c, tt * TT:(tt + 1) * TT],
                start=(c == 0), stop=(c == 15)),
                waits=([(uev, g - 1)] if g >= 2 and c == 0 else []),
                inc=(umm if c == 15 else None))
        em.emit("scalar", lambda e, wi=wi, tt=tt, pu=pu: e.copy(
            u_sb[:, wi, tt * TT:(tt + 1) * TT], pu[0:R, :]),
            waits=[(umm, g + 1)], inc=uev)

    heads = ([(0, h) for h in range(H)] + [(1, a) for a in range(KVH)]
             + [(2, a) for a in range(KVH)])
    wt_of = {0: t["wq_t"], 1: t["wk_t"], 2: t["wv_t"]}
    b_of = {0: s["bq_sb"], 1: s["bk_sb"], 2: s["bv_sb"]}
    g_glob = 0
    scp_g = 0
    vtr_g = 0
    qk_g = 0
    last_mm_for_wbuf = {}

    for hi, (role, h) in enumerate(heads):
        wbuf = wring[:, hi % 2]
        em.emit("gpsimd", lambda e, o=wbuf, i=wt_of[role][h]: e.dma_start(out=o, in_=i),
                waits=([(gmm, last_mm_for_wbuf[hi - 2])] if hi >= 2 else []), inc=wld)
        n_w = wld.n
        for j in range(NTT):
            g = g_glob; g_glob += 1
            pw, pl = pW[g % 2], pL[g % 2]
            jsl = slice(j * TT, (j + 1) * TT)
            waits0 = [(wld, n_w)]
            if g >= 2:
                waits0 += [(gsc, g - 1), (gad, g - 1)]
            if hi == 0 and j == 0:
                waits0 += [(uev, 12)]
            for c in range(16):
                em.emit("tensor", lambda e, c=c, pw=pw, w_=wbuf, jsl=jsl: e.matmul(
                    pw[:, :], w_[:, c, :], ln1[:, c, jsl],
                    start=(c == 0), stop=(c == 15)),
                    waits=(waits0 if c == 0 else []))
            em.emit("tensor", lambda e, pl=pl, role=role, h=h, jsl=jsl: e.matmul(
                pl[:, :], b_of[role][:, h, :], u_sb[:, role, jsl],
                start=True, stop=True), inc=gmm)
            em.emit("scalar", lambda e, pw=pw, role=role: e.activation(
                ev_f32[:, :], pw[:, :], AF.Copy, scale=s["scol"][:, role:role + 1]),
                waits=[(gmm, g + 1)] + ([(gad, g - 1)] if g >= 1 else []),
                inc=gsc)
            if role == 2:
                em.emit("vector", lambda e, pl=pl: e.tensor_tensor(
                    vbf[:, :], ev_f32[:, :], pl[:, :], ALU.add),
                    waits=[(gsc, g + 1)], inc=gad)
                vg = vtr_g; vtr_g += 1
                for ti in range(4):
                    em.emit("tensor", lambda e, ti=ti: e.transpose(
                        pT[:, ti * 128:(ti + 1) * 128], vbf[:, ti * 128:(ti + 1) * 128],
                        s["id_sb"][:, :]),
                        waits=(([(gad, g + 1)] + ([(svcp, vg)] if vg >= 1 else []))
                               if ti == 0 else []),
                        inc=(svtr if ti == 3 else None))
                em.emit("vector", lambda e, h=h, j=j: e.tensor_copy(
                    v_sb[:, 4 * j:4 * j + 4, h * 128:(h + 1) * 128], pT[:, :]),
                    waits=[(svtr, svtr.n)], inc=svcp)
            else:
                qg = qk_g; qk_g += 1
                em.emit("vector", lambda e, pl=pl: e.tensor_tensor(
                    qraw[:, :], ev_f32[:, :], pl[:, :], ALU.add),
                    waits=[(gsc, g + 1)], inc=gad)
                em.emit("scalar", lambda e: e.activation(
                    sq_sc[:, :], qraw[:, :], AF.Square),
                    waits=[(gad, g + 1)], inc=ssqq)
                em.emit("tensor", lambda e: e.matmul(
                    pS[0:1, :], s["ones_col"][:, :], sq_sc[:, :], start=True, stop=True),
                    waits=[(ssqq, ssqq.n), (sms, sms.n)], inc=sstat)
                em.emit("vector", lambda e: e.tensor_scalar(
                    rrow[:, 0, :], pS[0:1, :], 1.0 / 128.0, None, ALU.mult),
                    waits=[(sstat, sstat.n), (srst, srst.n)], inc=sms)
                em.emit("scalar", lambda e: e.activation(
                    rrow[:, 1, :], rrow[:, 0, :], AF.Ln, bias=s["cpool"][0:1, 1:2]),
                    waits=[(sms, sms.n)])
                em.emit("scalar", lambda e: e.activation(
                    rrow[:, 1, :], rrow[:, 1, :], AF.Exp, scale=s["cpool"][0:1, 2:3]),
                    inc=srst)
                if role == 0:
                    em.emit("vector", lambda e, h=h: e.tensor_scalar(
                        rrow[:, 0, :], rrow[:, 1, :], s["qgain_sb"][0:1, h:h + 1],
                        ISQ_HD, ALU.mult, ALU.mult),
                        waits=[(srst, srst.n), (sscp, sscp.n)], inc=ssrow)
                else:
                    em.emit("vector", lambda e: e.tensor_copy(rrow[:, 0, :], rrow[:, 1, :]),
                            waits=[(srst, srst.n), (sscp, sscp.n)], inc=ssrow)
                sg = scp_g; scp_g += 1
                pscp = pU[sg % 2]
                em.emit("tensor", lambda e, pscp=pscp: e.matmul(
                    pscp[:, :], s["ones_row"][:, :], rrow[:, 0, :], start=True, stop=True),
                    waits=[(ssrow, ssrow.n)] + ([(srope, sg - 1)] if sg >= 2 else []),
                    inc=sscp)
                em.emit("vector", lambda e, pscp=pscp: e.tensor_tensor(
                    qn_sb[:, :], qraw[:, :], pscp[:, :], ALU.mult),
                    waits=[(sscp, sscp.n)])
                if role == 0:
                    w_rope = [(qst, 16 * (h - 1))] if (h >= 2 and j == 0) else []
                    out64 = qf_hb[:, hi % 2][0:64, jsl]
                    outp_d, outp_s = qf_hb[:, hi % 2][64:128, jsl], None
                else:
                    w_rope = []
                    out64 = k_sb[0:64, h, jsl]
                    outp_d = k_sb[64:128, h, jsl]
                if qg == 0:
                    w_rope = w_rope + [(tld, 32)]
                em.emit("vector", lambda e, jsl=jsl: e.tensor_mul(
                    rt1[:, :], qn_sb[0:64, :], cs_sb[:, jsl]), waits=w_rope)
                em.emit("vector", lambda e, jsl=jsl: e.tensor_mul(
                    rt2[0:32, :], qn_sb[32:64, :], ss_sb[32:64, jsl]))
                em.emit("vector", lambda e, jsl=jsl: e.tensor_mul(
                    rt2[32:64, :], qn_sb[0:32, :], ss_sb[0:32, jsl]))
                em.emit("vector", lambda e, o=out64: e.tensor_tensor(
                    o, rt1[:, :], rt2[:, :], ALU.add))
                em.emit("vector", lambda e, o=outp_d: e.tensor_copy(
                    o, qn_sb[64:128, :]), inc=srope)
        last_mm_for_wbuf[hi] = gmm.n
        if role == 0:
            em.emit("sync", lambda e, h=h, hi=hi: e.dma_start(
                out=t["q_d"][h], in_=qf_hb[:, hi % 2]),
                waits=[(srope, srope.n)], inc=qst)

    if DEBUG:
        st = sems["dbg"]
        em.emit("sync", lambda e: e.dma_start(out=t["q_dbg"][:, :, :], in_=t["q_d"][:, :, :]),
                waits=[(qst, qst.n)], inc=st)
        for a in range(KVH):
            em.emit("sync", lambda e, a=a: e.dma_start(out=t["k_dbg"][a], in_=k_sb[:, a, :]),
                    waits=[(srope, srope.n)], inc=st)
        em.emit("sync", lambda e: e.dma_start(out=t["v_dbg"][:, :, :], in_=v_sb[:, :, :]),
                waits=[(svcp, svcp.n)], inc=st)
        em.wait("sync", st, st.n)


def ph3_attn(nc, t, s, em, sems, es, kv):
    sb = lambda n, shp, dt: es.enter_context(nc.sbuf_tensor(n, list(shp), dt))
    psf = lambda n, shp: es.enter_context(nc.psum_tensor(n, list(shp), F32))
    psb = lambda n, shp: es.enter_context(nc.psum_tensor(n, list(shp), BF16))

    k_sb, v_sb = kv["k_sb"], kv["v_sb"]
    qh = sb("p3_qh", (128, 4, T), BF16)
    oh = sb("p3_oh", (128, 4, T), BF16)
    probs = sb("p3_probs", (128, 2, T), BF16)
    pt_sb = sb("p3_pt", (128, 2, T), BF16)
    nmx = sb("p3_nmx", (128, 2), F32)
    ssum = sb("p3_ssum", (128, 2), F32)
    rc = sb("p3_rc", (128, 2), F32)

    NSC = 5
    scb = psf("p3_scb", (128, NSC, TT))
    scf = scb[:, :, :].rearrange("p a b -> p (a b)")
    tp = [psb(f"p3_tp{i}", (128, 1024)) for i in range(2)]
    op = psf("p3_op0", (128, 512))

    mk = sems["mk"]
    qld = [sems["mk"]("p3qld0", 16), sems["mk"]("p3qld1", 16)]
    smm = mk("p3mm"); smsk = mk("p3msk")
    smax = mk("p3max"); sexp = mk("p3exp")
    srcp = mk("p3rcp"); spn = mk("p3pn"); str_ = mk("p3tr")
    sptc = mk("p3ptc"); sav = mk("p3av"); soev = mk("p3oev")
    sost = [mk("p3ost0", 16), mk("p3ost1", 16)]
    p2qst = sems["p2qst"]

    slot_free = [0] * NSC
    last_tr = [0, 0]
    last_av = [0, 0]
    last_exp_w = [0, 0]
    tp_last = [0, 0]
    gtb = 0
    cur = 0

    for g in range(H // 2):
        kvh = g
        pp = (g % 2) * 2
        nq = [0, 0]
        for w in range(2):
            em.emit("sync", lambda e, w=w, g=g, pp=pp: e.dma_start(
                out=qh[:, pp + w], in_=t["q_d"][2 * g + w]),
                waits=[(p2qst, 256)] + ([(smm, smm.n)] if g >= 2 and w == 0 else []),
                inc=qld[g % 2])
        n_q = qld[g % 2].n
        for qt in range(16):
            cols = (qt + 1) * 128
            ngr = (cols + TT - 1) // TT
            s0s = [0, 0]

            def st_scores(w, qt=qt, cols=cols, ngr=ngr, kvh=kvh, pp=pp, n_q=n_q, gpar=g % 2):
                nonlocal cur
                if cur + ngr > NSC:
                    cur = 0
                s0 = cur; cur += ngr
                s0s[w] = s0
                for j in range(ngr):
                    ncol = min(TT, cols - j * TT)
                    em.emit("tensor", lambda e, j=j, sl=s0 + j, ncol=ncol, w=w,
                            qt=qt, kvh=kvh, pp=pp: e.matmul(
                        scb[:, sl, 0:ncol], qh[:, pp + w, qt * 128:(qt + 1) * 128],
                        k_sb[:, kvh, j * TT:j * TT + ncol], start=True, stop=True),
                        waits=[(qld[gpar], n_q), (sexp, slot_free[s0 + j])],
                        inc=(smm if j == ngr - 1 else None))

            def st_mask(w, qt=qt, cols=cols):
                s0 = s0s[w]
                dgr = (cols - 128) // TT
                dcol = (cols - 128) - dgr * TT
                em.emit("vector", lambda e, s0=s0, dgr=dgr, dcol=dcol: e.tensor_tensor(
                    scb[:, s0 + dgr, dcol:dcol + 128], scb[:, s0 + dgr, dcol:dcol + 128],
                    s["cm_sb"][:, :], ALU.add),
                    waits=[(smm, smm.n)], inc=smsk)

            def st_max(w, cols=cols):
                s0 = s0s[w]
                em.emit("vector", lambda e, s0=s0, cols=cols, w=w: e.reduce_max(
                    nmx[:, w:w + 1], scf[:, s0 * TT:s0 * TT + cols], axis=AX.X,
                    negate=True),
                    waits=[(smsk, smsk.n), (sexp, last_exp_w[w])], inc=smax)

            def st_exp(w, cols=cols, ngr=ngr):
                s0 = s0s[w]
                em.emit("scalar", lambda e, s0=s0, cols=cols, w=w: e.activation(
                    probs[:, w, 0:cols], scf[:, s0 * TT:s0 * TT + cols], AF.Exp,
                    bias=nmx[:, w:w + 1], accum_out=ssum[:, w:w + 1]),
                    waits=[(smax, smax.n), (str_, last_tr[w])], inc=sexp)
                for j in range(ngr):
                    slot_free[s0 + j] = sexp.n
                last_exp_w[w] = sexp.n

            def st_norm(w, cols=cols):
                em.emit("vector", lambda e, w=w: e.reciprocal(
                    rc[:, w:w + 1], ssum[:, w:w + 1]),
                    waits=[(sexp, sexp.n)], inc=srcp)
                em.emit("vector", lambda e, cols=cols, w=w: e.tensor_scalar(
                    probs[:, w, 0:cols], probs[:, w, 0:cols], rc[:, w:w + 1],
                    None, ALU.mult),
                    waits=[(srcp, srcp.n)], inc=spn)

            def st_tr(w, qt=qt):
                nonlocal gtb
                ntb = (qt + 4) // 4
                for tb in range(ntb):
                    kts = range(tb * 4, min(tb * 4 + 4, qt + 1))
                    bi = gtb % 2; gtb += 1
                    tpb = tp[bi]
                    for i, kt in enumerate(kts):
                        wtr = []
                        if i == 0:
                            wtr = [(spn, spn.n), (sptc, tp_last[bi])]
                            if tb == 0:
                                wtr.append((sav, last_av[w]))
                        em.emit("tensor", lambda e, i=i, kt=kt, tpb=tpb, w=w: e.transpose(
                            tpb[:, i * 128:(i + 1) * 128],
                            probs[:, w, kt * 128:(kt + 1) * 128], s["id_sb"][:, :]),
                            waits=wtr, inc=(str_ if i == len(kts) - 1 else None))
                    em.emit("vector", lambda e, tpb=tpb, nk=len(kts), tb=tb, w=w: e.tensor_copy(
                        pt_sb[:, w, tb * TT:tb * TT + nk * 128], tpb[:, 0:nk * 128]),
                        waits=[(str_, str_.n)], inc=sptc)
                    tp_last[bi] = sptc.n
                last_tr[w] = str_.n

            def st_av(w, qt=qt, kvh=kvh):
                for kt in range(qt + 1):
                    w_av = []
                    if kt == 0:
                        w_av = [(sptc, sptc.n), (soev, soev.n)]
                    em.emit("tensor", lambda e, kt=kt, kvh=kvh, qt=qt, w=w: e.matmul(
                        op[:, w * 128:(w + 1) * 128],
                        v_sb[:, kt, kvh * 128:(kvh + 1) * 128],
                        pt_sb[:, w, kt * 128:(kt + 1) * 128],
                        start=(kt == 0), stop=(kt == qt)),
                        waits=w_av, inc=(sav if kt == qt else None))
                last_av[w] = sav.n

            def st_evac(w, qt=qt, pp=pp, gpar=g % 2, g_=g):
                em.emit("scalar", lambda e, qt=qt, w=w, pp=pp: e.activation(
                    oh[:, pp + w][:, qt * 128:(qt + 1) * 128],
                    op[:, w * 128:(w + 1) * 128], AF.Copy),
                    waits=[(sav, sav.n)] +
                          ([(sost[gpar], sost[gpar].n)] if (g_ >= 2 and qt == 0 and w == 0)
                           else []),
                    inc=soev)

            if PH3_INTER and 2 * ngr <= NSC:
                for st in (st_scores, st_mask, st_max, st_exp, st_norm, st_tr):
                    st(0); st(1)
                st_av(0); st_evac(0); st_av(1); st_evac(1)
            else:
                for w in range(2):
                    for st in (st_scores, st_mask, st_max, st_exp, st_norm, st_tr):
                        st(w)
                st_av(0); st_evac(0); st_av(1); st_evac(1)
        for w in range(2):
            em.emit("sync", lambda e, g=g, w=w, pp=pp: e.dma_start(
                out=t["o_d"][2 * g + w], in_=oh[:, pp + w]),
                waits=[(soev, soev.n - (1 - w))], inc=sost[g % 2])

    if DEBUG:
        st = sems["dbg"]
        em.emit("sync", lambda e: e.dma_start(out=t["o_dbg"][:, :, :], in_=t["o_d"][:, :, :]),
                waits=[(sost[0], sost[0].n), (sost[1], sost[1].n)], inc=st)
        em.wait("sync", st, st.n)


def emit_ln_stats(em, s, src_bf, sqb, s1p, s2p, ap_, bp_, rows, S,
                  w_src, w_psum_war, w_bc_war):
    em.emit("scalar", lambda e: e.activation(sqb[:, :, :], src_bf[:, :, :], AF.Square),
            waits=list(w_src), inc=S["sq"])
    for c in range(16):
        em.emit("tensor", lambda e, c=c: e.matmul(
            s1p[0:1, :], s["ones_col"][:, :], src_bf[:, c, :],
            start=(c == 0), stop=(c == 15)),
            waits=([(S["sq"], S["sq"].n)] + list(w_psum_war)) if c == 0 else [])
    for c in range(16):
        em.emit("tensor", lambda e, c=c: e.matmul(
            s2p[0:1, :], s["ones_col"][:, :], sqb[:, c, :],
            start=(c == 0), stop=(c == 15)),
            inc=(S["stat"] if c == 15 else None))
    em.emit("vector", lambda e: e.tensor_scalar(
        rows[:, 0, :], s1p[0:1, :], 1.0 / D, None, ALU.mult),
        waits=[(S["stat"], S["stat"].n), (S["rstd"], S["rstd"].n)])
    em.emit("vector", lambda e: e.tensor_mul(rows[:, 1, :], rows[:, 0, :], rows[:, 0, :]))
    em.emit("vector", lambda e: e.scalar_tensor_tensor(
        rows[:, 2, :], s2p[0:1, :], 1.0 / D, rows[:, 1, :],
        ALU.mult, ALU.subtract), inc=S["var"])
    em.emit("scalar", lambda e: e.activation(
        rows[:, 3, :], rows[:, 2, :], AF.Ln, bias=s["cpool"][0:1, 0:1]),
        waits=[(S["var"], S["var"].n)])
    em.emit("scalar", lambda e: e.activation(
        rows[:, 3, :], rows[:, 3, :], AF.Exp, scale=s["cpool"][0:1, 2:3]),
        inc=S["rstd"])
    em.emit("vector", lambda e: e.tensor_mul(
        rows[:, 4, :], rows[:, 0, :], rows[:, 3, :]),
        waits=[(S["rstd"], S["rstd"].n)], inc=S["nmr"])
    em.emit("tensor", lambda e: e.matmul(
        ap_[:, :], s["ones_row"][:, :], rows[:, 3, :], start=True, stop=True),
        waits=[(S["nmr"], S["nmr"].n)] + list(w_bc_war))
    em.emit("tensor", lambda e: e.matmul(
        bp_[:, :], s["ones_row"][:, :], rows[:, 4, :], start=True, stop=True),
        inc=S["bc"])
    return S["bc"].n


def ph4_woln(nc, t, s, em, sems, es):
    sb = lambda n, shp, dt: es.enter_context(nc.sbuf_tensor(n, list(shp), dt))
    psf = lambda n, shp: es.enter_context(nc.psum_tensor(n, list(shp), F32))

    o_sb = sb("p4_o", (128, 2, 16, TT), BF16)
    wob = sb("p4_wob", (128, 2, 16, 128), BF16)
    uo_sb = sb("p4_uo", (R, TT), BF16)
    ev4 = sb("p4_ev", (128, TT), F32)
    ao_sb = sb("p4_ao", (128, 16, TT), F32)
    abf = sb("p4_abf", (128, 16, TT), BF16)
    sq4 = sb("p4_sq", (128, 16, TT), BF16)
    xf = sb("p4_x", (128, 16, TT), F32)
    ln2t = sb("p4_ln2", (128, 16, TT), BF16)
    rows = sb("p4_rows", (1, 5, TT), F32)
    tmp = sb("p4_tmp", (128, TT), F32)
    l2f = sb("p4_l2f", (128, 2, TT), F32)
    glit = sb("p4_glit", (16, TT), F32)

    pwW = [psf("p4_w0", (128, TT)), psf("p4_w1", (128, TT))]
    pwL = psf("p4_l", (128, TT))
    pu = psf("p4_u", (R, TT))
    s1p = psf("p4_s1", (1, TT)); s2p = psf("p4_s2", (1, TT))
    ap_ = psf("p4_a", (128, TT)); bp_ = psf("p4_b", (128, TT))

    mk = sems["mk"]
    sold = mk("p4old", 16); swo = mk("p4wo", 16); sxl = mk("p4xl", 16)
    su4 = mk("p4umm"); sue4 = mk("p4uev")
    sm4 = mk("p4mm"); sc4 = mk("p4sc"); sd4 = mk("p4ad")
    SA = sems["LS"]
    SB = sems["LS"]
    sxa = mk("p4xa"); scast2 = mk("p4c2"); sla = mk("p4la")
    sx2st = mk("p4x2st", 16); sln2st = mk("p4l2st", 16)
    sgmm = mk("p4gmm"); sgc = mk("p4gc"); sgev = mk("p4gev"); sgst = mk("p4gst", 16)
    slf = mk("p4lf"); slc = mk("p4lc")
    p3ost0 = sems["p3ost0"]; p3ost1 = sems["p3ost1"]

    for tt in range(NTT):
        ob = o_sb[:, tt % 2]
        jsl = slice(tt * TT, (tt + 1) * TT)
        for hh in range(16):
            em.emit("sync", lambda e, hh=hh, tt=tt, ob=ob: e.dma_start(
                out=ob[:, hh, :], in_=t["o_d"][hh][:, tt * TT:(tt + 1) * TT]),
                waits=(([(p3ost0, 128), (p3ost1, 128)] if tt == 0 else []) +
                       ([(sm4, sm4.n)] if (tt >= 2 and hh == 0) else [])),
                inc=sold)
        n_o = sold.n
        em.emit("sync", lambda e, tt=tt: e.dma_start(out=xf[:, :, :], in_=t["x_t"][tt]),
                waits=[(sx2st, sx2st.n), (sla, sla.n)], inc=sxl)
        n_x = sxl.n
        for c in range(16):
            em.emit("tensor", lambda e, c=c, ob=ob: e.matmul(
                pu[0:R, :], s["lora_a"][:, 3, c, :], ob[:, c, :],
                start=(c == 0), stop=(c == 15)),
                waits=([(sold, n_o)] + ([(sue4, sue4.n), (sgev, sgev.n)] if tt >= 1 else []))
                if c == 0 else [],
                inc=(su4 if c == 15 else None))
        em.emit("scalar", lambda e: e.copy(uo_sb[:, :], pu[0:R, :]),
                waits=[(su4, su4.n)], inc=sue4)
        for oc in range(16):
            wb = wob[:, oc % 2]
            em.emit("gpsimd", lambda e, oc=oc, wb=wb: e.dma_start(
                out=wb, in_=t["wo_t"][oc]),
                waits=([(sm4, sm4.n - 1)] if (tt * 16 + oc) >= 2 else []),
                inc=swo)
            n_w = swo.n
            g = tt * 16 + oc
            pw = pwW[g % 2]
            for c in range(16):
                em.emit("tensor", lambda e, c=c, pw=pw, wb=wb, ob=ob: e.matmul(
                    pw[:, :], wb[:, c, :], ob[:, c, :],
                    start=(c == 0), stop=(c == 15)),
                    waits=([(swo, n_w)] + ([(sc4, g - 1)] if g >= 2 else []))
                    if c == 0 else [])
            em.emit("tensor", lambda e, oc=oc: e.matmul(
                pwL[:, :], s["bo_sb"][:, oc, :], uo_sb[:, :], start=True, stop=True),
                waits=[(sue4, sue4.n)] + ([(sd4, g)] if g >= 1 else []),
                inc=sm4)
            em.emit("scalar", lambda e, pw=pw: e.activation(
                ev4[:, :], pw[:, :], AF.Copy, scale=s["scol"][:, 3:4]),
                waits=[(sm4, sm4.n)] + ([(sd4, g)] if g >= 1 else []),
                inc=sc4)
            em.emit("vector", lambda e, oc=oc: e.tensor_tensor(
                ao_sb[:, oc, :], ev4[:, :], pwL[:, :], ALU.add),
                waits=[(sc4, sc4.n)], inc=sd4)
        em.emit("scalar", lambda e: e.activation(abf[:, :, :], ao_sb[:, :, :], AF.Copy),
                waits=[(sd4, sd4.n)], inc=SA["sq"])
        emit_ln_stats(em, s, abf, sq4, s1p, s2p, ap_, bp_, rows, SA,
                      w_src=[],
                      w_psum_war=([(SB["var"], SB["var"].n)] if tt >= 1 else []),
                      w_bc_war=([(sla, sla.n)] if tt >= 1 else []))
        for c in range(16):
            em.emit("vector", lambda e, c=c: e.tensor_mul(tmp[:, :], ao_sb[:, c, :], ap_[:, :]),
                    waits=([(SA["bc"], SA["bc"].n), (sxl, n_x)] if c == 0 else []))
            em.emit("vector", lambda e, c=c: e.tensor_tensor(
                tmp[:, :], tmp[:, :], bp_[:, :], ALU.subtract))
            em.emit("vector", lambda e, c=c: e.tensor_tensor(
                xf[:, c, :], xf[:, c, :], tmp[:, :], ALU.add),
                inc=(sxa if c == 15 else None))
        em.emit("sync", lambda e, tt=tt: e.dma_start(out=t["x2_d"][tt], in_=xf[:, :, :]),
                waits=[(sxa, sxa.n)], inc=sx2st)
        em.emit("scalar", lambda e: e.activation(abf[:, :, :], xf[:, :, :], AF.Copy),
                waits=[(sxa, sxa.n)], inc=scast2)
        emit_ln_stats(em, s, abf, sq4, s1p, s2p, ap_, bp_, rows, SB,
                      w_src=[(scast2, scast2.n)],
                      w_psum_war=[(SA["var"], SA["var"].n)],
                      w_bc_war=[(sxa, sxa.n)])
        for c in range(16):
            lf = l2f[:, c % 2]
            em.emit("vector", lambda e, c=c: e.tensor_mul(tmp[:, :], xf[:, c, :], ap_[:, :]),
                    waits=([(SB["bc"], SB["bc"].n), (sln2st, sln2st.n)] if c == 0 else []))
            g4 = tt * 16 + c
            em.emit("vector", lambda e, c=c, lf=lf: e.tensor_tensor(
                lf[:, :], tmp[:, :], bp_[:, :], ALU.subtract),
                waits=([(sgc, g4 - 1)] if g4 >= 2 else []),
                inc=slf)
            em.emit("scalar", lambda e, c=c, lf=lf: e.activation(
                ln2t[:, c, :], lf[:, :], AF.Copy),
                waits=[(slf, slf.n)] + ([(sln2st, sln2st.n)] if c == 0 else []),
                inc=(sla if c == 15 else slc))
            em.emit("tensor", lambda e, c=c, lf=lf: e.matmul(
                pu[0:16, :], s["gate_sb"][:, c, :], lf[:, :],
                start=(c == 0), stop=(c == 15)),
                waits=[(slf, slf.n)] + ([(sue4, sue4.n), (sgev, sgev.n)] if c == 0 else []),
                inc=sgc)
        em.emit("scalar", lambda e: e.copy(glit[:, :], pu[0:16, :]),
                waits=[(sgc, sgc.n)] + ([(sgst, sgst.n)] if tt >= 1 else []),
                inc=sgev)
        em.emit("sync", lambda e, tt=tt: e.dma_start(out=t["glog_d"][tt], in_=glit[:, :]),
                waits=[(sgev, sgev.n)], inc=sgst)
        em.emit("sync", lambda e, tt=tt: e.dma_start(out=t["ln2_d"][tt], in_=ln2t[:, :, :]),
                waits=[(sla, sla.n)], inc=sln2st)

    if DEBUG:
        st = sems["dbg"]
        for tt in range(NTT):
            em.emit("sync", lambda e, tt=tt: e.dma_start(
                out=t["x2_dbg"][tt], in_=t["x2_d"][tt]), waits=[(sx2st, sx2st.n)], inc=st)
            em.emit("sync", lambda e, tt=tt: e.dma_start(
                out=t["ln2_dbg"][tt], in_=t["ln2_d"][tt]), waits=[(sln2st, sln2st.n)], inc=st)
        em.wait("sync", st, st.n)


def ph5_ffn(nc, t, s, em, sems, es):
    sb = lambda n, shp, dt: es.enter_context(nc.sbuf_tensor(n, list(shp), dt))
    psf = lambda n, shp: es.enter_context(nc.psum_tensor(n, list(shp), F32))
    TH = 2 * TT

    ln2h = sb("p5_ln2h", (128, 16, TH), BF16)
    y_acc = sb("p5_y", (128, 16, TH), F32)
    h_sb = sb("p5_h", (128, MCH, TH), BF16)
    w13 = sb("p5_w13", (128, 3, 2, 16, 128), BF16)
    w2b = sb("p5_w2", (128, 3, MCH, 128), BF16)
    oneh = sb("p5_oneh", (16, NE, 128), F32)
    glog = sb("p5_glog", (16, 2, TT), F32)
    gtok = sb("p5_gtok", (128, 8, NE), F32)
    gw_tok = sb("p5_gw", (128, 8, NE), F32)
    wfm = sb("p5_wfm", (16, 2, TT), F32)
    grow = sb("p5_grow", (128, 4), F32)
    gl2 = sb("p5_gl2", (128, NE), F32)
    sil_sb = sb("p5_sil", (128, 2, TT), BF16)
    tmp5 = sb("p5_tmp", (128, TT), F32)
    sq5 = sb("p5_sq", (128, 16, TT), BF16)
    rows = sb("p5_rows", (1, 5, TT), F32)
    xc = sb("p5_xc", (128, 1, TT), F32)
    outc = sb("p5_outc", (128, 1, TT), F32)

    hp = [psf("p5_hA", (128, TT)), psf("p5_hB", (128, TT))]
    hp3 = [psf("p5_hC", (128, TT)), psf("p5_hD", (128, TT))]
    yp = [psf("p5_yE", (128, TT)), psf("p5_yF", (128, TT))]
    gp = [psf("p5_gG", (128, TT)), psf("p5_gH", (128, TT))]

    mk = sems["mk"]
    sl2l = mk("p5l2l", 16); sgl = mk("p5gl", 16); sw13 = mk("p5w13", 16)
    sw2l = mk("p5w2l", 16); sxr = mk("p5xr", 16); sout = mk("p5out", 16)
    sgt = mk("p5gt"); sgtc = mk("p5gtc"); sgm1 = mk("p5gm1")
    sgm2 = mk("p5gm2"); sgnm = mk("p5gnm"); sgex = mk("p5gex")
    sga = mk("p5ga"); sgb = mk("p5gb"); sgc5 = mk("p5gc5"); sgd = mk("p5gd")
    sgrc = mk("p5grc"); sgw = mk("p5gw"); sgwt = mk("p5gwt"); swfm = mk("p5wfm")
    sbc = mk("p5bc"); sh5 = mk("p5hmm"); ssil = mk("p5sil")
    she = mk("p5he"); sy5 = mk("p5ymm"); syd = mk("p5yd")
    SC = sems["LS"]
    sfa = mk("p5fa")
    sol = mk("p5onl", 16)
    p4l2 = sems["p4l2st"]; p4gst = sems["p4gst"]; p4x2 = sems["p4x2st"]

    em.emit("sync", lambda e: e.dma_start(out=oneh[:, :, :], in_=t["onehot"][:, :, :]),
            inc=sol)
    n_oneh = sol.n

    for hf in range(2):
        for u in range(2):
            tt = 2 * hf + u
            em.emit("sync", lambda e, tt=tt, u=u: e.dma_start(
                out=ln2h[:, :, u * TT:(u + 1) * TT], in_=t["ln2_d"][tt]),
                waits=[(p4l2, 64)] + ([(sh5, sh5.n)] if hf >= 1 and u == 0 else []),
                inc=sl2l)
            em.emit("sync", lambda e, tt=tt, u=u: e.dma_start(
                out=glog[:, u, :], in_=t["glog_d"][tt]),
                waits=[(p4gst, 64)] + ([(sgt, sgt.n)] if hf >= 1 and u == 0 else []),
                inc=sgl)
        n_l2 = sl2l.n
        n_gl = sgl.n
        for sl in range(8):
            u, sli = sl // 4, sl % 4
            wtr = [(sgl, n_gl)]
            if (hf, sl) != (0, 0):
                wtr.append((sgtc, sgtc.n))
            if hf >= 1 and sl == 0:
                wtr.append((SC["var"], SC["var"].n))
            em.emit("tensor", lambda e, u=u, sli=sli: e.transpose(
                hp[0][:, 0:16], glog[:, u, sli * 128:(sli + 1) * 128],
                s["idf_sb"][0:16, 0:16]),
                waits=wtr, inc=sgt)
            em.emit("vector", lambda e, sl=sl: e.tensor_copy(
                gtok[:, sl, :], hp[0][:, 0:16]),
                waits=[(sgt, sgt.n)] +
                      ([(sgex, sgex.n)] if hf >= 1 and sl == 0 else []),
                inc=sgtc)
        for sl in range(8):
            em.emit("vector", lambda e, sl=sl: e.reduce_max(
                grow[:, 0:1], gtok[:, sl, :], axis=AX.X), inc=sgm1)
            em.emit("vector", lambda e, sl=sl: e.tensor_scalar(
                gl2[:, :], gtok[:, sl, :], grow[:, 0:1], None, ALU.is_ge),
                waits=[(sgm1, sgm1.n)], inc=sga)
            em.emit("vector", lambda e, sl=sl: e.scalar_tensor_tensor(
                gl2[:, :], gl2[:, :], NEG_BIG, gtok[:, sl, :], ALU.mult, ALU.add),
                waits=[(sga, sga.n)], inc=sgb)
            em.emit("vector", lambda e: e.reduce_max(
                grow[:, 1:2], gl2[:, :], axis=AX.X),
                waits=[(sgb, sgb.n)], inc=sgm2)
            em.emit("vector", lambda e, sl=sl: e.tensor_scalar(
                gl2[:, :], gtok[:, sl, :], grow[:, 1:2], None, ALU.is_ge),
                waits=[(sgm2, sgm2.n)], inc=sgc5)
            em.emit("vector", lambda e: e.tensor_scalar(
                grow[:, 0:1], grow[:, 0:1], -1.0, None, ALU.mult), inc=sgnm)
            em.emit("scalar", lambda e, sl=sl: e.activation(
                gw_tok[:, sl, :], gtok[:, sl, :], AF.Exp, bias=grow[:, 0:1],
                accum_out=grow[:, 2:3]),
                waits=[(sgnm, sgnm.n)], inc=sgex)
            em.emit("vector", lambda e: e.reciprocal(grow[:, 3:4], grow[:, 2:3]),
                    waits=[(sgex, sgex.n)], inc=sgrc)
            em.emit("vector", lambda e, sl=sl: e.tensor_scalar(
                gw_tok[:, sl, :], gw_tok[:, sl, :], grow[:, 3:4], None, ALU.mult),
                waits=[(sgrc, sgrc.n)], inc=sgd)
            em.emit("vector", lambda e, sl=sl: e.tensor_mul(
                gw_tok[:, sl, :], gw_tok[:, sl, :], gl2[:, :]),
                waits=[(sgd, sgd.n), (sgc5, sgc5.n)], inc=sgw)
        for sl in range(8):
            u, sli = sl // 4, sl % 4
            wtr = [(sgw, sgw.n)]
            if (hf, sl) != (0, 0):
                wtr.append((swfm, swfm.n))
            if hf >= 1 and sl == 0:
                wtr.append((SC["var"], SC["var"].n))
            em.emit("tensor", lambda e, sl=sl: e.transpose(
                hp[1][0:16, 0:128], gw_tok[:, sl, :], s["idf_sb"][:, :]),
                waits=wtr, inc=sgwt)
            em.emit("vector", lambda e, u=u, sli=sli: e.tensor_copy(
                wfm[0:16, u, sli * 128:(sli + 1) * 128], hp[1][0:16, 0:128]),
                waits=[(sgwt, sgwt.n)], inc=swfm)
        for e_ in range(PH5_NEXP):
            routed = e_ < NE
            if routed:
                for u in range(2):
                    em.emit("tensor", lambda e, e_=e_, u=u: e.matmul(
                        gp[u][:, :], oneh[:, e_, :], wfm[:, u, :],
                        start=True, stop=True),
                        waits=([(sol, n_oneh), (swfm, swfm.n), (she, she.n)]
                               if u == 0 else []),
                        inc=sbc)
            for m in range(MCH):
                wg = (hf * (NE + 2) + e_) * MCH + m
                r = wg % 3
                if routed:
                    src1, src3 = t["w1_t"][e_, m], t["w3_t"][e_, m]
                else:
                    src1 = t["sw1_t"][(e_ - NE) * MCH + m]
                    src3 = t["sw3_t"][(e_ - NE) * MCH + m]
                em.emit("gpsimd", lambda e, o=w13[:, r, 0], i=src1: e.dma_start(out=o, in_=i),
                        waits=([(sh5, 2 * (wg - 2))] if wg >= 3 else []), inc=sw13)
                em.emit("gpsimd", lambda e, o=w13[:, r, 1], i=src3: e.dma_start(out=o, in_=i),
                        inc=sw13)
                n_w = sw13.n
                for u in range(2):
                    idx = m * 2 + u
                    h1, h3 = hp[idx % 2], hp3[idx % 2]
                    usl = slice(u * TT, (u + 1) * TT)
                    w0 = [(sw13, n_w), (sl2l, n_l2)]
                    if ssil.n >= 2:
                        w0.append((ssil, ssil.n - 1))
                    if she.n >= 2:
                        w0.append((she, she.n - 1))
                    if e_ == 0 and m == 0 and u == 0 and hf >= 1:
                        w0.append((sfa, sfa.n))
                    if e_ == 0 and m == 0 and u == 0:
                        w0.append((sgtc, sgtc.n))
                    if e_ == 0 and m == 0 and u == 1:
                        w0.append((swfm, swfm.n))
                    for c in range(16):
                        em.emit("tensor", lambda e, c=c, h1=h1, r=r, usl=usl: e.matmul(
                            h1[:, :], w13[:, r, 0, c, :], ln2h[:, c, usl],
                            start=(c == 0), stop=(c == 15)),
                            waits=(w0 if c == 0 else []))
                    for c in range(16):
                        em.emit("tensor", lambda e, c=c, h3=h3, r=r, usl=usl: e.matmul(
                            h3[:, :], w13[:, r, 1, c, :], ln2h[:, c, usl],
                            start=(c == 0), stop=(c == 15)),
                            inc=(sh5 if c == 15 else None))
                    sil = sil_sb[:, idx % 2]
                    em.emit("scalar", lambda e, h1=h1, sil=sil: e.activation(
                        sil[:, :], h1[:, :], AF.Silu),
                        waits=[(sh5, sh5.n)] + ([(she, she.n - 1)] if she.n >= 2 else []),
                        inc=ssil)
                    wh = [(ssil, ssil.n)]
                    if m == 0 and u == 0:
                        wh.append((sy5, sy5.n))
                        if routed:
                            wh.append((sbc, sbc.n))
                    if routed:
                        em.emit("vector", lambda e, h3=h3, sil=sil: e.tensor_tensor(
                            tmp5[:, :], sil[:, :], h3[:, :], ALU.mult), waits=wh)
                        em.emit("vector", lambda e, m=m, u=u, usl=usl: e.tensor_tensor(
                            h_sb[:, m, usl], tmp5[:, :], gp[u][:, :], ALU.mult),
                            inc=she)
                    else:
                        em.emit("vector", lambda e, h3=h3, m=m, usl=usl, sil=sil: e.tensor_tensor(
                            h_sb[:, m, usl], sil[:, :], h3[:, :], ALU.mult),
                            waits=wh, inc=she)
            for dc in range(16):
                wg2 = (hf * (NE + 2) + e_) * 16 + dc
                r2 = wg2 % 3
                src2 = t["w2_t"][e_, dc] if routed else t["sw2_t"][dc, e_ - NE]
                em.emit("gpsimd", lambda e, o=w2b[:, r2], i=src2: e.dma_start(out=o, in_=i),
                        waits=([(sy5, 2 * (wg2 - 2))] if wg2 >= 3 else []), inc=sw2l)
                n_w2 = sw2l.n
                for u in range(2):
                    idx2 = dc * 2 + u
                    ypb = yp[idx2 % 2]
                    usl = slice(u * TT, (u + 1) * TT)
                    wv = [(sw2l, n_w2)]
                    if dc == 0 and u == 0:
                        wv.append((she, she.n))
                    if syd.n >= 2:
                        wv.append((syd, syd.n - 1))
                    for m in range(MCH):
                        em.emit("tensor", lambda e, m=m, ypb=ypb, r2=r2, usl=usl: e.matmul(
                            ypb[:, :], w2b[:, r2, m, :], h_sb[:, m, usl],
                            start=(m == 0), stop=(m == MCH - 1)),
                            waits=(wv if m == 0 else []),
                            inc=(sy5 if m == MCH - 1 else None))
                    wy = [(sy5, sy5.n)]
                    if e_ == 0 and dc == 0 and u == 0 and hf >= 1:
                        wy.append((sfa, sfa.n))
                    if e_ == 0:
                        em.emit("vector", lambda e, dc=dc, ypb=ypb, usl=usl: e.tensor_copy(
                            y_acc[:, dc, usl], ypb[:, :]), waits=wy, inc=syd)
                    else:
                        em.emit("vector", lambda e, dc=dc, ypb=ypb, usl=usl: e.tensor_tensor(
                            y_acc[:, dc, usl], y_acc[:, dc, usl], ypb[:, :], ALU.add),
                            waits=wy, inc=syd)
        if not PH5_FINAL:
            continue
        ybf = h_sb[:, :, :].rearrange("p a b -> p (a b)")[:, 0:16 * TT].rearrange(
            "p (c s) -> p c s", c=16)
        for u in range(2):
            tt = 2 * hf + u
            usl = slice(u * TT, (u + 1) * TT)
            em.emit("scalar", lambda e, usl=usl, ybf=ybf: e.activation(
                ybf[:, :, :], y_acc[:, :, usl], AF.Copy),
                waits=[(syd, syd.n), (sy5, sy5.n)], inc=SC["sq"])
            emit_ln_stats(em, s, ybf, sq5, hp[0][0:1, :], hp[1][0:1, :],
                          hp3[0], hp3[1], rows, SC,
                          w_src=[],
                          w_psum_war=[(she, she.n), (SC["var"], SC["var"].n)],
                          w_bc_war=[(sfa, sfa.n)])
            for c in range(16):
                xcb = xc[:, 0]
                ocb = outc[:, 0]
                gc = (hf * 2 + u) * 16 + c
                if PH5_RES:
                    em.emit("sync", lambda e, tt=tt, c=c, xcb=xcb: e.dma_start(
                        out=xcb, in_=t["x2_d"][tt][:, c, :]),
                        waits=[(p4x2, 64)] + ([(sfa, gc)] if gc >= 1 else []),
                        inc=sxr)
                n_xc = sxr.n
                em.emit("vector", lambda e, c=c, usl=usl: e.tensor_mul(
                    tmp5[:, :], y_acc[:, c, usl], hp3[0][:, :]),
                    waits=([(SC["bc"], SC["bc"].n)] if c == 0 else []))
                em.emit("vector", lambda e: e.tensor_tensor(
                    tmp5[:, :], tmp5[:, :], hp3[1][:, :], ALU.subtract))
                if PH5_RES:
                    em.emit("vector", lambda e, xcb=xcb, ocb=ocb, gc=gc: e.tensor_tensor(
                        ocb, xcb, tmp5[:, :], ALU.add),
                        waits=[(sxr, n_xc)] + ([(sout, 16 * gc)] if gc >= 1 else []),
                        inc=sfa)
                else:
                    em.emit("vector", lambda e, ocb=ocb, gc=gc: e.tensor_copy(
                        ocb, tmp5[:, :]),
                        waits=([(sout, 16 * gc)] if gc >= 1 else []),
                        inc=sfa)
                em.emit("sync", lambda e, tt=tt, c=c, ocb=ocb: e.dma_start(
                    out=t["out_t"][tt][:, c, :], in_=ocb),
                    waits=[(sfa, sfa.n)], inc=sout)
    em.wait("sync", sout, sout.n)
    em.wait("gpsimd", sout, sout.n)


BUILD_PHASES = ("ph1", "ph2", "ph3", "ph4", "ph5")


RACE_DET = True


def build_nc():
    nc = bass.Bass("TRN2", target_bir_lowering=False, detect_race_conditions=RACE_DET)
    t = declare_ios(nc)
    es = contextlib.ExitStack()
    with es:
        s = alloc_persistent(nc, es)
        es_lora = contextlib.ExitStack()
        for nm, shp in (("lora_a", [128, 4, 16, R]), ("bq_sb", [R, 16, 128]),
                        ("bk_sb", [R, 8, 128]), ("bv_sb", [R, 8, 128]),
                        ("bo_sb", [R, 16, 128])):
            s[nm] = es_lora.enter_context(nc.sbuf_tensor(nm, shp, BF16, side="right"))
        sems = {}
        for nm, step in (("ld0", 16), ("ld0s", 16), ("ev0", 1), ("xld", 16), ("ph1", 1),
                         ("dbg", 16), ("fin", 16)):
            sems[nm] = Sem(es, nc, nm, step)
        def _mk(nm, step=1):
            sems[nm] = Sem(es, nc, nm, step)
            return sems[nm]
        sems["mk"] = _mk
        sems["LS"] = {k: _mk("L" + k) for k in
                      ("sq", "stat", "var", "rstd", "nmr", "bc")}

        es0 = contextlib.ExitStack()
        with es0:
            em0 = Em()
            with nc.Block("ph0") as blk:
                ph0_setup(nc, t, s, em0, sems, es0)
                em0.install(blk)

        kv = {}
        es_kv = contextlib.ExitStack()
        with es_kv:
            if "ph2" in BUILD_PHASES:
                kv["k_sb"] = es_kv.enter_context(
                    nc.sbuf_tensor("k_sb", [128, KVH, T], BF16, side="right"))
                kv["v_sb"] = es_kv.enter_context(
                    nc.sbuf_tensor("v_sb", [128, 16, KVH * 128], BF16, side="right"))
            es_ln1 = contextlib.ExitStack()
            with es_ln1:
                if "ph1" in BUILD_PHASES:
                    em1 = Em()
                    es1loc = contextlib.ExitStack()
                    with es1loc:
                        with nc.Block("ph1") as blk:
                            ph1_ln1(nc, t, s, em1, sems, es_ln1, es1loc)
                            ln1 = es_ln1._ln1
                            em1.install(blk)
                if "ph2" in BUILD_PHASES:
                    es2 = contextlib.ExitStack()
                    with es2:
                        em2 = Em()
                        with nc.Block("ph2") as blk:
                            ph2_qkv(nc, t, s, em2, sems, es2, ln1, kv)
                            em2.install(blk)
            if "ph3" in BUILD_PHASES:
                es3 = contextlib.ExitStack()
                with es3:
                    em3 = Em()
                    with nc.Block("ph3") as blk:
                        ph3_attn(nc, t, s, em3, sems, es3, kv)
                        em3.install(blk)
        if "ph4" in BUILD_PHASES:
            es4 = contextlib.ExitStack()
            with es4:
                em4 = Em()
                with nc.Block("ph4") as blk:
                    ph4_woln(nc, t, s, em4, sems, es4)
                    em4.install(blk)
        es_lora.close()
        if "ph5" in BUILD_PHASES:
            es5 = contextlib.ExitStack()
            with es5:
                em5 = Em()
                with nc.Block("ph5") as blk:
                    ph5_ffn(nc, t, s, em5, sems, es5)
                    em5.install(blk)

        if "ph5" not in BUILD_PHASES or not PH5_FINAL:
            emf = Em()
            fin = sems["fin"]
            with nc.Block("dummy_out") as blk:
                for tt in range(NTT):
                    emf.emit("sync", lambda e, tt=tt: e.dma_start(
                        out=t["out_t"][tt], in_=t["x_t"][tt]), inc=fin)
                emf.wait("sync", fin, fin.n)
                emf.install(blk)
    return nc


def kernel(**inputs):
    global DEBUG
    nc = build_nc()
    w = pack_weights(inputs)
    consts = host_constants()
    xs = pack_x(np.asarray(inputs["L0_attn_x"], np.float32))
    in_maps = []
    for b in range(B):
        m = dict(w)
        m.update(consts)
        m["x_t"] = xs[b]
        in_maps.append(m)
    res = run_bass_kernel_spmd(nc, in_maps, core_ids=list(range(B)))
    outs = np.stack([unpack_out(res.results[b]["out_t"]) for b in range(B)])
    return outs.astype(np.float32)
